# version 07
# baseline (speedup 1.0000x reference)
import math
import numpy as np
import ml_dtypes
import concourse.bass as bass
import concourse.mybir as mybir
from concourse.bass_utils import run_bass_kernel_spmd

F32 = mybir.dt.float32
BF16 = mybir.dt.bfloat16
U8 = mybir.dt.uint8
AF = mybir.ActivationFunctionType
ALU = mybir.AluOpType
AX = mybir.AxisListType

T = 2048
E = 1024
CE = 8
NH, DH, DR, DKV = 16, 64, 32, 256
HF = 4096
HC = 32
TTW = 512
NTT = T // TTW
NG, HG = 4, 4
QTN = T // 128
ISCALE = 1.0 / math.sqrt(DH + DR)
EPS = 1e-5
NEG = -1e30

ENGS = ("tensor", "vector", "scalar", "gpsimd", "sync")
SEMS = ("tD", "vD", "sD", "gD", "qS", "qG", "qC")

STAGE = 3


class Plan:
    def __init__(self):
        self.ops = {e: [] for e in ENGS}
        self.sem = {s: 0 for s in SEMS}

    def at(self, name):
        return self.sem[name]

    def do(self, eng, fn, waits=(), inc=(), tag=""):
        waits = tuple((s, v) for (s, v) in waits if v > 0)
        for s, a in inc:
            self.sem[s] += a
        self.ops[eng].append((waits, tuple(inc), fn, tag))

    def barrier(self):
        snap = [(s, self.sem[s]) for s in SEMS if self.sem[s] > 0]
        for e in ENGS:
            self.do(e, None, waits=snap, tag="barrier")

    def check_deadlock(self):
        idx = {e: 0 for e in ENGS}
        cnt = {s: 0 for s in self.sem}
        progress = True
        while progress:
            progress = False
            for e in ENGS:
                while idx[e] < len(self.ops[e]):
                    waits, inc, _, _ = self.ops[e][idx[e]]
                    if any(cnt[s] < v for (s, v) in waits):
                        break
                    for s, a in inc:
                        cnt[s] += a
                    idx[e] += 1
                    progress = True
        stuck = {e: idx[e] for e in ENGS if idx[e] < len(self.ops[e])}
        if stuck:
            lines = []
            for e in stuck:
                waits, _, _, tag = self.ops[e][idx[e]]
                lines.append(
                    f"{e}@{idx[e]}/{len(self.ops[e])} tag={tag} "
                    f"waits={[(s, v, cnt[s]) for s, v in waits]}")
            raise RuntimeError("PLAN DEADLOCK:\n" + "\n".join(lines))


class Arena:
    def __init__(self, ap, nbytes):
        self.ap = ap
        self.nbytes = nbytes
        self.free = [(0, nbytes)]
        self.regions = {}
        self.peak = 0

    def alloc(self, name, per_part_bytes, dtype, shape):
        assert name not in self.regions, name
        nb = (per_part_bytes + 31) & ~31
        for i, (off, sz) in enumerate(self.free):
            if sz >= nb:
                if sz == nb:
                    self.free.pop(i)
                else:
                    self.free[i] = (off + nb, sz - nb)
                view = self.ap[0:shape[0],
                               off:off + per_part_bytes].bitcast(dtype)
                want = per_part_bytes // mybir.dt.size(dtype)
                assert view.shape[1] == want, (name, view.shape, want)
                if len(shape) == 2:
                    assert shape[1] == want, (name, shape, want)
                self.regions[name] = (off, nb)
                used = self.nbytes - sum(s for _, s in self.free)
                self.peak = max(self.peak, used)
                return view
        raise RuntimeError(f"arena OOM {name} ({nb}B); free={self.free}")

    def freeit(self, *names):
        for name in names:
            off, nb = self.regions.pop(name)
            self.free.append((off, nb))
        self.free.sort()
        merged = []
        for off, sz in self.free:
            if merged and merged[-1][0] + merged[-1][1] == off:
                merged[-1] = (merged[-1][0], merged[-1][1] + sz)
            else:
                merged.append((off, sz))
        self.free = merged


def build_program(stage=STAGE, debug=False, sub=5):
    nc = bass.Bass("TRN2", target_bir_lowering=False,
                   detect_race_conditions=False)

    xT = nc.dram_tensor("xT", [E, T], F32, kind="ExternalInput").ap()
    wq = nc.dram_tensor("wq", [E, NH * DH], F32, kind="ExternalInput").ap()
    wqr = nc.dram_tensor("wqr", [E, NH * DR], F32, kind="ExternalInput").ap()
    wkrd = nc.dram_tensor("wkr", [E, DR], F32, kind="ExternalInput").ap()
    wkvdd = nc.dram_tensor("wkvd", [E, DKV], F32, kind="ExternalInput").ap()
    wkud = nc.dram_tensor("wku", [DKV, NH * DH], F32, kind="ExternalInput").ap()
    wvud = nc.dram_tensor("wvu", [DKV, NH * DH], F32, kind="ExternalInput").ap()
    wod = nc.dram_tensor("wo", [E, E], F32, kind="ExternalInput").ap()
    w1td = nc.dram_tensor("w1t", [E, HF], F32, kind="ExternalInput").ap()
    w2td = nc.dram_tensor("w2t", [HF, E], F32, kind="ExternalInput").ap()
    b1d = nc.dram_tensor("b1l", [128, HC], F32, kind="ExternalInput").ap()
    b2d = nc.dram_tensor("b2row", [1, E], F32, kind="ExternalInput").ap()
    scd = nc.dram_tensor("scalebc", [128, 1], F32, kind="ExternalInput").ap()
    lbd = nc.dram_tensor("lobobc", [128, NH], F32, kind="ExternalInput").ap()
    idd = nc.dram_tensor("ident", [128, 128], BF16, kind="ExternalInput").ap()
    m0d = nc.dram_tensor("mask0", [128, 128], F32, kind="ExternalInput").ap()
    mtd = nc.dram_tensor("maskt01", [128, 128], BF16, kind="ExternalInput").ap()
    ctd = nc.dram_tensor("ctab", [128, T], BF16, kind="ExternalInput").ap()
    s2d = nc.dram_tensor("stab2", [128, T], BF16, kind="ExternalInput").ap()
    e97d = nc.dram_tensor("e97", [97, 4], BF16, kind="ExternalInput").ap()
    e8d = nc.dram_tensor("e8", [8, 256], BF16, kind="ExternalInput").ap()
    outT = nc.dram_tensor("outT", [E, T], F32, kind="ExternalOutput").ap()
    xnd = nc.dram_tensor("xnd", [128, CE * T], BF16, kind="Internal").ap()
    dbg = {}
    if debug:
        dbg["xn"] = nc.dram_tensor("dbg_xn", [128, CE * TTW], BF16,
                                   kind="ExternalOutput").ap()
        dbg["g"] = nc.dram_tensor("dbg_g", [128, HC * TTW], BF16,
                                  kind="ExternalOutput").ap()
        dbg["al"] = nc.dram_tensor("dbg_al", [4, TTW], BF16,
                                   kind="ExternalOutput").ap()
        dbg["bt"] = nc.dram_tensor("dbg_bt", [16, TTW], BF16,
                                   kind="ExternalOutput").ap()
        dbg["lt"] = nc.dram_tensor("dbg_lt", [16, E], BF16,
                                   kind="ExternalOutput").ap()
        dbg["ub"] = nc.dram_tensor("dbg_ub", [128, TTW], F32,
                                   kind="ExternalOutput").ap()
        dbg["rb"] = nc.dram_tensor("dbg_rb", [128, TTW], F32,
                                   kind="ExternalOutput").ap()
        dbg["xn2"] = nc.dram_tensor("dbg_xn2", [128, CE * TTW], BF16,
                                    kind="ExternalOutput").ap()
        dbg["g2"] = nc.dram_tensor("dbg_g2", [128, HC * TTW], BF16,
                                   kind="ExternalOutput").ap()
        dbg["al2"] = nc.dram_tensor("dbg_al2", [4, TTW], BF16,
                                    kind="ExternalOutput").ap()
        dbg["y"] = nc.dram_tensor("dbg_y", [128, QTN * E], BF16,
                                  kind="ExternalOutput").ap()
        dbg["qh0"] = nc.dram_tensor("dbg_qh0", [96, T], BF16,
                                    kind="ExternalOutput").ap()
        dbg["kh0"] = nc.dram_tensor("dbg_kh0", [96, T], BF16,
                                    kind="ExternalOutput").ap()
        dbg["vg"] = nc.dram_tensor("dbg_vg", [128, 16 * HG * 65], BF16,
                                   kind="ExternalOutput").ap()
        dbg["st4"] = nc.dram_tensor("dbg_st4", [128, 40], F32,
                                    kind="ExternalOutput").ap()
        dbg["lat"] = nc.dram_tensor("dbg_lat", [128, 2 * T], BF16,
                                    kind="ExternalOutput").ap()
        dbg["kro"] = nc.dram_tensor("dbg_kro", [96, T], BF16,
                                    kind="ExternalOutput").ap()
        dbg["yscr"] = nc.dram_tensor("dbg_yscr", [128, 520], F32,
                                     kind="ExternalOutput").ap()
    w1bfd = nc.dram_tensor("w1bf", [128, CE * HF], BF16, kind="Internal").ap()
    w2bfd = nc.dram_tensor("w2bf", [HC, 128, E], BF16, kind="Internal").ap()

    ARENA_B = 212832
    arena_h = nc.alloc_sbuf_tensor("arena", [128, ARENA_B], U8)
    A = Arena(arena_h.ap(), ARENA_B)

    pb_ctx = [nc.psum_tensor(f"pb{i}", [128, 512], F32) for i in range(8)]
    pb = [c.__enter__()[:] for c in pb_ctx]

    P = Plan()

    stream = A.alloc("stream", T * 4 * CE, F32, [128, CE * T])
    ident = A.alloc("ident", 128 * 2, BF16, [128, 128])
    b1l = A.alloc("b1l", HC * 4, F32, [128, HC])
    b2row = A.alloc("b2row", E * 4, F32, [1, E])
    b2hl = A.alloc("b2hl", E * 2, BF16, [2, E])
    b2tmp = A.alloc("b2tmp", E * 2, BF16, [1, E])
    onesr = A.alloc("onesr", TTW * 2, BF16, [2, TTW])
    ones128 = A.alloc("ones128", 128 * 2, BF16, [128, 128])
    scalebc = A.alloc("scalebc", 32, F32, [128, 8])
    epsc = A.alloc("epsc", 32, F32, [128, 8])

    def fdma(outap, inap, slow=False):
        def f(eng):
            if slow:
                return eng.dma_start(out=outap, in_=inap,
                                     allow_slow_non_contiguous=True)
            return eng.dma_start(out=outap, in_=inap)
        return f

    st_xq = []
    for xt in range(NTT):
        P.do("sync",
             fdma(stream.rearrange("p (c n) -> p c n", c=CE)
                  [:, :, xt * TTW:(xt + 1) * TTW],
                  xT.rearrange("(c p) n -> p c n", p=128)
                  [:, :, xt * TTW:(xt + 1) * TTW]),
             inc=[("qS", 16)], tag=f"ld_x{xt}")
        st_xq.append(P.at("qS"))
    for ap_, src in ((ident, idd), (b1l, b1d), (b2row, b2d),
                     (scalebc[:, 0:1], scd)):
        P.do("sync", fdma(ap_, src), inc=[("qS", 16)], tag="ld_const")
    q_consts = P.at("qS")

    P.do("vector", lambda eng: eng.memset(onesr, 1.0), inc=[("vD", 1)])
    P.do("vector", lambda eng: eng.memset(epsc, EPS), inc=[("vD", 1)])
    P.do("vector", lambda eng: eng.memset(ones128, 1.0), inc=[("vD", 1)])

    def f_b2(eng):
        eng.tensor_copy(b2hl[0:1, :], b2row)
        eng.nop(cycle_cnt=400)
        return eng.tensor_tensor(b2tmp, b2row, b2hl[0:1, :], ALU.subtract)
    P.do("vector", f_b2, waits=[("qS", q_consts)], inc=[("vD", 1)], tag="b2")
    P.do("sync", fdma(b2hl[1:2, :], b2tmp), waits=[("vD", P.at("vD"))],
         inc=[("qS", 16)], tag="b2lo_mv")
    A.freeit("b2tmp")
    v_init = P.at("vD")
    s_init = P.at("sD")
    st_q_init = P.at("qS")
    A.freeit("b2row")

    st = dict(prev_resid=0, ln_prev_t=0, w2cache_q=0)

    def emit_ln_pre(tt, scr, cast_waits):
        xb2, x22, ub, rb, t1s, t1g = scr
        out = []
        for c in range(CE):
            pcol = slice(c * TTW, (c + 1) * TTW)
            src = stream[:, c * T + tt * TTW: c * T + (tt + 1) * TTW]

            def f(eng, src=src, dst=xb2[:, pcol]):
                return eng.tensor_copy(dst, src)
            P.do("vector", f, waits=cast_waits(c), inc=[("vD", 1)],
                 tag=f"lncast{c}")
            v_c = P.at("vD")

            def f(eng, i=xb2[:, pcol], o=x22[:, pcol]):
                return eng.activation(o, i, AF.Square)
            P.do("scalar", f, waits=[("vD", v_c)] + cast_waits(c + 10),
                 inc=[("sD", 1)], tag=f"lnsq{c}")
            out.append(P.at("sD"))
        return out

    def emit_ln_post(tt, xn, scr, s_sq, mm_waits):
        xb2, x22, ub, rb, t1s, t1g = scr
        for c in range(CE):
            pcol = slice(c * TTW, (c + 1) * TTW)
            w = [("sD", s_sq[c])]
            if c == 0:
                w += mm_waits

            def f(eng, c=c, pcol=pcol):
                eng.matmul(pb[0], ones128, xb2[:, pcol],
                           start=(c == 0), stop=(c == CE - 1))
                return eng.matmul(pb[1], ones128, x22[:, pcol],
                                  start=(c == 0), stop=(c == CE - 1))
            P.do("tensor", f, waits=w, inc=[("tD", 1)], tag=f"lnmm{c}")
        st["ln_prev_t"] = P.at("tD")
        t_stats = P.at("tD")

        def f(eng):
            return eng.tensor_scalar(ub, pb[0][:, 0:TTW], 1.0 / E, None,
                                     ALU.mult)
        P.do("vector", f, waits=[("tD", t_stats)], inc=[("vD", 1)],
             tag="lnst1")
        v1 = P.at("vD")
        P.do("scalar", lambda eng: eng.activation(t1s, ub, AF.Square),
             waits=[("vD", v1)], inc=[("sD", 1)], tag="lnst2")
        s1 = P.at("sD")
        P.do("vector",
             lambda eng: eng.scalar_tensor_tensor(
                 t1s, pb[1][:, 0:TTW], 1.0 / E, t1s, ALU.mult, ALU.subtract),
             waits=[("sD", s1)], inc=[("vD", 1)], tag="lnst3")
        v2 = P.at("vD")

        def f(eng):
            eng.activation(t1g, t1s, AF.Ln, bias=epsc[:, 0:1])
            return eng.activation(rb, t1g, AF.Exp, scale=-0.5)
        P.do("scalar", f, waits=[("vD", v2)], inc=[("sD", 1)], tag="lnst4")
        s_rb = P.at("sD")

        vnorm = []
        for c in range(CE):
            src = stream[:, c * T + tt * TTW: c * T + (tt + 1) * TTW]
            dst = xn[:, c * TTW:(c + 1) * TTW]

            def f(eng, src=src, dst=dst):
                eng.tensor_tensor(t1s, src, ub, ALU.subtract)
                return eng.tensor_tensor(dst, t1s, rb, ALU.mult)
            P.do("vector", f, waits=[("sD", s_rb)], inc=[("vD", 1)],
                 tag=f"lnnv{c}")
            vnorm.append(P.at("vD"))
        return [("vD", P.at("vD"))], vnorm

    def emit_ffn(call):
        g_sb = A.alloc("g", HC * TTW * 2, BF16, [128, HC * TTW])
        xb2 = g_sb[:, 0:CE * TTW]
        x22 = g_sb[:, CE * TTW:2 * CE * TTW]
        xn = A.alloc("xn", CE * TTW * 2, BF16, [128, CE * TTW])
        ub = A.alloc("ub", TTW * 4, F32, [128, TTW])
        rb = A.alloc("rb", TTW * 4, F32, [128, TTW])
        t1s = A.alloc("t1s", TTW * 4, F32, [128, TTW])
        t1g = A.alloc("t1g", TTW * 4, F32, [128, TTW])
        w2c = A.alloc("w2c", 3 * E * 2, BF16, [128, 3 * E])
        lt = g_sb[0:16, 16 * TTW:16 * TTW + E]
        xn16 = A.alloc("xn16", CE * 16 * 2, BF16, [128, CE * 16])
        alphas = A.alloc("alphas", TTW * 2, BF16, [4, TTW])
        bt = [A.alloc(f"bt{j}", TTW * 2, BF16, [4, TTW]) for j in range(4)]
        e4a = A.alloc("e4a", HF * 2, BF16, [4, HF])
        e16j = [A.alloc(f"e16j{j}", E * 2, BF16, [4, E]) for j in range(4)]
        scr = (xb2, x22, ub, rb, t1s, t1g)
        if call == 0:
            w1p = [A.alloc(f"w1t{h}", 4 * HF * 2, BF16, [128, 4 * HF])
                   for h in range(2)]
        else:
            w1p = st["w1t2"]

        if call == 0:
            for h in range(2):
                P.do("gpsimd",
                     fdma(w1p[h].rearrange("p (c n) -> p c n", c=4),
                          w1td[h * 512:(h + 1) * 512, :]
                          .rearrange("(c p) n -> p c n", p=128)),
                     waits=[("vD", v_init)], inc=[("qG", 16)],
                     tag=f"ld_w1_{h}")
            w1_ready = ("qG", P.at("qG"))
            for h in range(2):
                P.do("scalar",
                     fdma(w1bfd[:, h * 4 * HF:(h + 1) * 4 * HF], w1p[h]),
                     waits=[w1_ready], inc=[("qC", 16)], tag=f"w1cache{h}")
            st["w1cache_q"] = ("qC", P.at("qC"))
        else:
            w1_ready = st["w1t2_ready"]

        s_aev = 0
        t_w2used = {}
        pctr = 0
        v_xn16 = 0
        s_sq_next = None
        for tt in range(NTT):
            if s_sq_next is None:
                def cw(c, tt=tt):
                    w = [("tD", st["ln_prev_t"]),
                         ("vD", st.get("early_resid", 0))]
                    if call == 0:
                        w.append(("qS", st_xq[tt]))
                    return w
                s_sq_next = emit_ln_pre(tt, scr, cw)
            mmw = [("vD", st.get("early_resid", 0)), ("vD", v_init),
                   ("qS", q_consts)]
            xn_ready, vnorm = emit_ln_post(tt, xn, scr, s_sq_next, mmw)
            s_sq_next = None

            if tt == 0:
                def f(eng):
                    return eng.tensor_copy(
                        xn16.rearrange("p (c n) -> p c n", c=CE),
                        xn.rearrange("p (c n) -> p c n", c=CE)[:, :, 1:17])
                P.do("vector", f, waits=xn_ready, inc=[("vD", 1)], tag="xn16")
                v_xn16 = P.at("vD")
                s_prev = s_aev
                for c in range(CE):
                    def f(eng, c=c):
                        return eng.matmul(pb[2][0:16, 0:128],
                                          xn[:, c * TTW + 1:c * TTW + 17],
                                          ident, start=True, stop=True)
                    P.do("tensor", f,
                         waits=xn_ready + [("sD", s_prev)],
                         inc=[("tD", 1)], tag=f"ltmm{c}")
                    t_lt = P.at("tD")

                    def f(eng, c=c):
                        return eng.copy(lt[0:16, c * 128:(c + 1) * 128],
                                        pb[2][0:16, 0:128])
                    P.do("scalar", f, waits=[("tD", t_lt)], inc=[("sD", 1)],
                         tag=f"ltev{c}")
                    s_prev = P.at("sD")
                s_aev = s_prev
                s_lt = s_prev
                P.do("vector", lambda eng: eng.memset(e4a, 0.0),
                     inc=[("vD", 1)], tag="e4a0")
                for j in range(4):
                    P.do("vector",
                         lambda eng, j=j: eng.memset(e16j[j], 0.0),
                         inc=[("vD", 1)], tag=f"e16z{j}")
                v_ez = P.at("vD")
                for i in range(4):
                    P.do("sync",
                         fdma(e4a[i:i + 1, 1024 * i:1024 * (i + 1)],
                              lt[2 * i:2 * i + 1, :]),
                         waits=[("sD", s_lt), ("vD", v_ez)],
                         inc=[("qS", 16)], tag=f"e4a{i}")
                for i4 in range(4):
                    for j in range(4):
                        P.do("sync",
                             fdma(e16j[j][i4:i4 + 1,
                                          256 * i4:256 * i4 + 256],
                                  lt[8 + 2 * i4:9 + 2 * i4, j:j + 1021:4],
                                  slow=True),
                             waits=[("sD", s_lt), ("vD", v_ez)],
                             inc=[("qS", 16)], tag=f"e16_{i4}_{j}")
                q_e = P.at("qS")

            for c in range(CE):
                w = [("vD", vnorm[c]), ("vD", v_xn16)]
                if c == 0:
                    w.append(("sD", s_aev))

                def f(eng, c=c):
                    return eng.matmul(
                        pb[2][0:4, 0:TTW],
                        xn16[:, c * 16 + 1:c * 16 + 8:2],
                        xn[:, c * TTW:(c + 1) * TTW],
                        start=(c == 0), stop=(c == CE - 1))
                P.do("tensor", f, waits=w, inc=[("tD", 1)], tag=f"amm{c}")
            t_a = P.at("tD")

            def f(eng):
                return eng.activation(alphas, pb[2][0:4, 0:TTW], AF.Copy,
                                      scale=scalebc[0:4, 0:1])
            P.do("scalar", f, waits=[("tD", t_a)], inc=[("sD", 1)], tag="aev")
            s_aev = P.at("sD")

            s_gelu = {}
            for m in range(HC):
                par = m % 2
                if m < 2:
                    for kc in range(CE):
                        w = [("vD", vnorm[kc])]
                        if m == 0 and kc == 0:
                            w += [("sD", s_aev), w1_ready, ("qS", q_e)]

                        def f(eng, m=m, par=par, kc=kc):
                            wsl = w1p[kc // 4][:, (kc % 4) * HF + m * 128:
                                               (kc % 4) * HF + (m + 1) * 128]
                            return eng.matmul(
                                pb[par], wsl,
                                xn[:, kc * TTW:(kc + 1) * TTW],
                                start=(kc == 0), stop=False)
                        P.do("tensor", f, waits=w, inc=[("tD", 1)],
                             tag=f"hmm{m}_{kc}")

                    def f(eng, m=m, par=par):
                        return eng.matmul(
                            pb[par], e4a[0:4, m * 128:(m + 1) * 128],
                            alphas[0:4, :], start=False, stop=True)
                    P.do("tensor", f, waits=[("sD", s_aev)],
                         inc=[("tD", 1)], tag=f"hmm{m}_l")
                else:
                    w = list(xn_ready) + [("sD", s_aev), w1_ready]
                    w.append(("sD", s_gelu[m - 2]))

                    def f(eng, m=m, par=par):
                        for kc in range(CE):
                            wsl = w1p[kc // 4][:, (kc % 4) * HF + m * 128:
                                               (kc % 4) * HF + (m + 1) * 128]
                            eng.matmul(
                                pb[par], wsl,
                                xn[:, kc * TTW:(kc + 1) * TTW],
                                start=(kc == 0), stop=False)
                        return eng.matmul(
                            pb[par], e4a[0:4, m * 128:(m + 1) * 128],
                            alphas[0:4, :], start=False, stop=True)
                    P.do("tensor", f, waits=w, inc=[("tD", 1)], tag=f"hmm{m}")
                t_h = P.at("tD")

                def f(eng, m=m, par=par):
                    return eng.activation(g_sb[:, m * TTW:(m + 1) * TTW],
                                          pb[par], AF.Gelu,
                                          bias=b1l[:, m:m + 1])
                P.do("scalar", f, waits=[("tD", t_h)], inc=[("sD", 1)],
                     tag=f"gelu{m}")
                s_gelu[m] = P.at("sD")
            s_gelu_all = P.at("sD")

            for j in range(4):
                for c in range(CE):
                    kc = 8 * j + c
                    w = [("sD", s_gelu[kc])]
                    if c == 0:
                        w.append(("sD", s_aev))

                    def f(eng, j=j, c=c, kc=kc):
                        return eng.matmul(
                            pb[2][0:4, 0:TTW],
                            xn16[:, c * 16 + 9:c * 16 + 16:2],
                            g_sb[:, kc * TTW:(kc + 1) * TTW],
                            start=(c == 0), stop=(c == CE - 1))
                    P.do("tensor", f, waits=w, inc=[("tD", 1)],
                         tag=f"bmm{j}_{c}")
                t_b = P.at("tD")

                def f(eng, j=j):
                    return eng.activation(bt[j][0:4, :],
                                          pb[2][0:4, 0:TTW], AF.Copy,
                                          scale=scalebc[0:4, 0:1])
                P.do("scalar", f, waits=[("tD", t_b)], inc=[("sD", 1)],
                     tag=f"bev{j}")
                s_aev = P.at("sD")
            s_beta = P.at("sD")

            first = (call == 0 and tt == 0)
            qdma = {}
            qcache = {}
            NBUF = 3

            def w2buf(buf):
                return w2c[:, buf * E:(buf + 1) * E]

            def plan_w2_dma(i):
                buf = (base_p + i) % NBUF
                w = []
                if base_p + i >= NBUF:
                    w.append(("tD", t_w2used[base_p + i - NBUF]))
                if buf >= 2:
                    w.append(("sD", s_gelu_all))
                if first and i >= NBUF:
                    w.append(("qC", qcache[i - NBUF]))
                dst = w2buf(buf)
                if first:
                    P.do("gpsimd",
                         fdma(dst, w2td[i * 128:(i + 1) * 128, :]),
                         waits=w, inc=[("qG", 16)], tag=f"w2l{i}")
                    qdma[i] = ("qG", P.at("qG"))
                    P.do("scalar", fdma(w2bfd[i], dst),
                         waits=[qdma[i]], inc=[("qC", 16)], tag=f"w2c{i}")
                    qcache[i] = P.at("qC")
                else:
                    P.do("sync", fdma(dst, w2bfd[i]),
                         waits=w + [("qC", st["w2cache_q"])],
                         inc=[("qS", 16)], tag=f"w2l{i}")
                    qdma[i] = ("qS", P.at("qS"))

            def plan_w2_mm(kc):
                buf = (base_p + kc) % NBUF
                w = [qdma[kc]]
                if kc == 0:
                    w += [("sD", s_gelu_all), ("sD", s_beta),
                          ("vD", st["prev_resid"])]
                else:
                    w.append(("sD", s_gelu[kc]))
                src = w2buf(buf)

                def f(eng, kc=kc, src=src):
                    r = None
                    for m in range(8):
                        r = eng.matmul(
                            pb[m],
                            src[:, m * 128:(m + 1) * 128],
                            g_sb[:, kc * TTW:(kc + 1) * TTW],
                            start=(kc == 0), stop=False)
                    return r
                P.do("tensor", f, waits=w, inc=[("tD", 1)], tag=f"omm{kc}")
                t_w2used[base_p + kc] = P.at("tD")

            base_p = pctr
            for i in range(HC + 2):
                if i < HC:
                    plan_w2_dma(i)
                if i >= 2:
                    plan_w2_mm(i - 2)
            pctr = base_p + HC
            if first:
                st["w2cache_q"] = max(qcache.values())

            if tt + 1 < NTT:
                def cw(c, bp=base_p, tt1=tt + 1):
                    if c < 10:
                        w = [("tD", t_w2used[bp + c])]
                    else:
                        w = [("tD", t_w2used[bp + 8 + (c - 10)])]
                    if call == 0:
                        w.append(("qS", st_xq[tt1]))
                    return w
                s_sq_next = emit_ln_pre(tt + 1, scr, cw)

            def f(eng):
                r = None
                for m in range(8):
                    for j in range(4):
                        r = eng.matmul(pb[m],
                                       e16j[j][0:4, m * 128:(m + 1) * 128],
                                       bt[j][0:4, :],
                                       start=False, stop=False)
                    r = eng.matmul(pb[m], b2hl[0:2, m * 128:(m + 1) * 128],
                                   onesr[0:2, :], start=False, stop=True)
                return r
            P.do("tensor", f, waits=[("sD", s_beta), ("vD", v_init),
                                     ("qS", max(q_e, st_q_init))],
                 inc=[("tD", 1)], tag="oxtr")
            t_out = P.at("tD")

            for m in range(8):
                dst = stream[:, m * T + tt * TTW:m * T + (tt + 1) * TTW]

                def f(eng, m=m, dst=dst):
                    return eng.scalar_tensor_tensor(
                        dst, pb[m], 0.5, dst, ALU.mult, ALU.add)
                P.do("vector", f, waits=[("tD", t_out)], inc=[("vD", 1)],
                     tag=f"res{m}")
                if m == 1:
                    st["early_resid"] = P.at("vD")
            st["prev_resid"] = P.at("vD")
            st["ln_t_gate"] = t_out

        A.freeit("g", "xn", "ub", "rb", "t1s", "t1g",
                 "w2c", "xn16", "alphas", "e4a",
                 *(("w1t0", "w1t1") if call == 0 else ("w1u0", "w1u1")),
                 *[f"bt{j}" for j in range(4)],
                 *[f"e16j{j}" for j in range(4)])

    def emit_attn(sub=5):
        P.barrier()
        mask0 = A.alloc("mask0", 128 * 4, F32, [128, 128])
        maskt01 = A.alloc("maskt01", 128 * 2, BF16, [128, 128])
        cbc = A.alloc("cbc", NH * 4, F32, [128, NH])
        P.do("sync", fdma(mask0, m0d), inc=[("qS", 16)], tag="ld_mask0")
        P.do("sync", fdma(maskt01, mtd), inc=[("qS", 16)], tag="ld_maskt")
        P.do("sync", fdma(cbc, lbd), inc=[("qS", 16)], tag="ld_lobo")
        q_m = P.at("qS")
        xn = A.alloc("xn", CE * TTW * 2, BF16, [128, CE * TTW])
        xb2 = A.alloc("xb2", CE * TTW * 2, BF16, [128, CE * TTW])
        x22 = A.alloc("x22", CE * TTW * 2, BF16, [128, CE * TTW])
        ub = A.alloc("ub", TTW * 4, F32, [128, TTW])
        rb = A.alloc("rb", TTW * 4, F32, [128, TTW])
        t1s = A.alloc("t1s", TTW * 4, F32, [128, TTW])
        t1g = A.alloc("t1g", TTW * 4, F32, [128, TTW])
        scr = (xb2, x22, ub, rb, t1s, t1g)
        latent = A.alloc("latent", 2 * T * 2, BF16, [128, 2 * T])
        krope = A.alloc("krope", T * 2, BF16, [96, T])
        ctab = A.alloc("ctab", T * 2, BF16, [128, T])
        stab2 = A.alloc("stab2", T * 2, BF16, [128, T])
        ro1 = A.alloc("ro1", TTW * 4, F32, [128, TTW])
        ro2 = A.alloc("ro2", TTW * 4, F32, [128, TTW])
        ro3 = A.alloc("ro3", TTW * 4, F32, [128, TTW])
        wkvd_sb = A.alloc("wkvd_sb", CE * DKV * 2, BF16, [128, CE * DKV])
        wkr_sb = A.alloc("wkr_sb", CE * DR * 2, BF16, [128, CE * DR])

        P.do("sync", fdma(ctab, ctd), inc=[("qS", 16)], tag="ld_ctab")
        P.do("sync", fdma(stab2, s2d), inc=[("qS", 16)], tag="ld_stab")
        q_tabs = P.at("qS")
        P.do("gpsimd", fdma(wkvd_sb.rearrange("p (c n) -> p c n", c=CE),
                            wkvdd.rearrange("(c p) n -> p c n", p=128)),
             inc=[("qG", 16)], tag="ld_wkvd")
        P.do("gpsimd", fdma(wkr_sb.rearrange("p (c n) -> p c n", c=CE),
                            wkrd.rearrange("(c p) n -> p c n", p=128)),
             inc=[("qG", 16)], tag="ld_wkr")
        q_kv = P.at("qG")

        s_latent_ev = {}
        v_krope = 0
        q_spill = [0] * NTT
        s_sq_next = None
        for tt in range(NTT):
            if s_sq_next is None:
                def cw(c):
                    return [("tD", st["ln_prev_t"]),
                            ("vD", st.get("early_resid", 0))]
                s_sq_next = emit_ln_pre(tt, scr, cw)
            mmw = [("vD", st.get("early_resid", 0)), ("vD", v_init),
                   ("qS", q_consts)]
            xn_ready, vnorm = emit_ln_post(tt, xn, scr, s_sq_next, mmw)
            s_sq_next = None
            P.do("sync",
                 fdma(xnd.rearrange("p (c n) -> p c n", c=CE)
                      [:, :, tt * TTW:(tt + 1) * TTW],
                      xn.rearrange("p (c n) -> p c n", c=CE)),
                 waits=xn_ready, inc=[("qS", 16)], tag=f"xnspill{tt}")
            q_spill[tt] = P.at("qS")
            if tt + 1 < NTT:
                def cw(c):
                    return [("tD", st["ln_prev_t"])]
                s_sq_next = emit_ln_pre(tt + 1, scr, cw)
            for mc in range(2):
                for kc in range(CE):
                    w = list(xn_ready) if kc == 0 else []
                    if kc == 0:
                        w += [("qG", q_kv),
                              ("sD", s_latent_ev.get(mc, 0))]

                    def f(eng, mc=mc, kc=kc):
                        return eng.matmul(
                            pb[2 + mc],
                            wkvd_sb[:, kc * DKV + mc * 128:
                                    kc * DKV + (mc + 1) * 128],
                            xn[:, kc * TTW:(kc + 1) * TTW],
                            start=(kc == 0), stop=(kc == CE - 1))
                    P.do("tensor", f, waits=w, inc=[("tD", 1)],
                         tag=f"lat{mc}_{kc}")
                t_l = P.at("tD")

                def f(eng, mc=mc, tt=tt):
                    return eng.copy(
                        latent[:, mc * T + tt * TTW:mc * T + (tt + 1) * TTW],
                        pb[2 + mc])
                P.do("scalar", f, waits=[("tD", t_l)], inc=[("sD", 1)],
                     tag=f"latev{mc}")
                s_latent_ev[mc] = P.at("sD")
            for kc in range(CE):
                w = list(xn_ready) if kc == 0 else []
                if kc == 0:
                    w += [("qG", q_kv), ("vD", v_krope)]

                def f(eng, kc=kc):
                    return eng.matmul(
                        pb[4][64:96, 0:TTW],
                        wkr_sb[:, kc * DR:(kc + 1) * DR],
                        xn[:, kc * TTW:(kc + 1) * TTW],
                        start=(kc == 0), stop=(kc == CE - 1))
                P.do("tensor", f, waits=w, inc=[("tD", 1)], tag=f"krmm{kc}")
            t_kr = P.at("tD")
            cs = slice(tt * TTW, (tt + 1) * TTW)

            def f(eng, cs=cs):
                eng.tensor_scalar(ro3[64:96, :], pb[4][64:96, 0:TTW], ISCALE,
                                  None, ALU.mult)
                eng.tensor_tensor(ro1[64:96, :], ro3[64:96, :],
                                  ctab[64:96, cs], ALU.mult)
                eng.tensor_tensor(ro2[64:96, :], ro3[64:96, :],
                                  stab2[64:96, cs], ALU.mult)
                mask = [(p + 16) % 32 for p in range(32)]
                eng.stream_shuffle(ro3[64:96, :], ro2[64:96, :], mask)
                return eng.tensor_tensor(krope[64:96, cs], ro1[64:96, :],
                                         ro3[64:96, :], ALU.add)
            P.do("vector", f, waits=[("tD", t_kr), ("qS", q_tabs)],
                 inc=[("vD", 1)], tag=f"krrope{tt}")
            v_krope = P.at("vD")
        v_krope_all = P.at("vD")
        A.freeit("xn", "xb2", "x22", "ub", "rb", "t1s", "t1g",
                 "wkvd_sb", "wkr_sb")

        if sub < 1.5:
            A.freeit("mask0", "maskt01", "cbc", "latent", "krope",
                     "ctab", "stab2", "ro1", "ro2", "ro3")
            return
        P.barrier()
        y_sb = A.alloc("y", QTN * E * 2, BF16, [128, QTN * E])
        qh = [A.alloc(f"qh{j}", T * 2, BF16, [96, T]) for j in range(HG)]
        kh = [A.alloc(f"kh{j}", T * 2, BF16, [96, T]) for j in range(HG)]
        v_g = A.alloc("v_g", 16 * HG * 65 * 2, BF16, [128, 16 * HG * 65])
        v_g4 = v_g.rearrange("p (c h n) -> p c h n", c=16, h=HG)
        xnb = [A.alloc(f"xnb{j}", CE * TTW * 2, BF16, [128, CE * TTW])
               for j in range(2)]
        wq_g = A.alloc("wq_g", CE * 256 * 2, BF16, [128, CE * 256])
        wqr_g = A.alloc("wqr_g", CE * 128 * 2, BF16, [128, CE * 128])
        wku_g = A.alloc("wku_g", 2 * 256 * 2, BF16, [128, 2 * 256])
        wvu_g = A.alloc("wvu_g", 2 * 256 * 2, BF16, [128, 2 * 256])
        ptb = [A.alloc(f"ptb{j}", 512 * 2, BF16, [128, 512])
               for j in range(4)]
        yscrT = A.alloc("yscrT", 1024 * 2, BF16, [128, 1024])
        zhi4 = A.alloc("zhi4", 512 * 2, BF16, [97, 512])
        zlo4 = A.alloc("zlo4", 512 * 2, BF16, [97, 512])
        mpart = A.alloc("mpart", 256 * 4, F32, [128, 256])
        m4a = A.alloc("m4a", 16 * 4, F32, [128, 16])
        dn1a = A.alloc("dn1a", 16 * 4, F32, [128, 16])
        ema = A.alloc("ema", 16 * 4, F32, [128, 16])
        dn2a = A.alloc("dn2a", 16 * 4, F32, [128, 16])
        rd4a = A.alloc("rd4a", 16 * 4, F32, [128, 16])
        rdhl = A.alloc("rdhl", 32 * 2, BF16, [128, 32])
        rdT8s = A.alloc("rdT8s", 512 * 2, BF16, [8, 512])
        e97sb = A.alloc("e97", 4 * 2, BF16, [97, 4])
        e8sb = A.alloc("e8", 256 * 2, BF16, [8, 256])
        P.do("sync", fdma(e97sb, e97d), inc=[("qS", 16)], tag="ld_e97")
        P.do("sync", fdma(e8sb, e8d), inc=[("qS", 16)], tag="ld_e8")
        q_esel = P.at("qS")

        def f(eng):
            return eng.memset(v_g4[:, :, :, 64:65], 1.0)
        P.do("vector", f, inc=[("vD", 1)], tag="vones")
        v_vones = P.at("vD")

        def f(eng):
            eng.memset(zhi4, 0.0)
            return eng.memset(zlo4, 0.0)
        P.do("gpsimd", f, inc=[("gD", 1)], tag="zinit")
        g_zinit = P.at("gD")
        fs = {"v_yn": 0, "t_zt": 0, "v_yn_hp": [0, 0, 0, 0]}

        t_flash_done = 0
        s_norm_done = 0
        v_flash_done = 0
        t_projmm_done = 0
        t_projuse = {}
        s_pevac = {}
        v_pevac = {}
        ngrun = 1 if sub < 3.5 else NG
        for g in range(ngrun):
            w = [("tD", t_projmm_done)]
            P.do("gpsimd", fdma(wq_g.rearrange("p (c n) -> p c n", c=CE),
                                wq[:, 256 * g:256 * (g + 1)]
                                .rearrange("(c p) n -> p c n", p=128)),
                 waits=w, inc=[("qG", 16)], tag=f"gwq{g}")
            P.do("gpsimd", fdma(wqr_g.rearrange("p (c n) -> p c n", c=CE),
                                wqr[:, 128 * g:128 * (g + 1)]
                                .rearrange("(c p) n -> p c n", p=128)),
                 waits=w, inc=[("qG", 16)], tag=f"gwqr{g}")
            P.do("gpsimd", fdma(wku_g.rearrange("p (c n) -> p c n", c=2),
                                wkud[:, 256 * g:256 * (g + 1)]
                                .rearrange("(c p) n -> p c n", p=128)),
                 waits=w, inc=[("qG", 16)], tag=f"gwku{g}")
            P.do("gpsimd", fdma(wvu_g.rearrange("p (c n) -> p c n", c=2),
                                wvud[:, 256 * g:256 * (g + 1)]
                                .rearrange("(c p) n -> p c n", p=128)),
                 waits=w, inc=[("qG", 16)], tag=f"gwvu{g}")
            q_gw = P.at("qG")

            q_xnb = [0, 0]
            for tt in range(NTT):
                buf = tt % 2
                w = [("qS", q_spill[tt]), ("tD", t_projmm_done)]
                if tt >= 2:
                    w.append(("tD", t_projuse[tt - 2]))

                def f(eng, tt=tt, buf=buf):
                    return eng.dma_start(
                        out=xnb[buf].rearrange("p (c n) -> p c n", c=CE),
                        in_=xnd.rearrange("p (c n) -> p c n", c=CE)
                        [:, :, tt * TTW:(tt + 1) * TTW])
                P.do("sync", f, waits=w, inc=[("qS", 16)],
                     tag=f"xnl{g}_{tt}")
                q_xnb[buf] = P.at("qS")

                xg = xnb[buf]
                cs = slice(tt * TTW, (tt + 1) * TTW)
                basew = [("qS", q_xnb[buf]), ("qG", q_gw),
                         ("tD", t_flash_done), ("sD", s_norm_done),
                         ("vD", v_flash_done)]
                if sub < 1.65:
                    t_projuse[tt] = P.at("tD")
                    continue
                if sub >= 1.65:
                    for hp in range(HG):
                        bank = hp % 2
                        for kc in range(CE):
                            w = list(basew) if (kc == 0) else []
                            if kc == 0:
                                w.append(("sD", s_pevac.get(("qc", bank), 0)))

                            def f(eng, hp=hp, kc=kc, xg=xg, bank=bank):
                                return eng.matmul(
                                    pb[bank][0:64, 0:TTW],
                                    wq_g[:, kc * 256 + 64 * hp:
                                         kc * 256 + 64 * hp + 64],
                                    xg[:, kc * TTW:(kc + 1) * TTW],
                                    start=(kc == 0), stop=(kc == CE - 1))
                            P.do("tensor", f, waits=w, inc=[("tD", 1)],
                                 tag=f"qc{g}_{tt}_{hp}_{kc}")
                        t_qc = P.at("tD")

                        def f(eng, hp=hp, bank=bank, cs=cs):
                            return eng.copy(qh[hp][0:64, cs],
                                            pb[bank][0:64, 0:TTW])
                        P.do("scalar", f, waits=[("tD", t_qc)],
                             inc=[("sD", 1)], tag=f"qcev{g}_{tt}_{hp}")
                        s_pevac[("qc", bank)] = P.at("sD")
                if sub < 1.75:
                    t_projuse[tt] = P.at("tD")
                    continue
                for hp in range(HG):
                    bank = 2 + hp % 2
                    for kc in range(CE):
                        w = list(basew) if kc == 0 else []
                        if kc == 0:
                            w.append(("vD", v_pevac.get(("qr", bank), 0)))

                        def f(eng, hp=hp, kc=kc, xg=xg, bank=bank):
                            return eng.matmul(
                                pb[bank][64:96, 0:TTW],
                                wqr_g[:, kc * 128 + 32 * hp:
                                      kc * 128 + 32 * hp + 32],
                                xg[:, kc * TTW:(kc + 1) * TTW],
                                start=(kc == 0), stop=(kc == CE - 1))
                        P.do("tensor", f, waits=w, inc=[("tD", 1)],
                             tag=f"qr{g}_{tt}_{hp}_{kc}")
                    t_qr = P.at("tD")

                    def f(eng, hp=hp, bank=bank, cs=cs):
                        eng.tensor_tensor(ro1[64:96, :], pb[bank][64:96, 0:TTW],
                                          ctab[64:96, cs], ALU.mult)
                        eng.tensor_tensor(ro2[64:96, :], pb[bank][64:96, 0:TTW],
                                          stab2[64:96, cs], ALU.mult)
                        mask = [(p + 16) % 32 for p in range(32)]
                        eng.stream_shuffle(ro3[64:96, :], ro2[64:96, :], mask)
                        return eng.tensor_tensor(qh[hp][64:96, cs],
                                                 ro1[64:96, :],
                                                 ro3[64:96, :], ALU.add)
                    P.do("vector", f, waits=[("tD", t_qr), ("qS", q_tabs)],
                         inc=[("vD", 1)], tag=f"qrope{g}_{tt}_{hp}")
                    v_pevac[("qr", bank)] = P.at("vD")
                if sub < 1.85:
                    t_projuse[tt] = P.at("tD")
                    continue
                for hp in range(HG):
                    bank = 4 if hp % 2 == 0 else 6
                    for lc in range(2):
                        w = list(basew) if lc == 0 else []
                        if lc == 0:
                            w.append(("sD", s_pevac.get(("kc", bank), 0)))

                        def f(eng, hp=hp, lc=lc, tt=tt, bank=bank):
                            return eng.matmul(
                                pb[bank][0:64, 0:TTW],
                                wku_g[:, lc * 256 + 64 * hp:
                                      lc * 256 + 64 * hp + 64],
                                latent[:, lc * T + tt * TTW:
                                       lc * T + (tt + 1) * TTW],
                                start=(lc == 0), stop=(lc == 1))
                        P.do("tensor", f, waits=w, inc=[("tD", 1)],
                             tag=f"kc{g}_{tt}_{hp}_{lc}")
                    t_kc = P.at("tD")

                    def f(eng, hp=hp, bank=bank, cs=cs):
                        return eng.activation(
                            kh[hp][0:64, cs], pb[bank][0:64, 0:TTW],
                            AF.Copy, scale=ISCALE)
                    P.do("scalar", f, waits=[("tD", t_kc)],
                         inc=[("sD", 1)], tag=f"kcev{g}_{tt}_{hp}")
                    s_pevac[("kc", bank)] = P.at("sD")
                if sub < 1.95:
                    t_projuse[tt] = P.at("tD")
                    continue
                for ts_ in range(4):
                    chunk = 4 * tt + ts_
                    bank = 5 if ts_ % 2 == 0 else 7
                    for lc in range(2):
                        w = list(basew) if lc == 0 else []
                        if lc == 0:
                            w += [("vD", v_vones),
                                  ("vD", v_pevac.get(("v", bank), 0))]

                        def f(eng, ts_=ts_, lc=lc, tt=tt, bank=bank):
                            return eng.matmul(
                                pb[bank][:, 0:256],
                                latent[:, lc * T + tt * TTW + ts_ * 128:
                                       lc * T + tt * TTW + (ts_ + 1) * 128],
                                wvu_g[:, lc * 256:(lc + 1) * 256],
                                start=(lc == 0), stop=(lc == 1))
                        P.do("tensor", f, waits=w, inc=[("tD", 1)],
                             tag=f"v{g}_{tt}_{ts_}_{lc}")
                    t_v = P.at("tD")

                    def f(eng, chunk=chunk, bank=bank):
                        return eng.tensor_copy(
                            v_g4[:, chunk:chunk + 1, :, 0:64],
                            pb[bank][:, 0:256]
                            .rearrange("p (o h n) -> p o h n", o=1, h=4))
                    P.do("vector", f, waits=[("tD", t_v)], inc=[("vD", 1)],
                         tag=f"vev{g}_{tt}_{ts_}")
                    v_pevac[("v", bank)] = P.at("vD")
                t_projuse[tt] = P.at("tD")
            t_projmm_done = P.at("tD")
            s_proj_done = P.at("sD")
            v_proj_done = P.at("vD")

            for j in range(HG):
                def f(eng, j=j):
                    return eng.tensor_copy(kh[j][64:96, :], krope[64:96, :])
                P.do("vector", f,
                     waits=[("vD", v_krope_all), ("tD", t_flash_done)],
                     inc=[("vD", 1)], tag=f"krcp{g}_{j}")
            v_krcp = P.at("vD")

            if sub < 2.5:
                continue
            rs = {"r": 0}
            ring_free = {}
            pctr = 0
            t_after_pv = {}
            g_pad = {}
            yb_free = {0: [], 1: []}
            pend_tail = None

            def emit_schunk(qsup, hp, qtl, ks, mb):
                ri = rs["r"]
                rs["r"] += 1
                bank = pb[ri % 4]
                qt = 4 * qsup + qtl
                qs = slice(qt * 128, (qt + 1) * 128)
                last = (ks == qsup)
                wd = (qtl + 1) * 128 if last else 512
                w = [("sD", s_proj_done), ("vD", v_krcp)]
                if ri - 4 in ring_free:
                    w.append(ring_free.pop(ri - 4))

                def f(eng, hp=hp, bank=bank, ks=ks, wd=wd, qs=qs):
                    return eng.matmul(
                        bank[:, 0:wd], qh[hp][:, qs],
                        kh[hp][:, ks * 512:ks * 512 + wd],
                        start=True, stop=True)
                P.do("tensor", f, waits=w, inc=[("tD", 1)],
                     tag=f"S{g}_{qt}_{hp}_{ks}")
                t_s = P.at("tD")
                mc = mb + qtl * 32 + hp * 8 + ks
                if not last:
                    def f(eng, bank=bank, wd=wd, mc=mc):
                        return eng.tensor_reduce(
                            mpart[:, mc:mc + 1], bank[:, 0:wd],
                            axis=AX.X, op=ALU.max)
                    P.do("vector", f, waits=[("tD", t_s)],
                         inc=[("vD", 1)], tag=f"Smax{g}_{qt}_{hp}_{ks}")
                else:
                    mcd = mc if wd == 128 else mc + 1

                    def f(eng, bank=bank, wd=wd, mc=mc, mcd=mcd):
                        eng.tensor_tensor(bank[:, wd - 128:wd],
                                          bank[:, wd - 128:wd],
                                          mask0, ALU.add)
                        if wd > 128:
                            eng.tensor_reduce(
                                mpart[:, mc:mc + 1], bank[:, 0:wd - 128],
                                axis=AX.X, op=ALU.max)
                        else:
                            eng.nop(cycle_cnt=300)
                        return eng.tensor_reduce(
                            mpart[:, mcd:mcd + 1], bank[:, wd - 128:wd],
                            axis=AX.X, op=ALU.max)
                    P.do("vector", f, waits=[("tD", t_s)],
                         inc=[("vD", 1)], tag=f"Smaxd{g}_{qt}_{hp}")
                ring_free[ri] = ("vD", P.at("vD"))

            for qsup in range(4):
                mb = (qsup % 2) * 128
                zlo_sems = [0, 0, 0, 0]
                yev_sems = [0, 0, 0, 0]
                sq = [(hp2, qtl, ks) for hp2 in range(HG)
                      for qtl in range(4) for ks in range(qsup + 1)]
                si = 0
                for hp in range(HG):
                    kcs = ([4 * qsup + j for j in range(4)] +
                           list(range(4 * qsup)))
                    nkc = len(kcs)
                    ydst = pb[6 + hp % 2]
                    for ei, kc in enumerate(kcs):
                        j = kc - 4 * qsup
                        c0 = j * 128 if j > 0 else 0
                        ri = rs["r"]
                        rs["r"] += 1
                        bank = pb[ri % 4]
                        tp = pctr % 4
                        w = [("sD", s_proj_done), ("vD", v_krcp)]
                        if ri - 4 in ring_free:
                            w.append(ring_free.pop(ri - 4))

                        def f(eng, hp=hp, bank=bank, kc=kc, c0=c0,
                              qsup=qsup):
                            return eng.matmul(
                                bank[:, c0:512],
                                kh[hp][:, kc * 128:(kc + 1) * 128],
                                qh[hp][:, qsup * 512 + c0:(qsup + 1) * 512],
                                start=True, stop=True)
                        P.do("tensor", f, waits=w, inc=[("tD", 1)],
                             tag=f"ST{g}_{qsup}_{hp}_{kc}")
                        t_st = P.at("tD")
                        pad = (qsup == 0 and ei == 3)
                        if pad:
                            def f(eng, tp=tp):
                                return eng.memset(ptb[tp][:, 0:384], 0.0)
                            P.do("gpsimd", f,
                                 waits=[("tD",
                                         t_after_pv.get(pctr - 4, 0))],
                                 inc=[("gD", 1)], tag=f"pad{g}_{hp}")
                            g_pad[(g, hp)] = P.at("gD")
                        w = [("tD", t_st),
                             ("tD", t_after_pv.get(pctr - 4, 0))]

                        def f(eng, tp=tp, bank=bank, c0=c0):
                            return eng.activation(ptb[tp][:, c0:512],
                                                  bank[:, c0:512], AF.Exp)
                        P.do("scalar", f, waits=w, inc=[("sD", 1)],
                             tag=f"exp{g}_{qsup}_{hp}_{kc}")
                        ring_free[ri] = ("sD", P.at("sD"))
                        s_exp = P.at("sD")
                        wpv = [("sD", s_exp)]
                        if j >= 0:
                            def f(eng, tp=tp, c0=c0):
                                return eng.tensor_tensor(
                                    ptb[tp][:, c0:c0 + 128],
                                    ptb[tp][:, c0:c0 + 128],
                                    maskt01, ALU.mult)
                            P.do("gpsimd", f, waits=[("sD", s_exp)],
                                 inc=[("gD", 1)],
                                 tag=f"dmask{g}_{qsup}_{hp}_{kc}")
                            wpv.append(("gD", P.at("gD")))
                        if pad:
                            wpv.append(("gD", g_pad[(g, hp)]))
                        if ei == 0:
                            wpv += yb_free[hp % 2]
                            wpv += [("vD", v_proj_done),
                                    ("sD", s_proj_done)]
                        pv_c0 = 0 if pad else c0

                        def f(eng, hp=hp, tp=tp, kc=kc, pv_c0=pv_c0, ei=ei,
                              ydst=ydst, nkc=nkc):
                            return eng.matmul(
                                ydst[0:65, pv_c0:512],
                                v_g4[:, kc, hp, 0:65],
                                ptb[tp][:, pv_c0:512],
                                start=(ei == 0), stop=(ei == nkc - 1))
                        P.do("tensor", f, waits=wpv, inc=[("tD", 1)],
                             tag=f"PV{g}_{qsup}_{hp}_{kc}")
                        t_after_pv[pctr] = P.at("tD")
                        pctr += 1
                        if si < len(sq):
                            hp2, qtl2, ks2 = sq[si]
                            si += 1
                            emit_schunk(qsup, hp2, qtl2, ks2, mb)
                        if hp == 0 and ei == 3 and pend_tail is not None:
                            pend_tail()
                            pend_tail = None
                    t_pv = P.at("tD")
                    po = 64 * (hp % 2)

                    def f(eng, hp=hp, ydst=ydst, po=po):
                        return eng.copy(
                            yscrT[po:po + 64, (hp // 2) * 512:
                                  (hp // 2) * 512 + 512],
                            ydst[0:64, 0:512])
                    P.do("scalar", f,
                         waits=[("tD", t_pv),
                                ("vD", fs["v_yn_hp"][hp])],
                         inc=[("sD", 1)], tag=f"yev{g}_{qsup}_{hp}")
                    yev_sems[hp] = P.at("sD")

                    def f(eng, hp=hp, ydst=ydst):
                        return eng.copy(zhi4[32 * hp:32 * hp + 1, :],
                                        ydst[64:65, 0:512])
                    P.do("scalar", f,
                         waits=[("tD", t_pv), ("tD", fs["t_zt"]),
                                ("gD", g_zinit)],
                         inc=[("sD", 1)], tag=f"zhi{g}_{qsup}_{hp}")
                    s_zhi = P.at("sD")

                    def f(eng, hp=hp, ydst=ydst):
                        return eng.tensor_tensor(
                            zlo4[32 * hp:32 * hp + 1, :],
                            ydst[64:65, 0:512],
                            zhi4[32 * hp:32 * hp + 1, :], ALU.subtract)
                    P.do("vector", f, waits=[("sD", s_zhi)],
                         inc=[("vD", 1)], tag=f"zlo{g}_{qsup}_{hp}")
                    zlo_sems[hp] = P.at("vD")
                    yb_free[hp % 2] = [("sD", yev_sems[hp]),
                                       ("vD", zlo_sems[hp])]

                def mk_tail(qsup=qsup, mb=mb, zlo_sems=list(zlo_sems),
                            yev_sems=list(yev_sems), g=g):
                    def tail():
                        for qtl in range(4):
                            w = []
                            if qtl == 0:
                                w = [("vD", zlo_sems[3]),
                                     ("vD", fs["v_yn"]),
                                     ("qS", q_esel),
                                     ("sD", s_proj_done)]

                            def f(eng, qtl=qtl):
                                eng.matmul(pb[4][:, 4 * qtl:4 * qtl + 4],
                                           zhi4[0:97,
                                                qtl * 128:(qtl + 1) * 128],
                                           e97sb, start=True, stop=False)
                                return eng.matmul(
                                    pb[4][:, 4 * qtl:4 * qtl + 4],
                                    zlo4[0:97, qtl * 128:(qtl + 1) * 128],
                                    e97sb, start=False, stop=True)
                            P.do("tensor", f, waits=w, inc=[("tD", 1)],
                                 tag=f"zt{g}_{qsup}_{qtl}")
                        fs["t_zt"] = P.at("tD")
                        t_zt = fs["t_zt"]
                        for qtl in range(4):
                            nc_ = qsup + (1 if qtl == 0 else 2)

                            def f(eng, qtl=qtl, nc_=nc_):
                                return eng.tensor_reduce(
                                    m4a[:, 4 * qtl:4 * qtl + 4].rearrange(
                                        "p (h o) -> p h o", o=1),
                                    mpart[:, mb + qtl * 32:
                                          mb + qtl * 32 + 32].rearrange(
                                        "p (h k) -> p h k", h=4)[:, :, 0:nc_],
                                    axis=AX.X, op=ALU.max)
                            P.do("vector", f, inc=[("vD", 1)],
                                 tag=f"m4{g}_{qsup}_{qtl}")
                        for qtl in range(4):
                            def f(eng, qtl=qtl):
                                return eng.tensor_tensor(
                                    dn1a[:, 4 * qtl:4 * qtl + 4],
                                    m4a[:, 4 * qtl:4 * qtl + 4],
                                    cbc[:, 4 * g:4 * (g + 1)], ALU.add)
                            P.do("vector", f, inc=[("vD", 1)],
                                 tag=f"dn1{g}_{qsup}_{qtl}")
                        v_dn1 = P.at("vD")
                        P.do("scalar",
                             lambda eng: eng.activation(ema, dn1a, AF.Exp),
                             waits=[("vD", v_dn1)], inc=[("sD", 1)],
                             tag=f"em{g}_{qsup}")
                        s_em = P.at("sD")
                        for qtl in range(4):
                            w = [("sD", s_em)]
                            if qtl == 0:
                                w.append(("tD", t_zt))

                            def f(eng, qtl=qtl):
                                return eng.tensor_tensor(
                                    dn2a[:, 4 * qtl:4 * qtl + 4],
                                    ema[:, 4 * qtl:4 * qtl + 4],
                                    pb[4][:, 4 * qtl:4 * qtl + 4], ALU.add)
                            P.do("vector", f, waits=w, inc=[("vD", 1)],
                                 tag=f"dn2{g}_{qsup}_{qtl}")
                        for qtl in range(4):
                            def f(eng, qtl=qtl):
                                return eng.reciprocal(
                                    rd4a[:, 4 * qtl:4 * qtl + 4],
                                    dn2a[:, 4 * qtl:4 * qtl + 4])
                            P.do("vector", f, inc=[("vD", 1)],
                                 tag=f"rd{g}_{qsup}_{qtl}")
                        P.do("vector",
                             lambda eng: eng.nop(cycle_cnt=200),
                             tag=f"rdnop{g}_{qsup}")
                        for qtl in range(4):
                            def f(eng, qtl=qtl):
                                return eng.tensor_copy(
                                    rdhl[:, 8 * qtl:8 * qtl + 4],
                                    rd4a[:, 4 * qtl:4 * qtl + 4])
                            P.do("vector", f, inc=[("vD", 1)],
                                 tag=f"rdhi{g}_{qsup}_{qtl}")
                        for qtl in range(4):
                            def f(eng, qtl=qtl):
                                return eng.tensor_tensor(
                                    rdhl[:, 8 * qtl + 4:8 * qtl + 8],
                                    rd4a[:, 4 * qtl:4 * qtl + 4],
                                    rdhl[:, 8 * qtl:8 * qtl + 4],
                                    ALU.subtract)
                            P.do("vector", f, inc=[("vD", 1)],
                                 tag=f"rdlo{g}_{qsup}_{qtl}")
                        v_rdhl = P.at("vD")
                        for qtl in range(4):
                            w = [("vD", v_rdhl)] if qtl == 0 else []

                            def f(eng, qtl=qtl):
                                return eng.matmul(
                                    pb[5][0:8, qtl * 128:(qtl + 1) * 128],
                                    rdhl[:, 8 * qtl:8 * qtl + 8], ident,
                                    start=True, stop=True)
                            P.do("tensor", f, waits=w, inc=[("tD", 1)],
                                 tag=f"rdT{g}_{qsup}_{qtl}")
                        t_rdT = P.at("tD")
                        P.do("vector",
                             lambda eng: eng.tensor_copy(rdT8s,
                                                         pb[5][0:8, :]),
                             waits=[("tD", t_rdT)], inc=[("vD", 1)],
                             tag=f"rdTc{g}_{qsup}")
                        v_rdT8s = P.at("vD")
                        t_rdbc = {}
                        for hp in range(4):
                            bank = 4 if hp < 2 else 5
                            poff = (hp % 2) * 64
                            w = [("vD", v_rdT8s)] if hp == 0 else []

                            def f(eng, hp=hp, bank=bank, poff=poff):
                                return eng.matmul(
                                    pb[bank][poff:poff + 64, :],
                                    e8sb[0:8, hp * 64:(hp + 1) * 64],
                                    rdT8s, start=True, stop=True)
                            P.do("tensor", f, waits=w, inc=[("tD", 1)],
                                 tag=f"rdbc{g}_{qsup}_{hp}")
                            t_rdbc[hp] = P.at("tD")
                        for hp in range(4):
                            h = g * HG + hp
                            fc = h // 2
                            po = 64 * (hp % 2)
                            bank = 4 if hp < 2 else 5

                            def f(eng, hp=hp, fc=fc, po=po, bank=bank,
                                  qsup=qsup):
                                return eng.tensor_tensor(
                                    y_sb[po:po + 64,
                                         fc * T + qsup * 512:
                                         fc * T + (qsup + 1) * 512],
                                    yscrT[po:po + 64, (hp // 2) * 512:
                                          (hp // 2) * 512 + 512],
                                    pb[bank][po:po + 64, 0:512], ALU.mult)
                            P.do("vector", f,
                                 waits=[("tD",
                                         t_rdbc[1 if hp < 2 else 3]),
                                        ("sD", yev_sems[hp])],
                                 inc=[("vD", 1)], tag=f"yn{g}_{qsup}_{hp}")
                            fs["v_yn_hp"][hp] = P.at("vD")
                        fs["v_yn"] = P.at("vD")
                    return tail
                pend_tail = mk_tail()
            if pend_tail is not None:
                pend_tail()
                pend_tail = None
            if debug and g == 0:
                P.do("sync", fdma(dbg["qh0"], qh[0][0:96, :]),
                     waits=[("sD", P.at("sD")), ("vD", P.at("vD"))],
                     inc=[("qS", 16)], tag="dbgqh0")
                P.do("sync", fdma(dbg["kh0"], kh[0][0:96, :]),
                     waits=[("sD", P.at("sD")), ("vD", P.at("vD"))],
                     inc=[("qS", 16)], tag="dbgkh0")
                P.do("sync", fdma(dbg["vg"], v_g),
                     waits=[("vD", P.at("vD"))],
                     inc=[("qS", 16)], tag="dbgvg")
            t_flash_done = P.at("tD")
            s_norm_done = P.at("sD")
            v_flash_done = fs["v_yn"]

        if debug:
            P.do("sync", fdma(dbg["y"], y_sb),
                 waits=[("sD", P.at("sD"))], inc=[("qS", 16)], tag="dbgy")
            P.do("sync", fdma(dbg["lat"], latent),
                 waits=[("sD", P.at("sD"))], inc=[("qS", 16)], tag="dbglat")
            P.do("sync", fdma(dbg["kro"], krope[0:96, :]),
                 waits=[("vD", P.at("vD"))], inc=[("qS", 16)], tag="dbgkro")
        P.barrier()
        A.freeit("xnb0", "xnb1", "wq_g", "wqr_g", "wku_g", "wvu_g",
                 "ctab", "stab2", "latent", "krope", "ro1", "ro2", "ro3",
                 "v_g", "ptb0", "ptb1", "ptb2", "ptb3", "mpart",
                 "yscrT", "zhi4", "zlo4",
                 "m4a", "dn1a", "ema", "dn2a", "rd4a", "rdhl", "rdT8s",
                 "e97", "e8",
                 *[f"qh{j}" for j in range(HG)],
                 *[f"kh{j}" for j in range(HG)])
        st["w1t2"] = [A.alloc(f"w1u{h}", 4 * HF * 2, BF16, [128, 4 * HF])
                      for h in range(2)]
        for h in range(2):
            P.do("sync",
                 fdma(st["w1t2"][h], w1bfd[:, h * 4 * HF:(h + 1) * 4 * HF]),
                 waits=[st["w1cache_q"]], inc=[("qS", 16)],
                 tag=f"w1prefetch{h}")
        st["w1t2_ready"] = ("qS", P.at("qS"))
        if sub < 4.5:
            A.freeit("y", "mask0", "maskt01", "cbc")
            P.barrier()
            return
        wo_sb = A.alloc("wo_sb", CE * E * 2, BF16, [128, CE * E])
        P.do("gpsimd", fdma(wo_sb.rearrange("p (c n) -> p c n", c=CE),
                            wod.rearrange("(c p) n -> p c n", p=128)),
             inc=[("qG", 16)], tag="ld_wo")
        q_wo = P.at("qG")

        for tt in range(NTT):
            for kc in range(CE):
                w = [("qG", q_wo)]
                if kc == 0:
                    w.append(("vD", st["prev_resid"]))

                def f(eng, kc=kc, tt=tt):
                    r = None
                    for m in range(8):
                        r = eng.matmul(
                            pb[m],
                            wo_sb[:, kc * E + m * 128:kc * E + (m + 1) * 128],
                            y_sb[:, kc * T + tt * TTW:kc * T + (tt + 1) * TTW],
                            start=(kc == 0), stop=(kc == CE - 1))
                    return r
                P.do("tensor", f, waits=w, inc=[("tD", 1)],
                     tag=f"wo{tt}_{kc}")
            t_wo = P.at("tD")
            for m in range(8):
                dst = stream[:, m * T + tt * TTW:m * T + (tt + 1) * TTW]

                def f(eng, m=m, dst=dst):
                    return eng.tensor_tensor(dst, dst, pb[m], ALU.add)
                P.do("vector", f, waits=[("tD", t_wo)], inc=[("vD", 1)],
                     tag=f"wores{tt}_{m}")
            st["prev_resid"] = P.at("vD")

        A.freeit("y", "wo_sb", "mask0", "maskt01", "cbc")
        P.barrier()

    emit_ffn(0)
    if stage >= 2:
        emit_attn(sub=sub)
    if stage >= 3:
        P.barrier()
        emit_ffn(1)

    def f(eng):
        return eng.dma_start(out=outT.rearrange("(c p) n -> p c n", p=128),
                             in_=stream.rearrange("p (c n) -> p c n", c=CE))
    P.do("sync", f, waits=[("vD", st["prev_resid"])], inc=[("qS", 16)],
         tag="st_out")
    P.do("sync", None, waits=[("qS", P.at("qS"))], tag="out_wait")

    P.check_deadlock()

    semctx = {n: nc.semaphore(n) for n in SEMS}
    semh = {n: c.__enter__() for n, c in semctx.items()}
    with nc.Block() as block:
        def runner(engname):
            def run(eng):
                seen = {}
                for waits, inc, fn, tag in P.ops[engname]:
                    for s, v in waits:
                        if seen.get(s, -1) >= v:
                            continue
                        seen[s] = v
                        eng.wait_ge(semh[s], v)
                    instr = fn(eng) if fn is not None else None
                    if inc:
                        if instr is None:
                            for s, a in inc:
                                eng.sem_inc(semh[s], a)
                        else:
                            for s, a in inc:
                                instr = instr.then_inc(semh[s], a)
            return run

        block.tensor(runner("tensor"))
        block.vector(runner("vector"))
        block.scalar(runner("scalar"))
        block.gpsimd(runner("gpsimd"))
        block.sync(runner("sync"))
    return nc


def _host_consts():
    ident = np.eye(128, dtype=ml_dtypes.bfloat16)
    i_ = np.arange(128)
    mask0 = np.where(i_[None, :] <= i_[:, None], 0.0, NEG).astype(np.float32)
    maskt01 = (i_[:, None] <= i_[None, :]).astype(ml_dtypes.bfloat16)
    d = np.arange(16, dtype=np.float64)
    inv = 1.0 / (10000.0 ** (d / 16.0))
    t = np.arange(T, dtype=np.float64)
    ang = t[None, :] * inv[:, None]
    c16, s16 = np.cos(ang), np.sin(ang)
    c32 = np.concatenate([c16, c16], axis=0)
    s2 = np.concatenate([s16, -s16], axis=0)
    ctab = np.tile(c32, (4, 1)).astype(ml_dtypes.bfloat16)
    stab2 = np.tile(s2, (4, 1)).astype(ml_dtypes.bfloat16)
    e97 = np.zeros((97, 4), dtype=ml_dtypes.bfloat16)
    for hp in range(4):
        e97[32 * hp, hp] = 1
    e8 = np.zeros((8, 256), dtype=ml_dtypes.bfloat16)
    for hp in range(4):
        e8[hp, hp * 64:(hp + 1) * 64] = 1
        e8[4 + hp, hp * 64:(hp + 1) * 64] = 1
    return ident, mask0, maskt01, ctab, stab2, e97, e8


def kernel(L0_attn_x, L0_attn_Wq, L0_attn_Wqr, L0_attn_Wkr, L0_attn_Wkvd,
           L0_attn_Wku, L0_attn_Wvu, L0_attn_Wo, L0_attn_lobo_log,
           L0_ffn_W1, L0_ffn_b1, L0_ffn_W2, L0_ffn_b2, L0_ffn_scale):
    x = np.asarray(L0_attn_x)
    B = x.shape[0]
    ident, mask0, maskt01, ctab, stab2, e97, e8 = _host_consts()
    shared = {
        "wq": np.ascontiguousarray(L0_attn_Wq, dtype=np.float32),
        "wqr": np.ascontiguousarray(L0_attn_Wqr, dtype=np.float32),
        "wkr": np.ascontiguousarray(L0_attn_Wkr, dtype=np.float32),
        "wkvd": np.ascontiguousarray(L0_attn_Wkvd, dtype=np.float32),
        "wku": np.ascontiguousarray(L0_attn_Wku, dtype=np.float32),
        "wvu": np.ascontiguousarray(L0_attn_Wvu, dtype=np.float32),
        "wo": np.ascontiguousarray(L0_attn_Wo, dtype=np.float32),
        "w1t": np.ascontiguousarray(np.asarray(L0_ffn_W1).T, dtype=np.float32),
        "w2t": np.ascontiguousarray(np.asarray(L0_ffn_W2).T, dtype=np.float32),
        "b1l": np.ascontiguousarray(
            np.asarray(L0_ffn_b1, dtype=np.float32).reshape(HC, 128).T),
        "b2row": np.asarray(L0_ffn_b2, dtype=np.float32).reshape(1, E).copy(),
        "scalebc": np.broadcast_to(
            np.asarray(L0_ffn_scale, dtype=np.float32).reshape(1, 1),
            (128, 1)).copy(),
        "lobobc": np.broadcast_to(
            np.asarray(L0_attn_lobo_log, dtype=np.float32).reshape(1, NH),
            (128, NH)).copy(),
        "ident": ident, "mask0": mask0, "maskt01": maskt01,
        "ctab": ctab, "stab2": stab2, "e97": e97, "e8": e8,
    }
    nc = build_program()
    in_maps = []
    for b in range(B):
        m = dict(shared)
        m["xT"] = np.ascontiguousarray(x[b].T, dtype=np.float32)
        in_maps.append(m)
    res = run_bass_kernel_spmd(nc, in_maps, core_ids=list(range(B)))
    out = np.stack([np.asarray(r["outT"]).T for r in res.results], axis=0)
    return np.ascontiguousarray(out, dtype=np.float32)


# version 08
# speedup vs baseline: 1.2589x; 1.2589x over previous
import math
import numpy as np
import ml_dtypes
import concourse.bass as bass
import concourse.mybir as mybir
from concourse.bass_utils import run_bass_kernel_spmd

F32 = mybir.dt.float32
BF16 = mybir.dt.bfloat16
U8 = mybir.dt.uint8
AF = mybir.ActivationFunctionType
ALU = mybir.AluOpType
AX = mybir.AxisListType

T = 2048
E = 1024
CE = 8
NH, DH, DR, DKV = 16, 64, 32, 256
HF = 4096
HC = 32
TTW = 512
NTT = T // TTW
NG, HG = 4, 4
QTN = T // 128
ISCALE = 1.0 / math.sqrt(DH + DR)
EPS = 1e-5
NEG = -1e30

ENGS = ("tensor", "vector", "scalar", "gpsimd", "sync")
SEMS = ("tD", "vD", "sD", "gD", "qS", "qG", "qC")

STAGE = 3


class Plan:
    def __init__(self):
        self.ops = {e: [] for e in ENGS}
        self.sem = {s: 0 for s in SEMS}

    def at(self, name):
        return self.sem[name]

    def do(self, eng, fn, waits=(), inc=(), tag=""):
        waits = tuple((s, v) for (s, v) in waits if v > 0)
        for s, a in inc:
            self.sem[s] += a
        self.ops[eng].append((waits, tuple(inc), fn, tag))

    def barrier(self):
        snap = [(s, self.sem[s]) for s in SEMS if self.sem[s] > 0]
        for e in ENGS:
            self.do(e, None, waits=snap, tag="barrier")

    def check_deadlock(self):
        idx = {e: 0 for e in ENGS}
        cnt = {s: 0 for s in self.sem}
        progress = True
        while progress:
            progress = False
            for e in ENGS:
                while idx[e] < len(self.ops[e]):
                    waits, inc, _, _ = self.ops[e][idx[e]]
                    if any(cnt[s] < v for (s, v) in waits):
                        break
                    for s, a in inc:
                        cnt[s] += a
                    idx[e] += 1
                    progress = True
        stuck = {e: idx[e] for e in ENGS if idx[e] < len(self.ops[e])}
        if stuck:
            lines = []
            for e in stuck:
                waits, _, _, tag = self.ops[e][idx[e]]
                lines.append(
                    f"{e}@{idx[e]}/{len(self.ops[e])} tag={tag} "
                    f"waits={[(s, v, cnt[s]) for s, v in waits]}")
            raise RuntimeError("PLAN DEADLOCK:\n" + "\n".join(lines))


class Arena:
    def __init__(self, ap, nbytes):
        self.ap = ap
        self.nbytes = nbytes
        self.free = [(0, nbytes)]
        self.regions = {}
        self.peak = 0

    def alloc(self, name, per_part_bytes, dtype, shape):
        assert name not in self.regions, name
        nb = (per_part_bytes + 31) & ~31
        for i, (off, sz) in enumerate(self.free):
            if sz >= nb:
                if sz == nb:
                    self.free.pop(i)
                else:
                    self.free[i] = (off + nb, sz - nb)
                view = self.ap[0:shape[0],
                               off:off + per_part_bytes].bitcast(dtype)
                want = per_part_bytes // mybir.dt.size(dtype)
                assert view.shape[1] == want, (name, view.shape, want)
                if len(shape) == 2:
                    assert shape[1] == want, (name, shape, want)
                self.regions[name] = (off, nb)
                used = self.nbytes - sum(s for _, s in self.free)
                self.peak = max(self.peak, used)
                return view
        raise RuntimeError(f"arena OOM {name} ({nb}B); free={self.free}")

    def freeit(self, *names):
        for name in names:
            off, nb = self.regions.pop(name)
            self.free.append((off, nb))
        self.free.sort()
        merged = []
        for off, sz in self.free:
            if merged and merged[-1][0] + merged[-1][1] == off:
                merged[-1] = (merged[-1][0], merged[-1][1] + sz)
            else:
                merged.append((off, sz))
        self.free = merged


def build_program(stage=STAGE, debug=False, sub=5):
    nc = bass.Bass("TRN2", target_bir_lowering=False,
                   detect_race_conditions=False)

    xT = nc.dram_tensor("xT", [E, T], F32, kind="ExternalInput").ap()
    wq = nc.dram_tensor("wq", [E, NH * DH], F32, kind="ExternalInput").ap()
    wqr = nc.dram_tensor("wqr", [E, NH * DR], F32, kind="ExternalInput").ap()
    wkrd = nc.dram_tensor("wkr", [E, DR], F32, kind="ExternalInput").ap()
    wkvdd = nc.dram_tensor("wkvd", [E, DKV], F32, kind="ExternalInput").ap()
    wkud = nc.dram_tensor("wku", [DKV, NH * DH], F32, kind="ExternalInput").ap()
    wvud = nc.dram_tensor("wvu", [DKV, NH * DH], F32, kind="ExternalInput").ap()
    wod = nc.dram_tensor("wo", [E, E], F32, kind="ExternalInput").ap()
    w1td = nc.dram_tensor("w1t", [E, HF], F32, kind="ExternalInput").ap()
    w2td = nc.dram_tensor("w2t", [HF, E], F32, kind="ExternalInput").ap()
    b1d = nc.dram_tensor("b1l", [128, HC], F32, kind="ExternalInput").ap()
    b2d = nc.dram_tensor("b2row", [1, E], F32, kind="ExternalInput").ap()
    scd = nc.dram_tensor("scalebc", [128, 1], F32, kind="ExternalInput").ap()
    lbd = nc.dram_tensor("lobobc", [128, NH], F32, kind="ExternalInput").ap()
    idd = nc.dram_tensor("ident", [128, 128], BF16, kind="ExternalInput").ap()
    m0d = nc.dram_tensor("mask0", [128, 128], F32, kind="ExternalInput").ap()
    mtd = nc.dram_tensor("maskt01", [128, 128], BF16, kind="ExternalInput").ap()
    ctd = nc.dram_tensor("ctab", [128, T], BF16, kind="ExternalInput").ap()
    s2d = nc.dram_tensor("stab2", [128, T], BF16, kind="ExternalInput").ap()
    e97d = nc.dram_tensor("e97", [97, 4], BF16, kind="ExternalInput").ap()
    e8d = nc.dram_tensor("e8", [8, 256], BF16, kind="ExternalInput").ap()
    outT = nc.dram_tensor("outT", [E, T], F32, kind="ExternalOutput").ap()
    xnd = nc.dram_tensor("xnd", [128, CE * T], BF16, kind="Internal").ap()
    dbg = {}
    if debug:
        dbg["xn"] = nc.dram_tensor("dbg_xn", [128, CE * TTW], BF16,
                                   kind="ExternalOutput").ap()
        dbg["g"] = nc.dram_tensor("dbg_g", [128, HC * TTW], BF16,
                                  kind="ExternalOutput").ap()
        dbg["al"] = nc.dram_tensor("dbg_al", [4, TTW], BF16,
                                   kind="ExternalOutput").ap()
        dbg["bt"] = nc.dram_tensor("dbg_bt", [16, TTW], BF16,
                                   kind="ExternalOutput").ap()
        dbg["lt"] = nc.dram_tensor("dbg_lt", [16, E], BF16,
                                   kind="ExternalOutput").ap()
        dbg["ub"] = nc.dram_tensor("dbg_ub", [128, TTW], F32,
                                   kind="ExternalOutput").ap()
        dbg["rb"] = nc.dram_tensor("dbg_rb", [128, TTW], F32,
                                   kind="ExternalOutput").ap()
        dbg["xn2"] = nc.dram_tensor("dbg_xn2", [128, CE * TTW], BF16,
                                    kind="ExternalOutput").ap()
        dbg["g2"] = nc.dram_tensor("dbg_g2", [128, HC * TTW], BF16,
                                   kind="ExternalOutput").ap()
        dbg["al2"] = nc.dram_tensor("dbg_al2", [4, TTW], BF16,
                                    kind="ExternalOutput").ap()
        dbg["y"] = nc.dram_tensor("dbg_y", [128, QTN * E], BF16,
                                  kind="ExternalOutput").ap()
        dbg["qh0"] = nc.dram_tensor("dbg_qh0", [96, T], BF16,
                                    kind="ExternalOutput").ap()
        dbg["kh0"] = nc.dram_tensor("dbg_kh0", [96, T], BF16,
                                    kind="ExternalOutput").ap()
        dbg["vg"] = nc.dram_tensor("dbg_vg", [128, 16 * HG * 65], BF16,
                                   kind="ExternalOutput").ap()
        dbg["st4"] = nc.dram_tensor("dbg_st4", [128, 40], F32,
                                    kind="ExternalOutput").ap()
        dbg["lat"] = nc.dram_tensor("dbg_lat", [128, 2 * T], BF16,
                                    kind="ExternalOutput").ap()
        dbg["kro"] = nc.dram_tensor("dbg_kro", [96, T], BF16,
                                    kind="ExternalOutput").ap()
        dbg["yscr"] = nc.dram_tensor("dbg_yscr", [128, 520], F32,
                                     kind="ExternalOutput").ap()
    w1bfd = nc.dram_tensor("w1bf", [128, CE * HF], BF16, kind="Internal").ap()
    w2bfd = nc.dram_tensor("w2bf", [HC, 128, E], BF16, kind="Internal").ap()

    ARENA_B = 212832
    arena_h = nc.alloc_sbuf_tensor("arena", [128, ARENA_B], U8)
    A = Arena(arena_h.ap(), ARENA_B)

    pb_ctx = [nc.psum_tensor(f"pb{i}", [128, 512], F32) for i in range(8)]
    pb = [c.__enter__()[:] for c in pb_ctx]

    P = Plan()

    stream = A.alloc("stream", T * 4 * CE, F32, [128, CE * T])
    ident = A.alloc("ident", 128 * 2, BF16, [128, 128])
    b1l = A.alloc("b1l", HC * 4, F32, [128, HC])
    b2row = A.alloc("b2row", E * 4, F32, [1, E])
    b2hl = A.alloc("b2hl", E * 2, BF16, [2, E])
    b2tmp = A.alloc("b2tmp", E * 2, BF16, [1, E])
    onesr = A.alloc("onesr", TTW * 2, BF16, [2, TTW])
    ones128 = A.alloc("ones128", 128 * 2, BF16, [128, 128])
    scalebc = A.alloc("scalebc", 32, F32, [128, 8])
    epsc = A.alloc("epsc", 32, F32, [128, 8])

    def fdma(outap, inap, slow=False):
        def f(eng):
            if slow:
                return eng.dma_start(out=outap, in_=inap,
                                     allow_slow_non_contiguous=True)
            return eng.dma_start(out=outap, in_=inap)
        return f

    st_xq = []
    for xt in range(NTT):
        P.do("sync",
             fdma(stream.rearrange("p (c n) -> p c n", c=CE)
                  [:, :, xt * TTW:(xt + 1) * TTW],
                  xT.rearrange("(c p) n -> p c n", p=128)
                  [:, :, xt * TTW:(xt + 1) * TTW]),
             inc=[("qS", 16)], tag=f"ld_x{xt}")
        st_xq.append(P.at("qS"))
    for ap_, src in ((ident, idd), (b1l, b1d), (b2row, b2d),
                     (scalebc[:, 0:1], scd)):
        P.do("sync", fdma(ap_, src), inc=[("qS", 16)], tag="ld_const")
    q_consts = P.at("qS")

    P.do("vector", lambda eng: eng.memset(onesr, 1.0), inc=[("vD", 1)])
    P.do("vector", lambda eng: eng.memset(epsc, EPS), inc=[("vD", 1)])
    P.do("vector", lambda eng: eng.memset(ones128, 1.0), inc=[("vD", 1)])

    def f_b2(eng):
        eng.tensor_copy(b2hl[0:1, :], b2row)
        eng.nop(cycle_cnt=400)
        return eng.tensor_tensor(b2tmp, b2row, b2hl[0:1, :], ALU.subtract)
    P.do("vector", f_b2, waits=[("qS", q_consts)], inc=[("vD", 1)], tag="b2")
    P.do("sync", fdma(b2hl[1:2, :], b2tmp), waits=[("vD", P.at("vD"))],
         inc=[("qS", 16)], tag="b2lo_mv")
    A.freeit("b2tmp")
    v_init = P.at("vD")
    s_init = P.at("sD")
    st_q_init = P.at("qS")
    A.freeit("b2row")

    st = dict(prev_resid=0, ln_prev_t=0, w2cache_q=0)

    def emit_ln_pre(tt, scr, cast_waits):
        xb2, x22, ub, rb, t1s, t1g = scr
        out = []
        for c in range(CE):
            pcol = slice(c * TTW, (c + 1) * TTW)
            src = stream[:, c * T + tt * TTW: c * T + (tt + 1) * TTW]

            def f(eng, src=src, dst=xb2[:, pcol]):
                return eng.tensor_copy(dst, src)
            P.do("vector", f, waits=cast_waits(c), inc=[("vD", 1)],
                 tag=f"lncast{c}")
            v_c = P.at("vD")

            def f(eng, i=xb2[:, pcol], o=x22[:, pcol]):
                return eng.activation(o, i, AF.Square)
            P.do("scalar", f, waits=[("vD", v_c)] + cast_waits(c + 10),
                 inc=[("sD", 1)], tag=f"lnsq{c}")
            out.append(P.at("sD"))
        return out

    def emit_ln_post(tt, xn, scr, s_sq, mm_waits):
        xb2, x22, ub, rb, t1s, t1g = scr
        for c in range(CE):
            pcol = slice(c * TTW, (c + 1) * TTW)
            w = [("sD", s_sq[c])]
            if c == 0:
                w += mm_waits

            def f(eng, c=c, pcol=pcol):
                eng.matmul(pb[0], ones128, xb2[:, pcol],
                           start=(c == 0), stop=(c == CE - 1))
                return eng.matmul(pb[1], ones128, x22[:, pcol],
                                  start=(c == 0), stop=(c == CE - 1))
            P.do("tensor", f, waits=w, inc=[("tD", 1)], tag=f"lnmm{c}")
        st["ln_prev_t"] = P.at("tD")
        t_stats = P.at("tD")

        def f(eng):
            return eng.tensor_scalar(ub, pb[0][:, 0:TTW], 1.0 / E, None,
                                     ALU.mult)
        P.do("vector", f, waits=[("tD", t_stats)], inc=[("vD", 1)],
             tag="lnst1")
        v1 = P.at("vD")
        P.do("scalar", lambda eng: eng.activation(t1s, ub, AF.Square),
             waits=[("vD", v1)], inc=[("sD", 1)], tag="lnst2")
        s1 = P.at("sD")
        P.do("vector",
             lambda eng: eng.scalar_tensor_tensor(
                 t1s, pb[1][:, 0:TTW], 1.0 / E, t1s, ALU.mult, ALU.subtract),
             waits=[("sD", s1)], inc=[("vD", 1)], tag="lnst3")
        v2 = P.at("vD")

        def f(eng):
            eng.activation(t1g, t1s, AF.Ln, bias=epsc[:, 0:1])
            return eng.activation(rb, t1g, AF.Exp, scale=-0.5)
        P.do("scalar", f, waits=[("vD", v2)], inc=[("sD", 1)], tag="lnst4")
        s_rb = P.at("sD")

        vnorm = []
        for c in range(CE):
            src = stream[:, c * T + tt * TTW: c * T + (tt + 1) * TTW]
            dst = xn[:, c * TTW:(c + 1) * TTW]

            def f(eng, src=src, dst=dst):
                eng.tensor_tensor(t1s, src, ub, ALU.subtract)
                return eng.tensor_tensor(dst, t1s, rb, ALU.mult)
            P.do("vector", f, waits=[("sD", s_rb)], inc=[("vD", 1)],
                 tag=f"lnnv{c}")
            vnorm.append(P.at("vD"))
        return [("vD", P.at("vD"))], vnorm

    def emit_ffn(call):
        g_sb = A.alloc("g", HC * TTW * 2, BF16, [128, HC * TTW])
        xb2 = g_sb[:, 0:CE * TTW]
        x22 = g_sb[:, CE * TTW:2 * CE * TTW]
        xn = A.alloc("xn", CE * TTW * 2, BF16, [128, CE * TTW])
        ub = A.alloc("ub", TTW * 4, F32, [128, TTW])
        rb = A.alloc("rb", TTW * 4, F32, [128, TTW])
        t1s = A.alloc("t1s", TTW * 4, F32, [128, TTW])
        t1g = A.alloc("t1g", TTW * 4, F32, [128, TTW])
        w2c = A.alloc("w2c", 3 * E * 2, BF16, [128, 3 * E])
        lt = g_sb[0:16, 16 * TTW:16 * TTW + E]
        xn16 = A.alloc("xn16", CE * 16 * 2, BF16, [128, CE * 16])
        alphas = A.alloc("alphas", TTW * 2, BF16, [4, TTW])
        bt = [A.alloc(f"bt{j}", TTW * 2, BF16, [4, TTW]) for j in range(4)]
        e4a = A.alloc("e4a", HF * 2, BF16, [4, HF])
        e16j = [A.alloc(f"e16j{j}", E * 2, BF16, [4, E]) for j in range(4)]
        scr = (xb2, x22, ub, rb, t1s, t1g)
        if call == 0:
            w1p = [A.alloc(f"w1t{h}", 4 * HF * 2, BF16, [128, 4 * HF])
                   for h in range(2)]
        else:
            w1p = st["w1t2"]

        if call == 0:
            for h in range(2):
                P.do("gpsimd",
                     fdma(w1p[h].rearrange("p (c n) -> p c n", c=4),
                          w1td[h * 512:(h + 1) * 512, :]
                          .rearrange("(c p) n -> p c n", p=128)),
                     waits=[("vD", v_init)], inc=[("qG", 16)],
                     tag=f"ld_w1_{h}")
            w1_ready = ("qG", P.at("qG"))
            for h in range(2):
                P.do("scalar",
                     fdma(w1bfd[:, h * 4 * HF:(h + 1) * 4 * HF], w1p[h]),
                     waits=[w1_ready], inc=[("qC", 16)], tag=f"w1cache{h}")
            st["w1cache_q"] = ("qC", P.at("qC"))
        else:
            w1_ready = st["w1t2_ready"]

        s_aev = 0
        t_w2used = {}
        pctr = 0
        v_xn16 = 0
        s_sq_next = None
        for tt in range(NTT):
            if s_sq_next is None:
                def cw(c, tt=tt):
                    w = [("tD", st["ln_prev_t"]),
                         ("vD", st.get("early_resid", 0))]
                    if call == 0:
                        w.append(("qS", st_xq[tt]))
                    return w
                s_sq_next = emit_ln_pre(tt, scr, cw)
            mmw = [("vD", st.get("early_resid", 0)), ("vD", v_init),
                   ("qS", q_consts)]
            xn_ready, vnorm = emit_ln_post(tt, xn, scr, s_sq_next, mmw)
            s_sq_next = None

            if tt == 0:
                def f(eng):
                    return eng.tensor_copy(
                        xn16.rearrange("p (c n) -> p c n", c=CE),
                        xn.rearrange("p (c n) -> p c n", c=CE)[:, :, 1:17])
                P.do("vector", f, waits=xn_ready, inc=[("vD", 1)], tag="xn16")
                v_xn16 = P.at("vD")
                s_prev = s_aev
                for c in range(CE):
                    def f(eng, c=c):
                        return eng.matmul(pb[2][0:16, 0:128],
                                          xn[:, c * TTW + 1:c * TTW + 17],
                                          ident, start=True, stop=True)
                    P.do("tensor", f,
                         waits=xn_ready + [("sD", s_prev)],
                         inc=[("tD", 1)], tag=f"ltmm{c}")
                    t_lt = P.at("tD")

                    def f(eng, c=c):
                        return eng.copy(lt[0:16, c * 128:(c + 1) * 128],
                                        pb[2][0:16, 0:128])
                    P.do("scalar", f, waits=[("tD", t_lt)], inc=[("sD", 1)],
                         tag=f"ltev{c}")
                    s_prev = P.at("sD")
                s_aev = s_prev
                s_lt = s_prev
                P.do("vector", lambda eng: eng.memset(e4a, 0.0),
                     inc=[("vD", 1)], tag="e4a0")
                for j in range(4):
                    P.do("vector",
                         lambda eng, j=j: eng.memset(e16j[j], 0.0),
                         inc=[("vD", 1)], tag=f"e16z{j}")
                v_ez = P.at("vD")
                for i in range(4):
                    P.do("sync",
                         fdma(e4a[i:i + 1, 1024 * i:1024 * (i + 1)],
                              lt[2 * i:2 * i + 1, :]),
                         waits=[("sD", s_lt), ("vD", v_ez)],
                         inc=[("qS", 16)], tag=f"e4a{i}")
                for i4 in range(4):
                    for j in range(4):
                        P.do("sync",
                             fdma(e16j[j][i4:i4 + 1,
                                          256 * i4:256 * i4 + 256],
                                  lt[8 + 2 * i4:9 + 2 * i4, j:j + 1021:4],
                                  slow=True),
                             waits=[("sD", s_lt), ("vD", v_ez)],
                             inc=[("qS", 16)], tag=f"e16_{i4}_{j}")
                q_e = P.at("qS")

            for c in range(CE):
                w = [("vD", vnorm[c]), ("vD", v_xn16)]
                if c == 0:
                    w.append(("sD", s_aev))

                def f(eng, c=c):
                    return eng.matmul(
                        pb[2][0:4, 0:TTW],
                        xn16[:, c * 16 + 1:c * 16 + 8:2],
                        xn[:, c * TTW:(c + 1) * TTW],
                        start=(c == 0), stop=(c == CE - 1))
                P.do("tensor", f, waits=w, inc=[("tD", 1)], tag=f"amm{c}")
            t_a = P.at("tD")

            def f(eng):
                return eng.activation(alphas, pb[2][0:4, 0:TTW], AF.Copy,
                                      scale=scalebc[0:4, 0:1])
            P.do("scalar", f, waits=[("tD", t_a)], inc=[("sD", 1)], tag="aev")
            s_aev = P.at("sD")

            s_gelu = {}
            for m in range(HC):
                par = m % 2
                if m < 2:
                    for kc in range(CE):
                        w = [("vD", vnorm[kc])]
                        if m == 0 and kc == 0:
                            w += [("sD", s_aev), w1_ready, ("qS", q_e)]

                        def f(eng, m=m, par=par, kc=kc):
                            wsl = w1p[kc // 4][:, (kc % 4) * HF + m * 128:
                                               (kc % 4) * HF + (m + 1) * 128]
                            return eng.matmul(
                                pb[par], wsl,
                                xn[:, kc * TTW:(kc + 1) * TTW],
                                start=(kc == 0), stop=False)
                        P.do("tensor", f, waits=w, inc=[("tD", 1)],
                             tag=f"hmm{m}_{kc}")

                    def f(eng, m=m, par=par):
                        return eng.matmul(
                            pb[par], e4a[0:4, m * 128:(m + 1) * 128],
                            alphas[0:4, :], start=False, stop=True)
                    P.do("tensor", f, waits=[("sD", s_aev)],
                         inc=[("tD", 1)], tag=f"hmm{m}_l")
                else:
                    w = list(xn_ready) + [("sD", s_aev), w1_ready]
                    w.append(("sD", s_gelu[m - 2]))

                    def f(eng, m=m, par=par):
                        for kc in range(CE):
                            wsl = w1p[kc // 4][:, (kc % 4) * HF + m * 128:
                                               (kc % 4) * HF + (m + 1) * 128]
                            eng.matmul(
                                pb[par], wsl,
                                xn[:, kc * TTW:(kc + 1) * TTW],
                                start=(kc == 0), stop=False)
                        return eng.matmul(
                            pb[par], e4a[0:4, m * 128:(m + 1) * 128],
                            alphas[0:4, :], start=False, stop=True)
                    P.do("tensor", f, waits=w, inc=[("tD", 1)], tag=f"hmm{m}")
                t_h = P.at("tD")

                def f(eng, m=m, par=par):
                    return eng.activation(g_sb[:, m * TTW:(m + 1) * TTW],
                                          pb[par], AF.Gelu,
                                          bias=b1l[:, m:m + 1])
                P.do("scalar", f, waits=[("tD", t_h)], inc=[("sD", 1)],
                     tag=f"gelu{m}")
                s_gelu[m] = P.at("sD")
            s_gelu_all = P.at("sD")

            for j in range(4):
                for c in range(CE):
                    kc = 8 * j + c
                    w = [("sD", s_gelu[kc])]
                    if c == 0:
                        w.append(("sD", s_aev))

                    def f(eng, j=j, c=c, kc=kc):
                        return eng.matmul(
                            pb[2][0:4, 0:TTW],
                            xn16[:, c * 16 + 9:c * 16 + 16:2],
                            g_sb[:, kc * TTW:(kc + 1) * TTW],
                            start=(c == 0), stop=(c == CE - 1))
                    P.do("tensor", f, waits=w, inc=[("tD", 1)],
                         tag=f"bmm{j}_{c}")
                t_b = P.at("tD")

                def f(eng, j=j):
                    return eng.activation(bt[j][0:4, :],
                                          pb[2][0:4, 0:TTW], AF.Copy,
                                          scale=scalebc[0:4, 0:1])
                P.do("scalar", f, waits=[("tD", t_b)], inc=[("sD", 1)],
                     tag=f"bev{j}")
                s_aev = P.at("sD")
            s_beta = P.at("sD")

            first = (call == 0 and tt == 0)
            qdma = {}
            qcache = {}
            NBUF = 3

            def w2buf(buf):
                return w2c[:, buf * E:(buf + 1) * E]

            def plan_w2_dma(i):
                buf = (base_p + i) % NBUF
                w = []
                if base_p + i >= NBUF:
                    w.append(("tD", t_w2used[base_p + i - NBUF]))
                if buf >= 2:
                    w.append(("sD", s_gelu_all))
                if first and i >= NBUF:
                    w.append(("qC", qcache[i - NBUF]))
                dst = w2buf(buf)
                if first:
                    P.do("gpsimd",
                         fdma(dst, w2td[i * 128:(i + 1) * 128, :]),
                         waits=w, inc=[("qG", 16)], tag=f"w2l{i}")
                    qdma[i] = ("qG", P.at("qG"))
                    P.do("scalar", fdma(w2bfd[i], dst),
                         waits=[qdma[i]], inc=[("qC", 16)], tag=f"w2c{i}")
                    qcache[i] = P.at("qC")
                else:
                    P.do("sync", fdma(dst, w2bfd[i]),
                         waits=w + [("qC", st["w2cache_q"])],
                         inc=[("qS", 16)], tag=f"w2l{i}")
                    qdma[i] = ("qS", P.at("qS"))

            def plan_w2_mm(kc):
                buf = (base_p + kc) % NBUF
                w = [qdma[kc]]
                if kc == 0:
                    w += [("sD", s_gelu_all), ("sD", s_beta),
                          ("vD", st["prev_resid"])]
                else:
                    w.append(("sD", s_gelu[kc]))
                src = w2buf(buf)

                def f(eng, kc=kc, src=src):
                    r = None
                    for m in range(8):
                        r = eng.matmul(
                            pb[m],
                            src[:, m * 128:(m + 1) * 128],
                            g_sb[:, kc * TTW:(kc + 1) * TTW],
                            start=(kc == 0), stop=False)
                    return r
                P.do("tensor", f, waits=w, inc=[("tD", 1)], tag=f"omm{kc}")
                t_w2used[base_p + kc] = P.at("tD")

            base_p = pctr
            for i in range(HC + 2):
                if i < HC:
                    plan_w2_dma(i)
                if i >= 2:
                    plan_w2_mm(i - 2)
            pctr = base_p + HC
            if first:
                st["w2cache_q"] = max(qcache.values())

            if tt + 1 < NTT:
                def cw(c, bp=base_p, tt1=tt + 1):
                    if c < 10:
                        w = [("tD", t_w2used[bp + c])]
                    else:
                        w = [("tD", t_w2used[bp + 8 + (c - 10)])]
                    if call == 0:
                        w.append(("qS", st_xq[tt1]))
                    return w
                s_sq_next = emit_ln_pre(tt + 1, scr, cw)

            def f(eng):
                r = None
                for m in range(8):
                    for j in range(4):
                        r = eng.matmul(pb[m],
                                       e16j[j][0:4, m * 128:(m + 1) * 128],
                                       bt[j][0:4, :],
                                       start=False, stop=False)
                    r = eng.matmul(pb[m], b2hl[0:2, m * 128:(m + 1) * 128],
                                   onesr[0:2, :], start=False, stop=True)
                return r
            P.do("tensor", f, waits=[("sD", s_beta), ("vD", v_init),
                                     ("qS", max(q_e, st_q_init))],
                 inc=[("tD", 1)], tag="oxtr")
            t_out = P.at("tD")

            for m in range(8):
                dst = stream[:, m * T + tt * TTW:m * T + (tt + 1) * TTW]

                def f(eng, m=m, dst=dst):
                    return eng.scalar_tensor_tensor(
                        dst, pb[m], 0.5, dst, ALU.mult, ALU.add)
                P.do("vector", f, waits=[("tD", t_out)], inc=[("vD", 1)],
                     tag=f"res{m}")
                if m == 1:
                    st["early_resid"] = P.at("vD")
            st["prev_resid"] = P.at("vD")
            st["ln_t_gate"] = t_out

        A.freeit("g", "xn", "ub", "rb", "t1s", "t1g",
                 "w2c", "xn16", "alphas", "e4a",
                 *(("w1t0", "w1t1") if call == 0 else ("w1u0", "w1u1")),
                 *[f"bt{j}" for j in range(4)],
                 *[f"e16j{j}" for j in range(4)])

    def emit_attn(sub=5):
        P.barrier()
        mask0 = A.alloc("mask0", 128 * 4, F32, [128, 128])
        maskt01 = A.alloc("maskt01", 128 * 2, BF16, [128, 128])
        cbc = A.alloc("cbc", NH * 4, F32, [128, NH])
        P.do("sync", fdma(mask0, m0d), inc=[("qS", 16)], tag="ld_mask0")
        P.do("sync", fdma(maskt01, mtd), inc=[("qS", 16)], tag="ld_maskt")
        P.do("sync", fdma(cbc, lbd), inc=[("qS", 16)], tag="ld_lobo")
        q_m = P.at("qS")
        xn = A.alloc("xn", CE * TTW * 2, BF16, [128, CE * TTW])
        xb2 = A.alloc("xb2", CE * TTW * 2, BF16, [128, CE * TTW])
        x22 = A.alloc("x22", CE * TTW * 2, BF16, [128, CE * TTW])
        ub = A.alloc("ub", TTW * 4, F32, [128, TTW])
        rb = A.alloc("rb", TTW * 4, F32, [128, TTW])
        t1s = A.alloc("t1s", TTW * 4, F32, [128, TTW])
        t1g = A.alloc("t1g", TTW * 4, F32, [128, TTW])
        scr = (xb2, x22, ub, rb, t1s, t1g)
        latent = A.alloc("latent", 2 * T * 2, BF16, [128, 2 * T])
        krope = A.alloc("krope", T * 2, BF16, [96, T])
        ctab = A.alloc("ctab", T * 2, BF16, [128, T])
        stab2 = A.alloc("stab2", T * 2, BF16, [128, T])
        ro1 = A.alloc("ro1", TTW * 4, F32, [128, TTW])
        ro2 = A.alloc("ro2", TTW * 4, F32, [128, TTW])
        ro3 = A.alloc("ro3", TTW * 4, F32, [128, TTW])
        wkvd_sb = A.alloc("wkvd_sb", CE * DKV * 2, BF16, [128, CE * DKV])
        wkr_sb = A.alloc("wkr_sb", CE * DR * 2, BF16, [128, CE * DR])

        P.do("sync", fdma(ctab, ctd), inc=[("qS", 16)], tag="ld_ctab")
        P.do("sync", fdma(stab2, s2d), inc=[("qS", 16)], tag="ld_stab")
        q_tabs = P.at("qS")
        P.do("gpsimd", fdma(wkvd_sb.rearrange("p (c n) -> p c n", c=CE),
                            wkvdd.rearrange("(c p) n -> p c n", p=128)),
             inc=[("qG", 16)], tag="ld_wkvd")
        P.do("gpsimd", fdma(wkr_sb.rearrange("p (c n) -> p c n", c=CE),
                            wkrd.rearrange("(c p) n -> p c n", p=128)),
             inc=[("qG", 16)], tag="ld_wkr")
        q_kv = P.at("qG")

        s_latent_ev = {}
        v_krope = 0
        q_spill = [0] * NTT
        s_sq_next = None
        for tt in range(NTT):
            if s_sq_next is None:
                def cw(c):
                    return [("tD", st["ln_prev_t"]),
                            ("vD", st.get("early_resid", 0))]
                s_sq_next = emit_ln_pre(tt, scr, cw)
            mmw = [("vD", st.get("early_resid", 0)), ("vD", v_init),
                   ("qS", q_consts)]
            xn_ready, vnorm = emit_ln_post(tt, xn, scr, s_sq_next, mmw)
            s_sq_next = None
            P.do("sync",
                 fdma(xnd.rearrange("p (c n) -> p c n", c=CE)
                      [:, :, tt * TTW:(tt + 1) * TTW],
                      xn.rearrange("p (c n) -> p c n", c=CE)),
                 waits=xn_ready, inc=[("qS", 16)], tag=f"xnspill{tt}")
            q_spill[tt] = P.at("qS")
            if tt + 1 < NTT:
                def cw(c):
                    return [("tD", st["ln_prev_t"])]
                s_sq_next = emit_ln_pre(tt + 1, scr, cw)
            for mc in range(2):
                for kc in range(CE):
                    w = list(xn_ready) if kc == 0 else []
                    if kc == 0:
                        w += [("qG", q_kv),
                              ("sD", s_latent_ev.get(mc, 0))]

                    def f(eng, mc=mc, kc=kc):
                        return eng.matmul(
                            pb[2 + mc],
                            wkvd_sb[:, kc * DKV + mc * 128:
                                    kc * DKV + (mc + 1) * 128],
                            xn[:, kc * TTW:(kc + 1) * TTW],
                            start=(kc == 0), stop=(kc == CE - 1))
                    P.do("tensor", f, waits=w, inc=[("tD", 1)],
                         tag=f"lat{mc}_{kc}")
                t_l = P.at("tD")

                def f(eng, mc=mc, tt=tt):
                    return eng.copy(
                        latent[:, mc * T + tt * TTW:mc * T + (tt + 1) * TTW],
                        pb[2 + mc])
                P.do("scalar", f, waits=[("tD", t_l)], inc=[("sD", 1)],
                     tag=f"latev{mc}")
                s_latent_ev[mc] = P.at("sD")
            for kc in range(CE):
                w = list(xn_ready) if kc == 0 else []
                if kc == 0:
                    w += [("qG", q_kv), ("vD", v_krope)]

                def f(eng, kc=kc):
                    return eng.matmul(
                        pb[4][64:96, 0:TTW],
                        wkr_sb[:, kc * DR:(kc + 1) * DR],
                        xn[:, kc * TTW:(kc + 1) * TTW],
                        start=(kc == 0), stop=(kc == CE - 1))
                P.do("tensor", f, waits=w, inc=[("tD", 1)], tag=f"krmm{kc}")
            t_kr = P.at("tD")
            cs = slice(tt * TTW, (tt + 1) * TTW)

            def f(eng, cs=cs):
                eng.tensor_scalar(ro3[64:96, :], pb[4][64:96, 0:TTW], ISCALE,
                                  None, ALU.mult)
                eng.tensor_tensor(ro1[64:96, :], ro3[64:96, :],
                                  ctab[64:96, cs], ALU.mult)
                eng.tensor_tensor(ro2[64:96, :], ro3[64:96, :],
                                  stab2[64:96, cs], ALU.mult)
                mask = [(p + 16) % 32 for p in range(32)]
                eng.stream_shuffle(ro3[64:96, :], ro2[64:96, :], mask)
                return eng.tensor_tensor(krope[64:96, cs], ro1[64:96, :],
                                         ro3[64:96, :], ALU.add)
            P.do("vector", f, waits=[("tD", t_kr), ("qS", q_tabs)],
                 inc=[("vD", 1)], tag=f"krrope{tt}")
            v_krope = P.at("vD")
        v_krope_all = P.at("vD")
        A.freeit("xn", "xb2", "x22", "ub", "rb", "t1s", "t1g",
                 "wkvd_sb", "wkr_sb")

        if sub < 1.5:
            A.freeit("mask0", "maskt01", "cbc", "latent", "krope",
                     "ctab", "stab2", "ro1", "ro2", "ro3")
            return
        P.barrier()
        y_sb = A.alloc("y", QTN * E * 2, BF16, [128, QTN * E])
        qh = [A.alloc(f"qh{j}", T * 2, BF16, [96, T]) for j in range(HG)]
        kh = [A.alloc(f"kh{j}", T * 2, BF16, [96, T]) for j in range(HG)]
        v_g = A.alloc("v_g", 16 * HG * 65 * 2, BF16, [128, 16 * HG * 65])
        v_g4 = v_g.rearrange("p (c h n) -> p c h n", c=16, h=HG)
        xnb = [A.alloc(f"xnb{j}", CE * TTW * 2, BF16, [128, CE * TTW])
               for j in range(2)]
        wq_g = A.alloc("wq_g", CE * 256 * 2, BF16, [128, CE * 256])
        wqr_g = A.alloc("wqr_g", CE * 128 * 2, BF16, [128, CE * 128])
        wku_g = A.alloc("wku_g", 2 * 256 * 2, BF16, [128, 2 * 256])
        wvu_g = A.alloc("wvu_g", 2 * 256 * 2, BF16, [128, 2 * 256])
        ptb = [A.alloc(f"ptb{j}", 512 * 2, BF16, [128, 512])
               for j in range(4)]
        yscrT = A.alloc("yscrT", 1024 * 2, BF16, [128, 1024])
        zhi4 = A.alloc("zhi4", 512 * 2, BF16, [97, 512])
        zlo4 = A.alloc("zlo4", 512 * 2, BF16, [97, 512])
        mpart = A.alloc("mpart", 256 * 4, F32, [128, 256])
        m4a = A.alloc("m4a", 16 * 4, F32, [128, 16])
        dn1a = A.alloc("dn1a", 16 * 4, F32, [128, 16])
        ema = A.alloc("ema", 16 * 4, F32, [128, 16])
        dn2a = A.alloc("dn2a", 16 * 4, F32, [128, 16])
        rd4a = A.alloc("rd4a", 16 * 4, F32, [128, 16])
        rdhl = A.alloc("rdhl", 32 * 2, BF16, [128, 32])
        rdT8s = A.alloc("rdT8s", 512 * 2, BF16, [8, 512])
        e97sb = A.alloc("e97", 4 * 2, BF16, [97, 4])
        e8sb = A.alloc("e8", 256 * 2, BF16, [8, 256])
        P.do("sync", fdma(e97sb, e97d), inc=[("qS", 16)], tag="ld_e97")
        P.do("sync", fdma(e8sb, e8d), inc=[("qS", 16)], tag="ld_e8")
        q_esel = P.at("qS")

        def f(eng):
            return eng.memset(v_g4[:, :, :, 64:65], 1.0)
        P.do("vector", f, inc=[("vD", 1)], tag="vones")
        v_vones = P.at("vD")

        def f(eng):
            eng.memset(zhi4, 0.0)
            return eng.memset(zlo4, 0.0)
        P.do("gpsimd", f, inc=[("gD", 1)], tag="zinit")
        g_zinit = P.at("gD")
        fs = {"v_yn": 0, "t_zt": 0, "v_yn_hp": [0, 0, 0, 0]}

        t_flash_done = 0
        s_norm_done = 0
        v_flash_done = 0
        t_projmm_done = 0
        t_projuse = {}
        s_pevac = {}
        v_pevac = {}
        ngrun = 1 if sub < 3.5 else NG
        for g in range(ngrun):
            w = [("tD", t_projmm_done)]
            P.do("gpsimd", fdma(wq_g.rearrange("p (c n) -> p c n", c=CE),
                                wq[:, 256 * g:256 * (g + 1)]
                                .rearrange("(c p) n -> p c n", p=128)),
                 waits=w, inc=[("qG", 16)], tag=f"gwq{g}")
            P.do("gpsimd", fdma(wqr_g.rearrange("p (c n) -> p c n", c=CE),
                                wqr[:, 128 * g:128 * (g + 1)]
                                .rearrange("(c p) n -> p c n", p=128)),
                 waits=w, inc=[("qG", 16)], tag=f"gwqr{g}")
            P.do("gpsimd", fdma(wku_g.rearrange("p (c n) -> p c n", c=2),
                                wkud[:, 256 * g:256 * (g + 1)]
                                .rearrange("(c p) n -> p c n", p=128)),
                 waits=w, inc=[("qG", 16)], tag=f"gwku{g}")
            P.do("gpsimd", fdma(wvu_g.rearrange("p (c n) -> p c n", c=2),
                                wvud[:, 256 * g:256 * (g + 1)]
                                .rearrange("(c p) n -> p c n", p=128)),
                 waits=w, inc=[("qG", 16)], tag=f"gwvu{g}")
            q_gw = P.at("qG")

            q_xnb = [0, 0]
            for tt in range(NTT):
                buf = tt % 2
                w = [("qS", q_spill[tt]), ("tD", t_projmm_done)]
                if tt >= 2:
                    w.append(("tD", t_projuse[tt - 2]))

                def f(eng, tt=tt, buf=buf):
                    return eng.dma_start(
                        out=xnb[buf].rearrange("p (c n) -> p c n", c=CE),
                        in_=xnd.rearrange("p (c n) -> p c n", c=CE)
                        [:, :, tt * TTW:(tt + 1) * TTW])
                P.do("sync", f, waits=w, inc=[("qS", 16)],
                     tag=f"xnl{g}_{tt}")
                q_xnb[buf] = P.at("qS")

                xg = xnb[buf]
                cs = slice(tt * TTW, (tt + 1) * TTW)
                basew = [("qS", q_xnb[buf]), ("qG", q_gw),
                         ("tD", t_flash_done), ("sD", s_norm_done),
                         ("vD", v_flash_done)]
                if sub < 1.65:
                    t_projuse[tt] = P.at("tD")
                    continue
                if sub >= 1.65:
                    for hp in range(HG):
                        bank = hp % 2
                        for kc in range(CE):
                            w = list(basew) if (kc == 0) else []
                            if kc == 0:
                                w.append(("sD", s_pevac.get(("qc", bank), 0)))

                            def f(eng, hp=hp, kc=kc, xg=xg, bank=bank):
                                return eng.matmul(
                                    pb[bank][0:64, 0:TTW],
                                    wq_g[:, kc * 256 + 64 * hp:
                                         kc * 256 + 64 * hp + 64],
                                    xg[:, kc * TTW:(kc + 1) * TTW],
                                    start=(kc == 0), stop=(kc == CE - 1))
                            P.do("tensor", f, waits=w, inc=[("tD", 1)],
                                 tag=f"qc{g}_{tt}_{hp}_{kc}")
                        t_qc = P.at("tD")

                        def f(eng, hp=hp, bank=bank, cs=cs):
                            return eng.copy(qh[hp][0:64, cs],
                                            pb[bank][0:64, 0:TTW])
                        P.do("scalar", f, waits=[("tD", t_qc)],
                             inc=[("sD", 1)], tag=f"qcev{g}_{tt}_{hp}")
                        s_pevac[("qc", bank)] = P.at("sD")
                if sub < 1.75:
                    t_projuse[tt] = P.at("tD")
                    continue
                for hp in range(HG):
                    bank = 2 + hp % 2
                    for kc in range(CE):
                        w = list(basew) if kc == 0 else []
                        if kc == 0:
                            w.append(("vD", v_pevac.get(("qr", bank), 0)))

                        def f(eng, hp=hp, kc=kc, xg=xg, bank=bank):
                            return eng.matmul(
                                pb[bank][64:96, 0:TTW],
                                wqr_g[:, kc * 128 + 32 * hp:
                                      kc * 128 + 32 * hp + 32],
                                xg[:, kc * TTW:(kc + 1) * TTW],
                                start=(kc == 0), stop=(kc == CE - 1))
                        P.do("tensor", f, waits=w, inc=[("tD", 1)],
                             tag=f"qr{g}_{tt}_{hp}_{kc}")
                    t_qr = P.at("tD")

                    def f(eng, hp=hp, bank=bank, cs=cs):
                        eng.tensor_tensor(ro1[64:96, :], pb[bank][64:96, 0:TTW],
                                          ctab[64:96, cs], ALU.mult)
                        eng.tensor_tensor(ro2[64:96, :], pb[bank][64:96, 0:TTW],
                                          stab2[64:96, cs], ALU.mult)
                        mask = [(p + 16) % 32 for p in range(32)]
                        eng.stream_shuffle(ro3[64:96, :], ro2[64:96, :], mask)
                        return eng.tensor_tensor(qh[hp][64:96, cs],
                                                 ro1[64:96, :],
                                                 ro3[64:96, :], ALU.add)
                    P.do("vector", f, waits=[("tD", t_qr), ("qS", q_tabs)],
                         inc=[("vD", 1)], tag=f"qrope{g}_{tt}_{hp}")
                    v_pevac[("qr", bank)] = P.at("vD")
                if sub < 1.85:
                    t_projuse[tt] = P.at("tD")
                    continue
                for hp in range(HG):
                    bank = 4 if hp % 2 == 0 else 6
                    for lc in range(2):
                        w = list(basew) if lc == 0 else []
                        if lc == 0:
                            w.append(("sD", s_pevac.get(("kc", bank), 0)))

                        def f(eng, hp=hp, lc=lc, tt=tt, bank=bank):
                            return eng.matmul(
                                pb[bank][0:64, 0:TTW],
                                wku_g[:, lc * 256 + 64 * hp:
                                      lc * 256 + 64 * hp + 64],
                                latent[:, lc * T + tt * TTW:
                                       lc * T + (tt + 1) * TTW],
                                start=(lc == 0), stop=(lc == 1))
                        P.do("tensor", f, waits=w, inc=[("tD", 1)],
                             tag=f"kc{g}_{tt}_{hp}_{lc}")
                    t_kc = P.at("tD")

                    def f(eng, hp=hp, bank=bank, cs=cs):
                        return eng.activation(
                            kh[hp][0:64, cs], pb[bank][0:64, 0:TTW],
                            AF.Copy, scale=ISCALE)
                    P.do("scalar", f, waits=[("tD", t_kc)],
                         inc=[("sD", 1)], tag=f"kcev{g}_{tt}_{hp}")
                    s_pevac[("kc", bank)] = P.at("sD")
                if sub < 1.95:
                    t_projuse[tt] = P.at("tD")
                    continue
                for ts_ in range(4):
                    chunk = 4 * tt + ts_
                    bank = 5 if ts_ % 2 == 0 else 7
                    for lc in range(2):
                        w = list(basew) if lc == 0 else []
                        if lc == 0:
                            w += [("vD", v_vones),
                                  ("vD", v_pevac.get(("v", bank), 0))]

                        def f(eng, ts_=ts_, lc=lc, tt=tt, bank=bank):
                            return eng.matmul(
                                pb[bank][:, 0:256],
                                latent[:, lc * T + tt * TTW + ts_ * 128:
                                       lc * T + tt * TTW + (ts_ + 1) * 128],
                                wvu_g[:, lc * 256:(lc + 1) * 256],
                                start=(lc == 0), stop=(lc == 1))
                        P.do("tensor", f, waits=w, inc=[("tD", 1)],
                             tag=f"v{g}_{tt}_{ts_}_{lc}")
                    t_v = P.at("tD")

                    def f(eng, chunk=chunk, bank=bank):
                        return eng.tensor_copy(
                            v_g4[:, chunk:chunk + 1, :, 0:64],
                            pb[bank][:, 0:256]
                            .rearrange("p (o h n) -> p o h n", o=1, h=4))
                    P.do("vector", f, waits=[("tD", t_v)], inc=[("vD", 1)],
                         tag=f"vev{g}_{tt}_{ts_}")
                    v_pevac[("v", bank)] = P.at("vD")
                t_projuse[tt] = P.at("tD")
            t_projmm_done = P.at("tD")
            s_proj_done = P.at("sD")
            v_proj_done = P.at("vD")

            for j in range(HG):
                def f(eng, j=j):
                    return eng.tensor_copy(kh[j][64:96, :], krope[64:96, :])
                P.do("vector", f,
                     waits=[("vD", v_krope_all), ("tD", t_flash_done)],
                     inc=[("vD", 1)], tag=f"krcp{g}_{j}")
            v_krcp = P.at("vD")

            if sub < 2.5:
                continue
            rs = {"r": 0}
            ring_free = {}
            pctr = 0
            t_after_pv = {}
            g_pad = {}
            yb_free = {0: [], 1: []}
            pend_tail = None

            def emit_schunk(qsup, hp, qtl, ks, mb):
                ri = rs["r"]
                rs["r"] += 1
                bank = pb[ri % 4]
                qt = 4 * qsup + qtl
                qs = slice(qt * 128, (qt + 1) * 128)
                last = (ks == qsup)
                wd = (qtl + 1) * 128 if last else 512
                w = [("sD", s_proj_done), ("vD", v_krcp)]
                if ri - 4 in ring_free:
                    w.append(ring_free.pop(ri - 4))

                def f(eng, hp=hp, bank=bank, ks=ks, wd=wd, qs=qs):
                    return eng.matmul(
                        bank[:, 0:wd], qh[hp][:, qs],
                        kh[hp][:, ks * 512:ks * 512 + wd],
                        start=True, stop=True)
                P.do("tensor", f, waits=w, inc=[("tD", 1)],
                     tag=f"S{g}_{qt}_{hp}_{ks}")
                t_s = P.at("tD")
                mc = mb + qtl * 32 + hp * 8 + ks
                if not last:
                    def f(eng, bank=bank, wd=wd, mc=mc):
                        return eng.tensor_reduce(
                            mpart[:, mc:mc + 1], bank[:, 0:wd],
                            axis=AX.X, op=ALU.max)
                    P.do("vector", f, waits=[("tD", t_s)],
                         inc=[("vD", 1)], tag=f"Smax{g}_{qt}_{hp}_{ks}")
                else:
                    mcd = mc if wd == 128 else mc + 1

                    def f(eng, bank=bank, wd=wd, mc=mc, mcd=mcd):
                        eng.tensor_tensor(bank[:, wd - 128:wd],
                                          bank[:, wd - 128:wd],
                                          mask0, ALU.add)
                        if wd > 128:
                            eng.tensor_reduce(
                                mpart[:, mc:mc + 1], bank[:, 0:wd - 128],
                                axis=AX.X, op=ALU.max)
                        else:
                            eng.nop(cycle_cnt=300)
                        return eng.tensor_reduce(
                            mpart[:, mcd:mcd + 1], bank[:, wd - 128:wd],
                            axis=AX.X, op=ALU.max)
                    P.do("vector", f, waits=[("tD", t_s)],
                         inc=[("vD", 1)], tag=f"Smaxd{g}_{qt}_{hp}")
                ring_free[ri] = ("vD", P.at("vD"))

            for qsup in range(4):
                mb = (qsup % 2) * 128
                zlo_sems = [0, 0, 0, 0]
                yev_sems = [0, 0, 0, 0]
                sq = [(hp2, qtl, ks) for hp2 in range(HG)
                      for qtl in range(4) for ks in range(qsup + 1)]
                si = 0
                for hp in range(HG):
                    kcs = ([4 * qsup + j for j in range(4)] +
                           list(range(4 * qsup)))
                    nkc = len(kcs)
                    ydst = pb[6 + hp % 2]
                    pend_pv = []
                    for ei, kc in enumerate(kcs):
                        j = kc - 4 * qsup
                        c0 = j * 128 if j > 0 else 0
                        ri = rs["r"]
                        rs["r"] += 1
                        bank = pb[ri % 4]
                        tp = pctr % 4
                        w = [("sD", s_proj_done), ("vD", v_krcp)]
                        if ri - 4 in ring_free:
                            w.append(ring_free.pop(ri - 4))

                        def f(eng, hp=hp, bank=bank, kc=kc, c0=c0,
                              qsup=qsup):
                            return eng.matmul(
                                bank[:, c0:512],
                                kh[hp][:, kc * 128:(kc + 1) * 128],
                                qh[hp][:, qsup * 512 + c0:(qsup + 1) * 512],
                                start=True, stop=True)
                        P.do("tensor", f, waits=w, inc=[("tD", 1)],
                             tag=f"ST{g}_{qsup}_{hp}_{kc}")
                        t_st = P.at("tD")
                        pad = (qsup == 0 and ei == 3)
                        if pad:
                            def f(eng, tp=tp):
                                return eng.memset(ptb[tp][:, 0:384], 0.0)
                            P.do("gpsimd", f,
                                 waits=[("tD",
                                         t_after_pv.get(pctr - 4, 0))],
                                 inc=[("gD", 1)], tag=f"pad{g}_{hp}")
                            g_pad[(g, hp)] = P.at("gD")
                        w = [("tD", t_st),
                             ("tD", t_after_pv.get(pctr - 4, 0))]

                        def f(eng, tp=tp, bank=bank, c0=c0):
                            return eng.activation(ptb[tp][:, c0:512],
                                                  bank[:, c0:512], AF.Exp)
                        P.do("scalar", f, waits=w, inc=[("sD", 1)],
                             tag=f"exp{g}_{qsup}_{hp}_{kc}")
                        ring_free[ri] = ("sD", P.at("sD"))
                        s_exp = P.at("sD")
                        wpv = [("sD", s_exp)]
                        if j >= 0:
                            def f(eng, tp=tp, c0=c0):
                                return eng.tensor_tensor(
                                    ptb[tp][:, c0:c0 + 128],
                                    ptb[tp][:, c0:c0 + 128],
                                    maskt01, ALU.mult)
                            P.do("gpsimd", f, waits=[("sD", s_exp)],
                                 inc=[("gD", 1)],
                                 tag=f"dmask{g}_{qsup}_{hp}_{kc}")
                            wpv.append(("gD", P.at("gD")))
                        if pad:
                            wpv.append(("gD", g_pad[(g, hp)]))
                        if ei == 0:
                            wpv += yb_free[hp % 2]
                            wpv += [("vD", v_proj_done),
                                    ("sD", s_proj_done)]
                        pv_c0 = 0 if pad else c0

                        def fpv(eng, hp=hp, tp=tp, kc=kc, pv_c0=pv_c0,
                                ei=ei, ydst=ydst, nkc=nkc):
                            return eng.matmul(
                                ydst[0:65, pv_c0:512],
                                v_g4[:, kc, hp, 0:65],
                                ptb[tp][:, pv_c0:512],
                                start=(ei == 0), stop=(ei == nkc - 1))
                        if si < len(sq):
                            hp2, qtl2, ks2 = sq[si]
                            si += 1
                            emit_schunk(qsup, hp2, qtl2, ks2, mb)
                        for fq, wq_, pc_, kq in pend_pv:
                            P.do("tensor", fq, waits=wq_, inc=[("tD", 1)],
                                 tag=f"PV{g}_{qsup}_{hp}_{kq}")
                            t_after_pv[pc_] = P.at("tD")
                        pend_pv = [(fpv, wpv, pctr, kc)]
                        pctr += 1
                        if hp == 0 and ei == 3 and pend_tail is not None:
                            pend_tail()
                            pend_tail = None
                    for fq, wq_, pc_, kq in pend_pv:
                        P.do("tensor", fq, waits=wq_, inc=[("tD", 1)],
                             tag=f"PV{g}_{qsup}_{hp}_{kq}")
                        t_after_pv[pc_] = P.at("tD")
                    pend_pv = []
                    t_pv = P.at("tD")
                    po = 64 * (hp % 2)

                    def f(eng, hp=hp, ydst=ydst, po=po):
                        return eng.copy(
                            yscrT[po:po + 64, (hp // 2) * 512:
                                  (hp // 2) * 512 + 512],
                            ydst[0:64, 0:512])
                    P.do("scalar", f,
                         waits=[("tD", t_pv),
                                ("vD", fs["v_yn_hp"][hp])],
                         inc=[("sD", 1)], tag=f"yev{g}_{qsup}_{hp}")
                    yev_sems[hp] = P.at("sD")

                    def f(eng, hp=hp, ydst=ydst):
                        return eng.copy(zhi4[32 * hp:32 * hp + 1, :],
                                        ydst[64:65, 0:512])
                    P.do("scalar", f,
                         waits=[("tD", t_pv), ("tD", fs["t_zt"]),
                                ("gD", g_zinit)],
                         inc=[("sD", 1)], tag=f"zhi{g}_{qsup}_{hp}")
                    s_zhi = P.at("sD")

                    def f(eng, hp=hp, ydst=ydst):
                        return eng.tensor_tensor(
                            zlo4[32 * hp:32 * hp + 1, :],
                            ydst[64:65, 0:512],
                            zhi4[32 * hp:32 * hp + 1, :], ALU.subtract)
                    P.do("vector", f, waits=[("sD", s_zhi)],
                         inc=[("vD", 1)], tag=f"zlo{g}_{qsup}_{hp}")
                    zlo_sems[hp] = P.at("vD")
                    yb_free[hp % 2] = [("sD", yev_sems[hp]),
                                       ("vD", zlo_sems[hp])]

                def mk_tail(qsup=qsup, mb=mb, zlo_sems=list(zlo_sems),
                            yev_sems=list(yev_sems), g=g):
                    def tail():
                        for qtl in range(4):
                            w = []
                            if qtl == 0:
                                w = [("vD", zlo_sems[3]),
                                     ("vD", fs["v_yn"]),
                                     ("qS", q_esel),
                                     ("sD", s_proj_done)]

                            def f(eng, qtl=qtl):
                                eng.matmul(pb[4][:, 4 * qtl:4 * qtl + 4],
                                           zhi4[0:97,
                                                qtl * 128:(qtl + 1) * 128],
                                           e97sb, start=True, stop=False)
                                return eng.matmul(
                                    pb[4][:, 4 * qtl:4 * qtl + 4],
                                    zlo4[0:97, qtl * 128:(qtl + 1) * 128],
                                    e97sb, start=False, stop=True)
                            P.do("tensor", f, waits=w, inc=[("tD", 1)],
                                 tag=f"zt{g}_{qsup}_{qtl}")
                        fs["t_zt"] = P.at("tD")
                        t_zt = fs["t_zt"]
                        for qtl in range(4):
                            nc_ = qsup + (1 if qtl == 0 else 2)

                            def f(eng, qtl=qtl, nc_=nc_):
                                return eng.tensor_reduce(
                                    m4a[:, 4 * qtl:4 * qtl + 4].rearrange(
                                        "p (h o) -> p h o", o=1),
                                    mpart[:, mb + qtl * 32:
                                          mb + qtl * 32 + 32].rearrange(
                                        "p (h k) -> p h k", h=4)[:, :, 0:nc_],
                                    axis=AX.X, op=ALU.max)
                            P.do("vector", f, inc=[("vD", 1)],
                                 tag=f"m4{g}_{qsup}_{qtl}")
                        for qtl in range(4):
                            def f(eng, qtl=qtl):
                                return eng.tensor_tensor(
                                    dn1a[:, 4 * qtl:4 * qtl + 4],
                                    m4a[:, 4 * qtl:4 * qtl + 4],
                                    cbc[:, 4 * g:4 * (g + 1)], ALU.add)
                            P.do("vector", f, inc=[("vD", 1)],
                                 tag=f"dn1{g}_{qsup}_{qtl}")
                        v_dn1 = P.at("vD")
                        P.do("scalar",
                             lambda eng: eng.activation(ema, dn1a, AF.Exp),
                             waits=[("vD", v_dn1)], inc=[("sD", 1)],
                             tag=f"em{g}_{qsup}")
                        s_em = P.at("sD")
                        for qtl in range(4):
                            w = [("sD", s_em)]
                            if qtl == 0:
                                w.append(("tD", t_zt))

                            def f(eng, qtl=qtl):
                                return eng.tensor_tensor(
                                    dn2a[:, 4 * qtl:4 * qtl + 4],
                                    ema[:, 4 * qtl:4 * qtl + 4],
                                    pb[4][:, 4 * qtl:4 * qtl + 4], ALU.add)
                            P.do("vector", f, waits=w, inc=[("vD", 1)],
                                 tag=f"dn2{g}_{qsup}_{qtl}")
                        for qtl in range(4):
                            def f(eng, qtl=qtl):
                                return eng.reciprocal(
                                    rd4a[:, 4 * qtl:4 * qtl + 4],
                                    dn2a[:, 4 * qtl:4 * qtl + 4])
                            P.do("vector", f, inc=[("vD", 1)],
                                 tag=f"rd{g}_{qsup}_{qtl}")
                        P.do("vector",
                             lambda eng: eng.nop(cycle_cnt=200),
                             tag=f"rdnop{g}_{qsup}")
                        for qtl in range(4):
                            def f(eng, qtl=qtl):
                                return eng.tensor_copy(
                                    rdhl[:, 8 * qtl:8 * qtl + 4],
                                    rd4a[:, 4 * qtl:4 * qtl + 4])
                            P.do("vector", f, inc=[("vD", 1)],
                                 tag=f"rdhi{g}_{qsup}_{qtl}")
                        for qtl in range(4):
                            def f(eng, qtl=qtl):
                                return eng.tensor_tensor(
                                    rdhl[:, 8 * qtl + 4:8 * qtl + 8],
                                    rd4a[:, 4 * qtl:4 * qtl + 4],
                                    rdhl[:, 8 * qtl:8 * qtl + 4],
                                    ALU.subtract)
                            P.do("vector", f, inc=[("vD", 1)],
                                 tag=f"rdlo{g}_{qsup}_{qtl}")
                        v_rdhl = P.at("vD")
                        for qtl in range(4):
                            w = [("vD", v_rdhl)] if qtl == 0 else []

                            def f(eng, qtl=qtl):
                                return eng.matmul(
                                    pb[5][0:8, qtl * 128:(qtl + 1) * 128],
                                    rdhl[:, 8 * qtl:8 * qtl + 8], ident,
                                    start=True, stop=True)
                            P.do("tensor", f, waits=w, inc=[("tD", 1)],
                                 tag=f"rdT{g}_{qsup}_{qtl}")
                        t_rdT = P.at("tD")
                        P.do("vector",
                             lambda eng: eng.tensor_copy(rdT8s,
                                                         pb[5][0:8, :]),
                             waits=[("tD", t_rdT)], inc=[("vD", 1)],
                             tag=f"rdTc{g}_{qsup}")
                        v_rdT8s = P.at("vD")
                        t_rdbc = {}
                        for hp in range(4):
                            bank = 4 if hp < 2 else 5
                            poff = (hp % 2) * 64
                            w = [("vD", v_rdT8s)] if hp == 0 else []

                            def f(eng, hp=hp, bank=bank, poff=poff):
                                return eng.matmul(
                                    pb[bank][poff:poff + 64, :],
                                    e8sb[0:8, hp * 64:(hp + 1) * 64],
                                    rdT8s, start=True, stop=True)
                            P.do("tensor", f, waits=w, inc=[("tD", 1)],
                                 tag=f"rdbc{g}_{qsup}_{hp}")
                            t_rdbc[hp] = P.at("tD")
                        for hp in range(4):
                            h = g * HG + hp
                            fc = h // 2
                            po = 64 * (hp % 2)
                            bank = 4 if hp < 2 else 5

                            def f(eng, hp=hp, fc=fc, po=po, bank=bank,
                                  qsup=qsup):
                                return eng.tensor_tensor(
                                    y_sb[po:po + 64,
                                         fc * T + qsup * 512:
                                         fc * T + (qsup + 1) * 512],
                                    yscrT[po:po + 64, (hp // 2) * 512:
                                          (hp // 2) * 512 + 512],
                                    pb[bank][po:po + 64, 0:512], ALU.mult)
                            P.do("vector", f,
                                 waits=[("tD",
                                         t_rdbc[1 if hp < 2 else 3]),
                                        ("sD", yev_sems[hp])],
                                 inc=[("vD", 1)], tag=f"yn{g}_{qsup}_{hp}")
                            fs["v_yn_hp"][hp] = P.at("vD")
                        fs["v_yn"] = P.at("vD")
                    return tail
                pend_tail = mk_tail()
            if pend_tail is not None:
                pend_tail()
                pend_tail = None
            if debug and g == 0:
                P.do("sync", fdma(dbg["qh0"], qh[0][0:96, :]),
                     waits=[("sD", P.at("sD")), ("vD", P.at("vD"))],
                     inc=[("qS", 16)], tag="dbgqh0")
                P.do("sync", fdma(dbg["kh0"], kh[0][0:96, :]),
                     waits=[("sD", P.at("sD")), ("vD", P.at("vD"))],
                     inc=[("qS", 16)], tag="dbgkh0")
                P.do("sync", fdma(dbg["vg"], v_g),
                     waits=[("vD", P.at("vD"))],
                     inc=[("qS", 16)], tag="dbgvg")
            t_flash_done = P.at("tD")
            s_norm_done = P.at("sD")
            v_flash_done = fs["v_yn"]

        if debug:
            P.do("sync", fdma(dbg["y"], y_sb),
                 waits=[("sD", P.at("sD"))], inc=[("qS", 16)], tag="dbgy")
            P.do("sync", fdma(dbg["lat"], latent),
                 waits=[("sD", P.at("sD"))], inc=[("qS", 16)], tag="dbglat")
            P.do("sync", fdma(dbg["kro"], krope[0:96, :]),
                 waits=[("vD", P.at("vD"))], inc=[("qS", 16)], tag="dbgkro")
        P.barrier()
        A.freeit("xnb0", "xnb1", "wq_g", "wqr_g", "wku_g", "wvu_g",
                 "ctab", "stab2", "latent", "krope", "ro1", "ro2", "ro3",
                 "v_g", "ptb0", "ptb1", "ptb2", "ptb3", "mpart",
                 "yscrT", "zhi4", "zlo4",
                 "m4a", "dn1a", "ema", "dn2a", "rd4a", "rdhl", "rdT8s",
                 "e97", "e8",
                 *[f"qh{j}" for j in range(HG)],
                 *[f"kh{j}" for j in range(HG)])
        st["w1t2"] = [A.alloc(f"w1u{h}", 4 * HF * 2, BF16, [128, 4 * HF])
                      for h in range(2)]
        for h in range(2):
            P.do("sync",
                 fdma(st["w1t2"][h], w1bfd[:, h * 4 * HF:(h + 1) * 4 * HF]),
                 waits=[st["w1cache_q"]], inc=[("qS", 16)],
                 tag=f"w1prefetch{h}")
        st["w1t2_ready"] = ("qS", P.at("qS"))
        if sub < 4.5:
            A.freeit("y", "mask0", "maskt01", "cbc")
            P.barrier()
            return
        wo_sb = A.alloc("wo_sb", CE * E * 2, BF16, [128, CE * E])
        P.do("gpsimd", fdma(wo_sb.rearrange("p (c n) -> p c n", c=CE),
                            wod.rearrange("(c p) n -> p c n", p=128)),
             inc=[("qG", 16)], tag="ld_wo")
        q_wo = P.at("qG")

        for tt in range(NTT):
            for kc in range(CE):
                w = [("qG", q_wo)]
                if kc == 0:
                    w.append(("vD", st["prev_resid"]))

                def f(eng, kc=kc, tt=tt):
                    r = None
                    for m in range(8):
                        r = eng.matmul(
                            pb[m],
                            wo_sb[:, kc * E + m * 128:kc * E + (m + 1) * 128],
                            y_sb[:, kc * T + tt * TTW:kc * T + (tt + 1) * TTW],
                            start=(kc == 0), stop=(kc == CE - 1))
                    return r
                P.do("tensor", f, waits=w, inc=[("tD", 1)],
                     tag=f"wo{tt}_{kc}")
            t_wo = P.at("tD")
            for m in range(8):
                dst = stream[:, m * T + tt * TTW:m * T + (tt + 1) * TTW]

                def f(eng, m=m, dst=dst):
                    return eng.tensor_tensor(dst, dst, pb[m], ALU.add)
                P.do("vector", f, waits=[("tD", t_wo)], inc=[("vD", 1)],
                     tag=f"wores{tt}_{m}")
            st["prev_resid"] = P.at("vD")

        A.freeit("y", "wo_sb", "mask0", "maskt01", "cbc")
        P.barrier()

    emit_ffn(0)
    if stage >= 2:
        emit_attn(sub=sub)
    if stage >= 3:
        P.barrier()
        emit_ffn(1)

    def f(eng):
        return eng.dma_start(out=outT.rearrange("(c p) n -> p c n", p=128),
                             in_=stream.rearrange("p (c n) -> p c n", c=CE))
    P.do("sync", f, waits=[("vD", st["prev_resid"])], inc=[("qS", 16)],
         tag="st_out")
    P.do("sync", None, waits=[("qS", P.at("qS"))], tag="out_wait")

    P.check_deadlock()

    semctx = {n: nc.semaphore(n) for n in SEMS}
    semh = {n: c.__enter__() for n, c in semctx.items()}
    with nc.Block() as block:
        def runner(engname):
            def run(eng):
                seen = {}
                for waits, inc, fn, tag in P.ops[engname]:
                    for s, v in waits:
                        if seen.get(s, -1) >= v:
                            continue
                        seen[s] = v
                        eng.wait_ge(semh[s], v)
                    instr = fn(eng) if fn is not None else None
                    if inc:
                        if instr is None:
                            for s, a in inc:
                                eng.sem_inc(semh[s], a)
                        else:
                            for s, a in inc:
                                instr = instr.then_inc(semh[s], a)
            return run

        block.tensor(runner("tensor"))
        block.vector(runner("vector"))
        block.scalar(runner("scalar"))
        block.gpsimd(runner("gpsimd"))
        block.sync(runner("sync"))
    return nc


def _host_consts():
    ident = np.eye(128, dtype=ml_dtypes.bfloat16)
    i_ = np.arange(128)
    mask0 = np.where(i_[None, :] <= i_[:, None], 0.0, NEG).astype(np.float32)
    maskt01 = (i_[:, None] <= i_[None, :]).astype(ml_dtypes.bfloat16)
    d = np.arange(16, dtype=np.float64)
    inv = 1.0 / (10000.0 ** (d / 16.0))
    t = np.arange(T, dtype=np.float64)
    ang = t[None, :] * inv[:, None]
    c16, s16 = np.cos(ang), np.sin(ang)
    c32 = np.concatenate([c16, c16], axis=0)
    s2 = np.concatenate([s16, -s16], axis=0)
    ctab = np.tile(c32, (4, 1)).astype(ml_dtypes.bfloat16)
    stab2 = np.tile(s2, (4, 1)).astype(ml_dtypes.bfloat16)
    e97 = np.zeros((97, 4), dtype=ml_dtypes.bfloat16)
    for hp in range(4):
        e97[32 * hp, hp] = 1
    e8 = np.zeros((8, 256), dtype=ml_dtypes.bfloat16)
    for hp in range(4):
        e8[hp, hp * 64:(hp + 1) * 64] = 1
        e8[4 + hp, hp * 64:(hp + 1) * 64] = 1
    return ident, mask0, maskt01, ctab, stab2, e97, e8


def kernel(L0_attn_x, L0_attn_Wq, L0_attn_Wqr, L0_attn_Wkr, L0_attn_Wkvd,
           L0_attn_Wku, L0_attn_Wvu, L0_attn_Wo, L0_attn_lobo_log,
           L0_ffn_W1, L0_ffn_b1, L0_ffn_W2, L0_ffn_b2, L0_ffn_scale):
    x = np.asarray(L0_attn_x)
    B = x.shape[0]
    ident, mask0, maskt01, ctab, stab2, e97, e8 = _host_consts()
    shared = {
        "wq": np.ascontiguousarray(L0_attn_Wq, dtype=np.float32),
        "wqr": np.ascontiguousarray(L0_attn_Wqr, dtype=np.float32),
        "wkr": np.ascontiguousarray(L0_attn_Wkr, dtype=np.float32),
        "wkvd": np.ascontiguousarray(L0_attn_Wkvd, dtype=np.float32),
        "wku": np.ascontiguousarray(L0_attn_Wku, dtype=np.float32),
        "wvu": np.ascontiguousarray(L0_attn_Wvu, dtype=np.float32),
        "wo": np.ascontiguousarray(L0_attn_Wo, dtype=np.float32),
        "w1t": np.ascontiguousarray(np.asarray(L0_ffn_W1).T, dtype=np.float32),
        "w2t": np.ascontiguousarray(np.asarray(L0_ffn_W2).T, dtype=np.float32),
        "b1l": np.ascontiguousarray(
            np.asarray(L0_ffn_b1, dtype=np.float32).reshape(HC, 128).T),
        "b2row": np.asarray(L0_ffn_b2, dtype=np.float32).reshape(1, E).copy(),
        "scalebc": np.broadcast_to(
            np.asarray(L0_ffn_scale, dtype=np.float32).reshape(1, 1),
            (128, 1)).copy(),
        "lobobc": np.broadcast_to(
            np.asarray(L0_attn_lobo_log, dtype=np.float32).reshape(1, NH),
            (128, NH)).copy(),
        "ident": ident, "mask0": mask0, "maskt01": maskt01,
        "ctab": ctab, "stab2": stab2, "e97": e97, "e8": e8,
    }
    nc = build_program()
    in_maps = []
    for b in range(B):
        m = dict(shared)
        m["xT"] = np.ascontiguousarray(x[b].T, dtype=np.float32)
        in_maps.append(m)
    res = run_bass_kernel_spmd(nc, in_maps, core_ids=list(range(B)))
    out = np.stack([np.asarray(r["outT"]).T for r in res.results], axis=0)
    return np.ascontiguousarray(out, dtype=np.float32)


# version 18 (final)
# speedup vs baseline: 1.3160x; 1.0454x over previous
import math
import numpy as np
import ml_dtypes
import concourse.bass as bass
import concourse.mybir as mybir
from concourse.bass_utils import run_bass_kernel_spmd

F32 = mybir.dt.float32
BF16 = mybir.dt.bfloat16
U8 = mybir.dt.uint8
AF = mybir.ActivationFunctionType
ALU = mybir.AluOpType
AX = mybir.AxisListType

T = 2048
E = 1024
CE = 8
NH, DH, DR, DKV = 16, 64, 32, 256
HF = 4096
HC = 32
TTW = 512
NTT = T // TTW
NG, HG = 4, 4
QTN = T // 128
ISCALE = 1.0 / math.sqrt(DH + DR)
EPS = 1e-5
NEG = -1e30

ENGS = ("tensor", "vector", "scalar", "gpsimd", "sync")
SEMS = ("tD", "vD", "sD", "gD", "qS", "qG", "qC")

STAGE = 3


class Plan:
    def __init__(self):
        self.ops = {e: [] for e in ENGS}
        self.sem = {s: 0 for s in SEMS}

    def at(self, name):
        return self.sem[name]

    def do(self, eng, fn, waits=(), inc=(), tag=""):
        waits = tuple((s, v) for (s, v) in waits if v > 0)
        for s, a in inc:
            self.sem[s] += a
        self.ops[eng].append((waits, tuple(inc), fn, tag))

    def barrier(self):
        snap = [(s, self.sem[s]) for s in SEMS if self.sem[s] > 0]
        for e in ENGS:
            self.do(e, None, waits=snap, tag="barrier")

    def check_deadlock(self):
        idx = {e: 0 for e in ENGS}
        cnt = {s: 0 for s in self.sem}
        progress = True
        while progress:
            progress = False
            for e in ENGS:
                while idx[e] < len(self.ops[e]):
                    waits, inc, _, _ = self.ops[e][idx[e]]
                    if any(cnt[s] < v for (s, v) in waits):
                        break
                    for s, a in inc:
                        cnt[s] += a
                    idx[e] += 1
                    progress = True
        stuck = {e: idx[e] for e in ENGS if idx[e] < len(self.ops[e])}
        if stuck:
            lines = []
            for e in stuck:
                waits, _, _, tag = self.ops[e][idx[e]]
                lines.append(
                    f"{e}@{idx[e]}/{len(self.ops[e])} tag={tag} "
                    f"waits={[(s, v, cnt[s]) for s, v in waits]}")
            raise RuntimeError("PLAN DEADLOCK:\n" + "\n".join(lines))


class Arena:
    def __init__(self, ap, nbytes):
        self.ap = ap
        self.nbytes = nbytes
        self.free = [(0, nbytes)]
        self.regions = {}
        self.peak = 0

    def alloc(self, name, per_part_bytes, dtype, shape):
        assert name not in self.regions, name
        nb = (per_part_bytes + 31) & ~31
        for i, (off, sz) in enumerate(self.free):
            if sz >= nb:
                if sz == nb:
                    self.free.pop(i)
                else:
                    self.free[i] = (off + nb, sz - nb)
                view = self.ap[0:shape[0],
                               off:off + per_part_bytes].bitcast(dtype)
                want = per_part_bytes // mybir.dt.size(dtype)
                assert view.shape[1] == want, (name, view.shape, want)
                if len(shape) == 2:
                    assert shape[1] == want, (name, shape, want)
                self.regions[name] = (off, nb)
                used = self.nbytes - sum(s for _, s in self.free)
                self.peak = max(self.peak, used)
                return view
        raise RuntimeError(f"arena OOM {name} ({nb}B); free={self.free}")

    def freeit(self, *names):
        for name in names:
            off, nb = self.regions.pop(name)
            self.free.append((off, nb))
        self.free.sort()
        merged = []
        for off, sz in self.free:
            if merged and merged[-1][0] + merged[-1][1] == off:
                merged[-1] = (merged[-1][0], merged[-1][1] + sz)
            else:
                merged.append((off, sz))
        self.free = merged


def build_program(stage=STAGE, debug=False, sub=5):
    nc = bass.Bass("TRN2", target_bir_lowering=False,
                   detect_race_conditions=False)

    xT = nc.dram_tensor("xT", [E, T], F32, kind="ExternalInput").ap()
    wq = nc.dram_tensor("wq", [E, NH * DH], F32, kind="ExternalInput").ap()
    wqr = nc.dram_tensor("wqr", [E, NH * DR], F32, kind="ExternalInput").ap()
    wkrd = nc.dram_tensor("wkr", [E, DR], F32, kind="ExternalInput").ap()
    wkvdd = nc.dram_tensor("wkvd", [E, DKV], F32, kind="ExternalInput").ap()
    wkud = nc.dram_tensor("wku", [DKV, NH * DH], F32, kind="ExternalInput").ap()
    wvud = nc.dram_tensor("wvu", [DKV, NH * DH], F32, kind="ExternalInput").ap()
    wod = nc.dram_tensor("wo", [E, E], F32, kind="ExternalInput").ap()
    w1td = nc.dram_tensor("w1t", [E, HF], F32, kind="ExternalInput").ap()
    w2td = nc.dram_tensor("w2t", [HF, E], F32, kind="ExternalInput").ap()
    b1d = nc.dram_tensor("b1l", [128, HC], F32, kind="ExternalInput").ap()
    b2d = nc.dram_tensor("b2row", [1, E], F32, kind="ExternalInput").ap()
    scd = nc.dram_tensor("scalebc", [128, 1], F32, kind="ExternalInput").ap()
    lbd = nc.dram_tensor("lobobc", [128, NH], F32, kind="ExternalInput").ap()
    idd = nc.dram_tensor("ident", [128, 128], BF16, kind="ExternalInput").ap()
    m0d = nc.dram_tensor("mask0", [128, 128], F32, kind="ExternalInput").ap()
    mtd = nc.dram_tensor("maskt01", [128, 128], BF16, kind="ExternalInput").ap()
    ctd = nc.dram_tensor("ctab", [128, T], BF16, kind="ExternalInput").ap()
    s2d = nc.dram_tensor("stab2", [128, T], BF16, kind="ExternalInput").ap()
    e97d = nc.dram_tensor("e97", [97, 4], BF16, kind="ExternalInput").ap()
    e8d = nc.dram_tensor("e8", [8, 256], BF16, kind="ExternalInput").ap()
    outT = nc.dram_tensor("outT", [E, T], F32, kind="ExternalOutput").ap()
    xnd = nc.dram_tensor("xnd", [128, CE * T], BF16, kind="Internal").ap()
    dbg = {}
    if debug:
        dbg["xn"] = nc.dram_tensor("dbg_xn", [128, CE * TTW], BF16,
                                   kind="ExternalOutput").ap()
        dbg["g"] = nc.dram_tensor("dbg_g", [128, HC * TTW], BF16,
                                  kind="ExternalOutput").ap()
        dbg["al"] = nc.dram_tensor("dbg_al", [4, TTW], BF16,
                                   kind="ExternalOutput").ap()
        dbg["bt"] = nc.dram_tensor("dbg_bt", [16, TTW], BF16,
                                   kind="ExternalOutput").ap()
        dbg["lt"] = nc.dram_tensor("dbg_lt", [16, E], BF16,
                                   kind="ExternalOutput").ap()
        dbg["ub"] = nc.dram_tensor("dbg_ub", [128, TTW], F32,
                                   kind="ExternalOutput").ap()
        dbg["rb"] = nc.dram_tensor("dbg_rb", [128, TTW], F32,
                                   kind="ExternalOutput").ap()
        dbg["xn2"] = nc.dram_tensor("dbg_xn2", [128, CE * TTW], BF16,
                                    kind="ExternalOutput").ap()
        dbg["g2"] = nc.dram_tensor("dbg_g2", [128, HC * TTW], BF16,
                                   kind="ExternalOutput").ap()
        dbg["al2"] = nc.dram_tensor("dbg_al2", [4, TTW], BF16,
                                    kind="ExternalOutput").ap()
        dbg["y"] = nc.dram_tensor("dbg_y", [128, QTN * E], BF16,
                                  kind="ExternalOutput").ap()
        dbg["qh0"] = nc.dram_tensor("dbg_qh0", [96, T], BF16,
                                    kind="ExternalOutput").ap()
        dbg["kh0"] = nc.dram_tensor("dbg_kh0", [96, T], BF16,
                                    kind="ExternalOutput").ap()
        dbg["vg"] = nc.dram_tensor("dbg_vg", [128, 16 * HG * 65], BF16,
                                   kind="ExternalOutput").ap()
        dbg["st4"] = nc.dram_tensor("dbg_st4", [128, 40], F32,
                                    kind="ExternalOutput").ap()
        dbg["lat"] = nc.dram_tensor("dbg_lat", [128, 2 * T], BF16,
                                    kind="ExternalOutput").ap()
        dbg["kro"] = nc.dram_tensor("dbg_kro", [96, T], BF16,
                                    kind="ExternalOutput").ap()
        dbg["yscr"] = nc.dram_tensor("dbg_yscr", [128, 520], F32,
                                     kind="ExternalOutput").ap()
    w1bfd = nc.dram_tensor("w1bf", [128, CE * HF], BF16, kind="Internal").ap()
    w2bfd = nc.dram_tensor("w2bf", [HC, 128, E], BF16, kind="Internal").ap()

    ARENA_B = 212832
    arena_h = nc.alloc_sbuf_tensor("arena", [128, ARENA_B], U8)
    A = Arena(arena_h.ap(), ARENA_B)

    pb_ctx = [nc.psum_tensor(f"pb{i}", [128, 512], F32) for i in range(8)]
    pb = [c.__enter__()[:] for c in pb_ctx]

    P = Plan()

    stream = A.alloc("stream", T * 4 * CE, F32, [128, CE * T])
    ident = A.alloc("ident", 128 * 2, BF16, [128, 128])
    b1l = A.alloc("b1l", HC * 4, F32, [128, HC])
    b2row = A.alloc("b2row", E * 4, F32, [1, E])
    b2hl = A.alloc("b2hl", E * 2, BF16, [2, E])
    b2tmp = A.alloc("b2tmp", E * 2, BF16, [1, E])
    onesr = A.alloc("onesr", TTW * 2, BF16, [2, TTW])
    ones128 = A.alloc("ones128", 128 * 2, BF16, [128, 128])
    scalebc = A.alloc("scalebc", 32, F32, [128, 8])
    epsc = A.alloc("epsc", 32, F32, [128, 8])

    def fdma(outap, inap, slow=False):
        def f(eng):
            if slow:
                return eng.dma_start(out=outap, in_=inap,
                                     allow_slow_non_contiguous=True)
            return eng.dma_start(out=outap, in_=inap)
        return f

    st_xq = []
    for xt in range(NTT):
        P.do("sync",
             fdma(stream.rearrange("p (c n) -> p c n", c=CE)
                  [:, :, xt * TTW:(xt + 1) * TTW],
                  xT.rearrange("(c p) n -> p c n", p=128)
                  [:, :, xt * TTW:(xt + 1) * TTW]),
             inc=[("qS", 16)], tag=f"ld_x{xt}")
        st_xq.append(P.at("qS"))
    for ap_, src in ((ident, idd), (b1l, b1d), (b2row, b2d),
                     (scalebc[:, 0:1], scd)):
        P.do("sync", fdma(ap_, src), inc=[("qS", 16)], tag="ld_const")
    q_consts = P.at("qS")

    P.do("vector", lambda eng: eng.memset(onesr, 1.0), inc=[("vD", 1)])
    P.do("vector", lambda eng: eng.memset(epsc, EPS), inc=[("vD", 1)])
    P.do("vector", lambda eng: eng.memset(ones128, 1.0), inc=[("vD", 1)])

    def f_b2(eng):
        eng.tensor_copy(b2hl[0:1, :], b2row)
        eng.nop(cycle_cnt=400)
        return eng.tensor_tensor(b2tmp, b2row, b2hl[0:1, :], ALU.subtract)
    P.do("vector", f_b2, waits=[("qS", q_consts)], inc=[("vD", 1)], tag="b2")
    P.do("sync", fdma(b2hl[1:2, :], b2tmp), waits=[("vD", P.at("vD"))],
         inc=[("qS", 16)], tag="b2lo_mv")
    A.freeit("b2tmp")
    v_init = P.at("vD")
    s_init = P.at("sD")
    st_q_init = P.at("qS")
    A.freeit("b2row")

    st = dict(prev_resid=0, ln_prev_t=0, w2cache_q=0)

    def emit_ln_pre(tt, scr, cast_waits):
        xb2, x22, ub, rb, t1s, t1g = scr
        out = []
        for c in range(CE):
            pcol = slice(c * TTW, (c + 1) * TTW)
            src = stream[:, c * T + tt * TTW: c * T + (tt + 1) * TTW]

            def f(eng, src=src, dst=xb2[:, pcol]):
                return eng.tensor_copy(dst, src)
            P.do("vector", f, waits=cast_waits(c), inc=[("vD", 1)],
                 tag=f"lncast{c}")
            v_c = P.at("vD")

            def f(eng, i=xb2[:, pcol], o=x22[:, pcol]):
                return eng.activation(o, i, AF.Square)
            P.do("scalar", f, waits=[("vD", v_c)] + cast_waits(c + 10),
                 inc=[("sD", 1)], tag=f"lnsq{c}")
            out.append(P.at("sD"))
        return out

    def emit_ln_post(tt, xn, scr, s_sq, mm_waits):
        xb2, x22, ub, rb, t1s, t1g = scr
        for c in range(CE):
            pcol = slice(c * TTW, (c + 1) * TTW)
            w = [("sD", s_sq[c])]
            if c == 0:
                w += mm_waits

            def f(eng, c=c, pcol=pcol):
                eng.matmul(pb[0], ones128, xb2[:, pcol],
                           start=(c == 0), stop=(c == CE - 1))
                return eng.matmul(pb[1], ones128, x22[:, pcol],
                                  start=(c == 0), stop=(c == CE - 1))
            P.do("tensor", f, waits=w, inc=[("tD", 1)], tag=f"lnmm{c}")
        st["ln_prev_t"] = P.at("tD")
        t_stats = P.at("tD")

        def f(eng):
            return eng.tensor_scalar(ub, pb[0][:, 0:TTW], 1.0 / E, None,
                                     ALU.mult)
        P.do("vector", f, waits=[("tD", t_stats)], inc=[("vD", 1)],
             tag="lnst1")
        v1 = P.at("vD")
        P.do("scalar", lambda eng: eng.activation(t1s, ub, AF.Square),
             waits=[("vD", v1)], inc=[("sD", 1)], tag="lnst2")
        s1 = P.at("sD")
        P.do("vector",
             lambda eng: eng.scalar_tensor_tensor(
                 t1s, pb[1][:, 0:TTW], 1.0 / E, t1s, ALU.mult, ALU.subtract),
             waits=[("sD", s1)], inc=[("vD", 1)], tag="lnst3")
        v2 = P.at("vD")

        def f(eng):
            eng.activation(t1g, t1s, AF.Ln, bias=epsc[:, 0:1])
            return eng.activation(rb, t1g, AF.Exp, scale=-0.5)
        P.do("scalar", f, waits=[("vD", v2)], inc=[("sD", 1)], tag="lnst4")
        s_rb = P.at("sD")

        vnorm = []
        for c in range(CE):
            src = stream[:, c * T + tt * TTW: c * T + (tt + 1) * TTW]
            dst = xn[:, c * TTW:(c + 1) * TTW]

            def f(eng, src=src, dst=dst):
                eng.tensor_tensor(t1s, src, ub, ALU.subtract)
                return eng.tensor_tensor(dst, t1s, rb, ALU.mult)
            P.do("vector", f, waits=[("sD", s_rb)], inc=[("vD", 1)],
                 tag=f"lnnv{c}")
            vnorm.append(P.at("vD"))
        return [("vD", P.at("vD"))], vnorm

    def emit_ffn(call):
        g_sb = A.alloc("g", HC * TTW * 2, BF16, [128, HC * TTW])
        xb2 = g_sb[:, 0:CE * TTW]
        x22 = g_sb[:, CE * TTW:2 * CE * TTW]
        xn = A.alloc("xn", CE * TTW * 2, BF16, [128, CE * TTW])
        ub = A.alloc("ub", TTW * 4, F32, [128, TTW])
        rb = A.alloc("rb", TTW * 4, F32, [128, TTW])
        t1s = A.alloc("t1s", TTW * 4, F32, [128, TTW])
        t1g = A.alloc("t1g", TTW * 4, F32, [128, TTW])
        w2c = A.alloc("w2c", 3 * E * 2, BF16, [128, 3 * E])
        lt = g_sb[0:16, 16 * TTW:16 * TTW + E]
        xn16 = A.alloc("xn16", CE * 16 * 2, BF16, [128, CE * 16])
        alphas = A.alloc("alphas", TTW * 2, BF16, [4, TTW])
        bt = [A.alloc(f"bt{j}", TTW * 2, BF16, [4, TTW]) for j in range(4)]
        e4a = A.alloc("e4a", HF * 2, BF16, [4, HF])
        e16j = [A.alloc(f"e16j{j}", E * 2, BF16, [4, E]) for j in range(4)]
        scr = (xb2, x22, ub, rb, t1s, t1g)
        if call == 0:
            w1p = [A.alloc(f"w1t{h}", 4 * HF * 2, BF16, [128, 4 * HF])
                   for h in range(2)]
        else:
            w1p = st["w1t2"]

        if call == 0:
            for h in range(2):
                P.do("gpsimd",
                     fdma(w1p[h].rearrange("p (c n) -> p c n", c=4),
                          w1td[h * 512:(h + 1) * 512, :]
                          .rearrange("(c p) n -> p c n", p=128)),
                     waits=[("vD", v_init)], inc=[("qG", 16)],
                     tag=f"ld_w1_{h}")
            w1_ready = ("qG", P.at("qG"))
            for h in range(2):
                P.do("scalar",
                     fdma(w1bfd[:, h * 4 * HF:(h + 1) * 4 * HF], w1p[h]),
                     waits=[w1_ready], inc=[("qC", 16)], tag=f"w1cache{h}")
            st["w1cache_q"] = ("qC", P.at("qC"))
        else:
            w1_ready = st["w1t2_ready"]

        s_aev = 0
        t_w2used = {}
        pctr = 0
        v_xn16 = 0
        s_sq_next = None
        for tt in range(NTT):
            if s_sq_next is None:
                def cw(c, tt=tt):
                    w = [("tD", st["ln_prev_t"]),
                         ("vD", st.get("early_resid", 0))]
                    if call == 0:
                        w.append(("qS", st_xq[tt]))
                    return w
                s_sq_next = emit_ln_pre(tt, scr, cw)
            mmw = [("vD", st.get("early_resid", 0)), ("vD", v_init),
                   ("qS", q_consts)]
            xn_ready, vnorm = emit_ln_post(tt, xn, scr, s_sq_next, mmw)
            s_sq_next = None

            if tt == 0:
                def f(eng):
                    return eng.tensor_copy(
                        xn16.rearrange("p (c n) -> p c n", c=CE),
                        xn.rearrange("p (c n) -> p c n", c=CE)[:, :, 1:17])
                P.do("vector", f, waits=xn_ready, inc=[("vD", 1)], tag="xn16")
                v_xn16 = P.at("vD")
                s_prev = s_aev
                for c in range(CE):
                    def f(eng, c=c):
                        return eng.matmul(pb[2][0:16, 0:128],
                                          xn[:, c * TTW + 1:c * TTW + 17],
                                          ident, start=True, stop=True)
                    P.do("tensor", f,
                         waits=xn_ready + [("sD", s_prev)],
                         inc=[("tD", 1)], tag=f"ltmm{c}")
                    t_lt = P.at("tD")

                    def f(eng, c=c):
                        return eng.copy(lt[0:16, c * 128:(c + 1) * 128],
                                        pb[2][0:16, 0:128])
                    P.do("scalar", f, waits=[("tD", t_lt)], inc=[("sD", 1)],
                         tag=f"ltev{c}")
                    s_prev = P.at("sD")
                s_aev = s_prev
                s_lt = s_prev
                P.do("vector", lambda eng: eng.memset(e4a, 0.0),
                     inc=[("vD", 1)], tag="e4a0")
                for j in range(4):
                    P.do("vector",
                         lambda eng, j=j: eng.memset(e16j[j], 0.0),
                         inc=[("vD", 1)], tag=f"e16z{j}")
                v_ez = P.at("vD")
                for i in range(4):
                    P.do("sync",
                         fdma(e4a[i:i + 1, 1024 * i:1024 * (i + 1)],
                              lt[2 * i:2 * i + 1, :]),
                         waits=[("sD", s_lt), ("vD", v_ez)],
                         inc=[("qS", 16)], tag=f"e4a{i}")
                for i4 in range(4):
                    for j in range(4):
                        P.do("sync",
                             fdma(e16j[j][i4:i4 + 1,
                                          256 * i4:256 * i4 + 256],
                                  lt[8 + 2 * i4:9 + 2 * i4, j:j + 1021:4],
                                  slow=True),
                             waits=[("sD", s_lt), ("vD", v_ez)],
                             inc=[("qS", 16)], tag=f"e16_{i4}_{j}")
                q_e = P.at("qS")

            for c in range(CE):
                w = [("vD", vnorm[c]), ("vD", v_xn16)]
                if c == 0:
                    w.append(("sD", s_aev))

                def f(eng, c=c):
                    return eng.matmul(
                        pb[2][0:4, 0:TTW],
                        xn16[:, c * 16 + 1:c * 16 + 8:2],
                        xn[:, c * TTW:(c + 1) * TTW],
                        start=(c == 0), stop=(c == CE - 1))
                P.do("tensor", f, waits=w, inc=[("tD", 1)], tag=f"amm{c}")
            t_a = P.at("tD")

            def f(eng):
                return eng.activation(alphas, pb[2][0:4, 0:TTW], AF.Copy,
                                      scale=scalebc[0:4, 0:1])
            P.do("scalar", f, waits=[("tD", t_a)], inc=[("sD", 1)], tag="aev")
            s_aev = P.at("sD")

            s_gelu = {}
            for m in range(HC):
                par = m % 2
                if m < 2:
                    for kc in range(CE):
                        w = [("vD", vnorm[kc])]
                        if m == 0 and kc == 0:
                            w += [("sD", s_aev), w1_ready, ("qS", q_e)]

                        def f(eng, m=m, par=par, kc=kc):
                            wsl = w1p[kc // 4][:, (kc % 4) * HF + m * 128:
                                               (kc % 4) * HF + (m + 1) * 128]
                            return eng.matmul(
                                pb[par], wsl,
                                xn[:, kc * TTW:(kc + 1) * TTW],
                                start=(kc == 0), stop=False)
                        P.do("tensor", f, waits=w, inc=[("tD", 1)],
                             tag=f"hmm{m}_{kc}")

                    def f(eng, m=m, par=par):
                        return eng.matmul(
                            pb[par], e4a[0:4, m * 128:(m + 1) * 128],
                            alphas[0:4, :], start=False, stop=True)
                    P.do("tensor", f, waits=[("sD", s_aev)],
                         inc=[("tD", 1)], tag=f"hmm{m}_l")
                else:
                    w = list(xn_ready) + [("sD", s_aev), w1_ready]
                    w.append(("sD", s_gelu[m - 2]))

                    def f(eng, m=m, par=par):
                        for kc in range(CE):
                            wsl = w1p[kc // 4][:, (kc % 4) * HF + m * 128:
                                               (kc % 4) * HF + (m + 1) * 128]
                            eng.matmul(
                                pb[par], wsl,
                                xn[:, kc * TTW:(kc + 1) * TTW],
                                start=(kc == 0), stop=False)
                        return eng.matmul(
                            pb[par], e4a[0:4, m * 128:(m + 1) * 128],
                            alphas[0:4, :], start=False, stop=True)
                    P.do("tensor", f, waits=w, inc=[("tD", 1)], tag=f"hmm{m}")
                t_h = P.at("tD")

                def f(eng, m=m, par=par):
                    return eng.activation(g_sb[:, m * TTW:(m + 1) * TTW],
                                          pb[par], AF.Gelu,
                                          bias=b1l[:, m:m + 1])
                P.do("scalar", f, waits=[("tD", t_h)], inc=[("sD", 1)],
                     tag=f"gelu{m}")
                s_gelu[m] = P.at("sD")
            s_gelu_all = P.at("sD")

            for j in range(4):
                for c in range(CE):
                    kc = 8 * j + c
                    w = [("sD", s_gelu[kc])]
                    if c == 0:
                        w.append(("sD", s_aev))

                    def f(eng, j=j, c=c, kc=kc):
                        return eng.matmul(
                            pb[2][0:4, 0:TTW],
                            xn16[:, c * 16 + 9:c * 16 + 16:2],
                            g_sb[:, kc * TTW:(kc + 1) * TTW],
                            start=(c == 0), stop=(c == CE - 1))
                    P.do("tensor", f, waits=w, inc=[("tD", 1)],
                         tag=f"bmm{j}_{c}")
                t_b = P.at("tD")

                def f(eng, j=j):
                    return eng.activation(bt[j][0:4, :],
                                          pb[2][0:4, 0:TTW], AF.Copy,
                                          scale=scalebc[0:4, 0:1])
                P.do("scalar", f, waits=[("tD", t_b)], inc=[("sD", 1)],
                     tag=f"bev{j}")
                s_aev = P.at("sD")
            s_beta = P.at("sD")

            first = (call == 0 and tt == 0)
            qdma = {}
            qcache = {}
            NBUF = 3

            def w2buf(buf):
                return w2c[:, buf * E:(buf + 1) * E]

            def plan_w2_dma(i):
                buf = (base_p + i) % NBUF
                w = []
                if base_p + i >= NBUF:
                    w.append(("tD", t_w2used[base_p + i - NBUF]))
                if buf >= 2:
                    w.append(("sD", s_gelu_all))
                if first and i >= NBUF:
                    w.append(("qC", qcache[i - NBUF]))
                dst = w2buf(buf)
                if first:
                    P.do("gpsimd",
                         fdma(dst, w2td[i * 128:(i + 1) * 128, :]),
                         waits=w, inc=[("qG", 16)], tag=f"w2l{i}")
                    qdma[i] = ("qG", P.at("qG"))
                    P.do("scalar", fdma(w2bfd[i], dst),
                         waits=[qdma[i]], inc=[("qC", 16)], tag=f"w2c{i}")
                    qcache[i] = P.at("qC")
                else:
                    P.do("sync", fdma(dst, w2bfd[i]),
                         waits=w + [("qC", st["w2cache_q"])],
                         inc=[("qS", 16)], tag=f"w2l{i}")
                    qdma[i] = ("qS", P.at("qS"))

            def plan_w2_mm(kc):
                buf = (base_p + kc) % NBUF
                w = [qdma[kc]]
                if kc == 0:
                    w += [("sD", s_gelu_all), ("sD", s_beta),
                          ("vD", st["prev_resid"])]
                else:
                    w.append(("sD", s_gelu[kc]))
                src = w2buf(buf)

                def f(eng, kc=kc, src=src):
                    r = None
                    for m in range(8):
                        r = eng.matmul(
                            pb[m],
                            src[:, m * 128:(m + 1) * 128],
                            g_sb[:, kc * TTW:(kc + 1) * TTW],
                            start=(kc == 0), stop=False)
                    return r
                P.do("tensor", f, waits=w, inc=[("tD", 1)], tag=f"omm{kc}")
                t_w2used[base_p + kc] = P.at("tD")

            base_p = pctr
            for i in range(HC + 2):
                if i < HC:
                    plan_w2_dma(i)
                if i >= 2:
                    plan_w2_mm(i - 2)
            pctr = base_p + HC
            if first:
                st["w2cache_q"] = max(qcache.values())

            if tt + 1 < NTT:
                def cw(c, bp=base_p, tt1=tt + 1):
                    if c < 10:
                        w = [("tD", t_w2used[bp + c])]
                    else:
                        w = [("tD", t_w2used[bp + 8 + (c - 10)])]
                    if call == 0:
                        w.append(("qS", st_xq[tt1]))
                    return w
                s_sq_next = emit_ln_pre(tt + 1, scr, cw)

            def f(eng):
                r = None
                for m in range(8):
                    for j in range(4):
                        r = eng.matmul(pb[m],
                                       e16j[j][0:4, m * 128:(m + 1) * 128],
                                       bt[j][0:4, :],
                                       start=False, stop=False)
                    r = eng.matmul(pb[m], b2hl[0:2, m * 128:(m + 1) * 128],
                                   onesr[0:2, :], start=False, stop=True)
                return r
            P.do("tensor", f, waits=[("sD", s_beta), ("vD", v_init),
                                     ("qS", max(q_e, st_q_init))],
                 inc=[("tD", 1)], tag="oxtr")
            t_out = P.at("tD")

            for m in range(8):
                dst = stream[:, m * T + tt * TTW:m * T + (tt + 1) * TTW]

                def f(eng, m=m, dst=dst):
                    return eng.scalar_tensor_tensor(
                        dst, pb[m], 0.5, dst, ALU.mult, ALU.add)
                P.do("vector", f, waits=[("tD", t_out)], inc=[("vD", 1)],
                     tag=f"res{m}")
                if m == 1:
                    st["early_resid"] = P.at("vD")
            st["prev_resid"] = P.at("vD")
            st["ln_t_gate"] = t_out

        A.freeit("g", "xn", "ub", "rb", "t1s", "t1g",
                 "w2c", "xn16", "alphas", "e4a",
                 *(("w1t0", "w1t1") if call == 0 else ("w1u0", "w1u1")),
                 *[f"bt{j}" for j in range(4)],
                 *[f"e16j{j}" for j in range(4)])

    def emit_attn(sub=5):
        P.barrier()
        mask0 = A.alloc("mask0", 128 * 4, F32, [128, 128])
        maskt01 = A.alloc("maskt01", 128 * 2, BF16, [128, 128])
        cbc = A.alloc("cbc", NH * 4, F32, [128, NH])
        P.do("sync", fdma(mask0, m0d), inc=[("qS", 16)], tag="ld_mask0")
        P.do("sync", fdma(maskt01, mtd), inc=[("qS", 16)], tag="ld_maskt")
        P.do("sync", fdma(cbc, lbd), inc=[("qS", 16)], tag="ld_lobo")
        q_m = P.at("qS")
        xn = A.alloc("xn", CE * TTW * 2, BF16, [128, CE * TTW])
        xb2 = A.alloc("xb2", CE * TTW * 2, BF16, [128, CE * TTW])
        x22 = A.alloc("x22", CE * TTW * 2, BF16, [128, CE * TTW])
        ub = A.alloc("ub", TTW * 4, F32, [128, TTW])
        rb = A.alloc("rb", TTW * 4, F32, [128, TTW])
        t1s = A.alloc("t1s", TTW * 4, F32, [128, TTW])
        t1g = A.alloc("t1g", TTW * 4, F32, [128, TTW])
        scr = (xb2, x22, ub, rb, t1s, t1g)
        latent = A.alloc("latent", 2 * T * 2, BF16, [128, 2 * T])
        krope = A.alloc("krope", T * 2, BF16, [96, T])
        ctab = A.alloc("ctab", T * 2, BF16, [128, T])
        stab2 = A.alloc("stab2", T * 2, BF16, [128, T])
        ro1 = A.alloc("ro1", TTW * 4, F32, [128, TTW])
        ro2 = A.alloc("ro2", TTW * 4, F32, [128, TTW])
        ro3 = A.alloc("ro3", TTW * 4, F32, [128, TTW])
        wkvd_sb = A.alloc("wkvd_sb", CE * DKV * 2, BF16, [128, CE * DKV])
        wkr_sb = A.alloc("wkr_sb", CE * DR * 2, BF16, [128, CE * DR])

        P.do("sync", fdma(ctab, ctd), inc=[("qS", 16)], tag="ld_ctab")
        P.do("sync", fdma(stab2, s2d), inc=[("qS", 16)], tag="ld_stab")
        q_tabs = P.at("qS")
        P.do("gpsimd", fdma(wkvd_sb.rearrange("p (c n) -> p c n", c=CE),
                            wkvdd.rearrange("(c p) n -> p c n", p=128)),
             inc=[("qG", 16)], tag="ld_wkvd")
        P.do("gpsimd", fdma(wkr_sb.rearrange("p (c n) -> p c n", c=CE),
                            wkrd.rearrange("(c p) n -> p c n", p=128)),
             inc=[("qG", 16)], tag="ld_wkr")
        q_kv = P.at("qG")

        s_latent_ev = {}
        v_krope = 0
        q_spill = [0] * NTT
        s_sq_next = None
        for tt in range(NTT):
            if s_sq_next is None:
                def cw(c):
                    return [("tD", st["ln_prev_t"]),
                            ("vD", st.get("early_resid", 0))]
                s_sq_next = emit_ln_pre(tt, scr, cw)
            mmw = [("vD", st.get("early_resid", 0)), ("vD", v_init),
                   ("qS", q_consts)]
            xn_ready, vnorm = emit_ln_post(tt, xn, scr, s_sq_next, mmw)
            s_sq_next = None
            P.do("sync",
                 fdma(xnd.rearrange("p (c n) -> p c n", c=CE)
                      [:, :, tt * TTW:(tt + 1) * TTW],
                      xn.rearrange("p (c n) -> p c n", c=CE)),
                 waits=xn_ready, inc=[("qS", 16)], tag=f"xnspill{tt}")
            q_spill[tt] = P.at("qS")
            if tt + 1 < NTT:
                def cw(c):
                    return [("tD", st["ln_prev_t"])]
                s_sq_next = emit_ln_pre(tt + 1, scr, cw)
            for mc in range(2):
                for kc in range(CE):
                    w = [("vD", vnorm[kc])]
                    if kc == 0:
                        w += [("qG", q_kv),
                              ("sD", s_latent_ev.get(mc, 0))]

                    def f(eng, mc=mc, kc=kc):
                        return eng.matmul(
                            pb[2 + mc],
                            wkvd_sb[:, kc * DKV + mc * 128:
                                    kc * DKV + (mc + 1) * 128],
                            xn[:, kc * TTW:(kc + 1) * TTW],
                            start=(kc == 0), stop=(kc == CE - 1))
                    P.do("tensor", f, waits=w, inc=[("tD", 1)],
                         tag=f"lat{mc}_{kc}")
                t_l = P.at("tD")

                def f(eng, mc=mc, tt=tt):
                    return eng.copy(
                        latent[:, mc * T + tt * TTW:mc * T + (tt + 1) * TTW],
                        pb[2 + mc])
                P.do("scalar", f, waits=[("tD", t_l)], inc=[("sD", 1)],
                     tag=f"latev{mc}")
                s_latent_ev[mc] = P.at("sD")
            for kc in range(CE):
                w = [("vD", vnorm[kc])]
                if kc == 0:
                    w += [("qG", q_kv), ("vD", v_krope)]

                def f(eng, kc=kc):
                    return eng.matmul(
                        pb[4][64:96, 0:TTW],
                        wkr_sb[:, kc * DR:(kc + 1) * DR],
                        xn[:, kc * TTW:(kc + 1) * TTW],
                        start=(kc == 0), stop=(kc == CE - 1))
                P.do("tensor", f, waits=w, inc=[("tD", 1)], tag=f"krmm{kc}")
            t_kr = P.at("tD")
            cs = slice(tt * TTW, (tt + 1) * TTW)

            def f(eng, cs=cs):
                eng.tensor_scalar(ro3[64:96, :], pb[4][64:96, 0:TTW], ISCALE,
                                  None, ALU.mult)
                eng.tensor_tensor(ro1[64:96, :], ro3[64:96, :],
                                  ctab[64:96, cs], ALU.mult)
                eng.tensor_tensor(ro2[64:96, :], ro3[64:96, :],
                                  stab2[64:96, cs], ALU.mult)
                mask = [(p + 16) % 32 for p in range(32)]
                eng.stream_shuffle(ro3[64:96, :], ro2[64:96, :], mask)
                return eng.tensor_tensor(krope[64:96, cs], ro1[64:96, :],
                                         ro3[64:96, :], ALU.add)
            P.do("vector", f, waits=[("tD", t_kr), ("qS", q_tabs)],
                 inc=[("vD", 1)], tag=f"krrope{tt}")
            v_krope = P.at("vD")
        v_krope_all = P.at("vD")
        A.freeit("xn", "xb2", "x22", "ub", "rb", "t1s", "t1g",
                 "wkvd_sb", "wkr_sb")

        if sub < 1.5:
            A.freeit("mask0", "maskt01", "cbc", "latent", "krope",
                     "ctab", "stab2", "ro1", "ro2", "ro3")
            return
        P.barrier()
        y_sb = A.alloc("y", QTN * E * 2, BF16, [128, QTN * E])
        qh = [A.alloc(f"qh{j}", T * 2, BF16, [96, T]) for j in range(HG)]
        kh = [A.alloc(f"kh{j}", T * 2, BF16, [96, T]) for j in range(HG)]
        v_g = A.alloc("v_g", 16 * HG * 65 * 2, BF16, [128, 16 * HG * 65])
        v_g4 = v_g.rearrange("p (c h n) -> p c h n", c=16, h=HG)
        xnb = [A.alloc(f"xnb{j}", CE * TTW * 2, BF16, [128, CE * TTW])
               for j in range(2)]
        wq_g = A.alloc("wq_g", CE * 256 * 2, BF16, [128, CE * 256])
        wqr_g = A.alloc("wqr_g", CE * 128 * 2, BF16, [128, CE * 128])
        wku_g = A.alloc("wku_g", 2 * 256 * 2, BF16, [128, 2 * 256])
        wvu_g = A.alloc("wvu_g", 2 * 256 * 2, BF16, [128, 2 * 256])
        ptb = [A.alloc(f"ptb{j}", 512 * 2, BF16, [128, 512])
               for j in range(4)]
        yscrT = A.alloc("yscrT", 1024 * 2, BF16, [128, 1024])
        zhi4 = A.alloc("zhi4", 512 * 2, BF16, [97, 512])
        zlo4 = A.alloc("zlo4", 512 * 2, BF16, [97, 512])
        mpart = A.alloc("mpart", 256 * 4, F32, [128, 256])
        m4a = A.alloc("m4a", 16 * 4, F32, [128, 16])
        dn1a = A.alloc("dn1a", 16 * 4, F32, [128, 16])
        ema = A.alloc("ema", 16 * 4, F32, [128, 16])
        dn2a = A.alloc("dn2a", 16 * 4, F32, [128, 16])
        rd4a = A.alloc("rd4a", 16 * 4, F32, [128, 16])
        rdhl = A.alloc("rdhl", 32 * 2, BF16, [128, 32])
        rdT8s = A.alloc("rdT8s", 512 * 2, BF16, [8, 512])
        e97sb = A.alloc("e97", 4 * 2, BF16, [97, 4])
        e8sb = A.alloc("e8", 256 * 2, BF16, [8, 256])
        P.do("sync", fdma(e97sb, e97d), inc=[("qS", 16)], tag="ld_e97")
        P.do("sync", fdma(e8sb, e8d), inc=[("qS", 16)], tag="ld_e8")
        q_esel = P.at("qS")

        def f(eng):
            return eng.memset(v_g4[:, :, :, 64:65], 1.0)
        P.do("vector", f, inc=[("vD", 1)], tag="vones")
        v_vones = P.at("vD")

        def f(eng):
            eng.memset(zhi4, 0.0)
            return eng.memset(zlo4, 0.0)
        P.do("gpsimd", f, inc=[("gD", 1)], tag="zinit")
        g_zinit = P.at("gD")
        fs = {"v_yn": 0, "t_zt": 0, "v_yn_hp": [0, 0, 0, 0]}

        t_flash_done = 0
        s_norm_done = 0
        v_flash_done = 0
        t_projmm_done = 0
        t_projuse = {}
        s_pevac = {}
        v_pevac = {}
        ngrun = 1 if sub < 3.5 else NG
        for g in range(ngrun):
            w = [("tD", t_projmm_done)]
            P.do("gpsimd", fdma(wq_g.rearrange("p (c n) -> p c n", c=CE),
                                wq[:, 256 * g:256 * (g + 1)]
                                .rearrange("(c p) n -> p c n", p=128)),
                 waits=w, inc=[("qG", 16)], tag=f"gwq{g}")
            P.do("gpsimd", fdma(wqr_g.rearrange("p (c n) -> p c n", c=CE),
                                wqr[:, 128 * g:128 * (g + 1)]
                                .rearrange("(c p) n -> p c n", p=128)),
                 waits=w, inc=[("qG", 16)], tag=f"gwqr{g}")
            P.do("gpsimd", fdma(wku_g.rearrange("p (c n) -> p c n", c=2),
                                wkud[:, 256 * g:256 * (g + 1)]
                                .rearrange("(c p) n -> p c n", p=128)),
                 waits=w, inc=[("qG", 16)], tag=f"gwku{g}")
            P.do("gpsimd", fdma(wvu_g.rearrange("p (c n) -> p c n", c=2),
                                wvud[:, 256 * g:256 * (g + 1)]
                                .rearrange("(c p) n -> p c n", p=128)),
                 waits=w, inc=[("qG", 16)], tag=f"gwvu{g}")
            q_gw = P.at("qG")

            q_xnb = [0, 0]
            for tt in range(NTT):
                buf = tt % 2
                w = [("qS", q_spill[tt]), ("tD", t_projmm_done)]
                if tt >= 2:
                    w.append(("tD", t_projuse[tt - 2]))

                def f(eng, tt=tt, buf=buf):
                    return eng.dma_start(
                        out=xnb[buf].rearrange("p (c n) -> p c n", c=CE),
                        in_=xnd.rearrange("p (c n) -> p c n", c=CE)
                        [:, :, tt * TTW:(tt + 1) * TTW])
                P.do("sync", f, waits=w, inc=[("qS", 16)],
                     tag=f"xnl{g}_{tt}")
                q_xnb[buf] = P.at("qS")

                xg = xnb[buf]
                cs = slice(tt * TTW, (tt + 1) * TTW)
                basew = [("qS", q_xnb[buf]), ("qG", q_gw),
                         ("tD", t_flash_done), ("sD", s_norm_done),
                         ("vD", v_flash_done)]
                if sub < 1.65:
                    t_projuse[tt] = P.at("tD")
                    continue
                if sub >= 1.65:
                    for hp in range(HG):
                        bank = hp % 2
                        for kc in range(CE):
                            w = list(basew) if (kc == 0) else []
                            if kc == 0:
                                w.append(("sD", s_pevac.get(("qc", bank), 0)))

                            def f(eng, hp=hp, kc=kc, xg=xg, bank=bank):
                                return eng.matmul(
                                    pb[bank][0:64, 0:TTW],
                                    wq_g[:, kc * 256 + 64 * hp:
                                         kc * 256 + 64 * hp + 64],
                                    xg[:, kc * TTW:(kc + 1) * TTW],
                                    start=(kc == 0), stop=(kc == CE - 1))
                            P.do("tensor", f, waits=w, inc=[("tD", 1)],
                                 tag=f"qc{g}_{tt}_{hp}_{kc}")
                        t_qc = P.at("tD")

                        def f(eng, hp=hp, bank=bank, cs=cs):
                            return eng.copy(qh[hp][0:64, cs],
                                            pb[bank][0:64, 0:TTW])
                        P.do("scalar", f, waits=[("tD", t_qc)],
                             inc=[("sD", 1)], tag=f"qcev{g}_{tt}_{hp}")
                        s_pevac[("qc", bank)] = P.at("sD")
                if sub < 1.75:
                    t_projuse[tt] = P.at("tD")
                    continue
                for hp in range(HG):
                    bank = 2 + hp % 2
                    for kc in range(CE):
                        w = list(basew) if kc == 0 else []
                        if kc == 0:
                            w.append(("vD", v_pevac.get(("qr", bank), 0)))

                        def f(eng, hp=hp, kc=kc, xg=xg, bank=bank):
                            return eng.matmul(
                                pb[bank][64:96, 0:TTW],
                                wqr_g[:, kc * 128 + 32 * hp:
                                      kc * 128 + 32 * hp + 32],
                                xg[:, kc * TTW:(kc + 1) * TTW],
                                start=(kc == 0), stop=(kc == CE - 1))
                        P.do("tensor", f, waits=w, inc=[("tD", 1)],
                             tag=f"qr{g}_{tt}_{hp}_{kc}")
                    t_qr = P.at("tD")

                    def f(eng, hp=hp, bank=bank, cs=cs):
                        eng.tensor_tensor(ro1[64:96, :], pb[bank][64:96, 0:TTW],
                                          ctab[64:96, cs], ALU.mult)
                        eng.tensor_tensor(ro2[64:96, :], pb[bank][64:96, 0:TTW],
                                          stab2[64:96, cs], ALU.mult)
                        mask = [(p + 16) % 32 for p in range(32)]
                        eng.stream_shuffle(ro3[64:96, :], ro2[64:96, :], mask)
                        return eng.tensor_tensor(qh[hp][64:96, cs],
                                                 ro1[64:96, :],
                                                 ro3[64:96, :], ALU.add)
                    P.do("vector", f, waits=[("tD", t_qr), ("qS", q_tabs)],
                         inc=[("vD", 1)], tag=f"qrope{g}_{tt}_{hp}")
                    v_pevac[("qr", bank)] = P.at("vD")
                if sub < 1.85:
                    t_projuse[tt] = P.at("tD")
                    continue
                for hp in range(HG):
                    bank = 4 if hp % 2 == 0 else 6
                    for lc in range(2):
                        w = list(basew) if lc == 0 else []
                        if lc == 0:
                            w.append(("sD", s_pevac.get(("kc", bank), 0)))

                        def f(eng, hp=hp, lc=lc, tt=tt, bank=bank):
                            return eng.matmul(
                                pb[bank][0:64, 0:TTW],
                                wku_g[:, lc * 256 + 64 * hp:
                                      lc * 256 + 64 * hp + 64],
                                latent[:, lc * T + tt * TTW:
                                       lc * T + (tt + 1) * TTW],
                                start=(lc == 0), stop=(lc == 1))
                        P.do("tensor", f, waits=w, inc=[("tD", 1)],
                             tag=f"kc{g}_{tt}_{hp}_{lc}")
                    t_kc = P.at("tD")

                    def f(eng, hp=hp, bank=bank, cs=cs):
                        return eng.activation(
                            kh[hp][0:64, cs], pb[bank][0:64, 0:TTW],
                            AF.Copy, scale=ISCALE)
                    P.do("scalar", f, waits=[("tD", t_kc)],
                         inc=[("sD", 1)], tag=f"kcev{g}_{tt}_{hp}")
                    s_pevac[("kc", bank)] = P.at("sD")
                if sub < 1.95:
                    t_projuse[tt] = P.at("tD")
                    continue
                for ts_ in range(4):
                    chunk = 4 * tt + ts_
                    bank = 5 if ts_ % 2 == 0 else 7
                    for lc in range(2):
                        w = list(basew) if lc == 0 else []
                        if lc == 0:
                            w += [("vD", v_vones),
                                  ("vD", v_pevac.get(("v", bank), 0))]

                        def f(eng, ts_=ts_, lc=lc, tt=tt, bank=bank):
                            return eng.matmul(
                                pb[bank][:, 0:256],
                                latent[:, lc * T + tt * TTW + ts_ * 128:
                                       lc * T + tt * TTW + (ts_ + 1) * 128],
                                wvu_g[:, lc * 256:(lc + 1) * 256],
                                start=(lc == 0), stop=(lc == 1))
                        P.do("tensor", f, waits=w, inc=[("tD", 1)],
                             tag=f"v{g}_{tt}_{ts_}_{lc}")
                    t_v = P.at("tD")

                    def f(eng, chunk=chunk, bank=bank):
                        return eng.tensor_copy(
                            v_g4[:, chunk:chunk + 1, :, 0:64],
                            pb[bank][:, 0:256]
                            .rearrange("p (o h n) -> p o h n", o=1, h=4))
                    P.do("vector", f, waits=[("tD", t_v)], inc=[("vD", 1)],
                         tag=f"vev{g}_{tt}_{ts_}")
                    v_pevac[("v", bank)] = P.at("vD")
                t_projuse[tt] = P.at("tD")
            t_projmm_done = P.at("tD")
            s_proj_done = P.at("sD")
            v_proj_done = P.at("vD")

            for j in range(HG):
                def f(eng, j=j):
                    return eng.tensor_copy(kh[j][64:96, :], krope[64:96, :])
                P.do("vector", f,
                     waits=[("vD", v_krope_all), ("tD", t_flash_done)],
                     inc=[("vD", 1)], tag=f"krcp{g}_{j}")
            v_krcp = P.at("vD")

            if sub < 2.5:
                continue
            rs = {"r": 0}
            ring_free = {}
            pctr = 0
            t_after_pv = {}
            g_pad = {}
            yb_free = {0: [], 1: []}
            pend_segs = []

            def emit_schunk(qsup, hp, qtl, ks, mb):
                ri = rs["r"]
                rs["r"] += 1
                bank = pb[ri % 4]
                qt = 4 * qsup + qtl
                qs = slice(qt * 128, (qt + 1) * 128)
                last = (ks == qsup)
                wd = (qtl + 1) * 128 if last else 512
                w = [("sD", s_proj_done), ("vD", v_krcp)]
                if ri - 4 in ring_free:
                    w.append(ring_free.pop(ri - 4))

                def f(eng, hp=hp, bank=bank, ks=ks, wd=wd, qs=qs):
                    return eng.matmul(
                        bank[:, 0:wd], qh[hp][:, qs],
                        kh[hp][:, ks * 512:ks * 512 + wd],
                        start=True, stop=True)
                P.do("tensor", f, waits=w, inc=[("tD", 1)],
                     tag=f"S{g}_{qt}_{hp}_{ks}")
                t_s = P.at("tD")
                mc = mb + qtl * 32 + hp * 8 + ks
                if not last:
                    def f(eng, bank=bank, wd=wd, mc=mc):
                        return eng.tensor_reduce(
                            mpart[:, mc:mc + 1], bank[:, 0:wd],
                            axis=AX.X, op=ALU.max)
                    P.do("vector", f, waits=[("tD", t_s)],
                         inc=[("vD", 1)], tag=f"Smax{g}_{qt}_{hp}_{ks}")
                else:
                    mcd = mc if wd == 128 else mc + 1

                    def f(eng, bank=bank, wd=wd, mc=mc, mcd=mcd):
                        eng.tensor_tensor(bank[:, wd - 128:wd],
                                          bank[:, wd - 128:wd],
                                          mask0, ALU.add)
                        if wd > 128:
                            eng.tensor_reduce(
                                mpart[:, mc:mc + 1], bank[:, 0:wd - 128],
                                axis=AX.X, op=ALU.max)
                        else:
                            eng.nop(cycle_cnt=300)
                        return eng.tensor_reduce(
                            mpart[:, mcd:mcd + 1], bank[:, wd - 128:wd],
                            axis=AX.X, op=ALU.max)
                    P.do("vector", f, waits=[("tD", t_s)],
                         inc=[("vD", 1)], tag=f"Smaxd{g}_{qt}_{hp}")
                ring_free[ri] = ("vD", P.at("vD"))

            for qsup in range(4):
                mb = (qsup % 2) * 128
                zlo_sems = [0, 0, 0, 0]
                yev_sems = [0, 0, 0, 0]
                pend_zx = None
                pend_pv = []
                pvdone = {}
                sq = [(hp2, qtl, ks) for hp2 in range(HG)
                      for qtl in range(4) for ks in range(qsup + 1)]
                si = 0
                for hp in range(HG):
                    kcs = ([4 * qsup + j for j in range(4)] +
                           list(range(4 * qsup)))
                    nkc = len(kcs)
                    ydst = pb[6 + hp % 2]
                    for ei, kc in enumerate(kcs):
                        j = kc - 4 * qsup
                        c0 = j * 128 if j > 0 else 0
                        ri = rs["r"]
                        rs["r"] += 1
                        bank = pb[ri % 4]
                        tp = pctr % 4
                        w = [("sD", s_proj_done), ("vD", v_krcp)]
                        if ri - 4 in ring_free:
                            w.append(ring_free.pop(ri - 4))

                        def f(eng, hp=hp, bank=bank, kc=kc, c0=c0,
                              qsup=qsup):
                            return eng.matmul(
                                bank[:, c0:512],
                                kh[hp][:, kc * 128:(kc + 1) * 128],
                                qh[hp][:, qsup * 512 + c0:(qsup + 1) * 512],
                                start=True, stop=True)
                        P.do("tensor", f, waits=w, inc=[("tD", 1)],
                             tag=f"ST{g}_{qsup}_{hp}_{kc}")
                        t_st = P.at("tD")
                        pad = (qsup == 0 and ei == 3)
                        if pad:
                            def f(eng, tp=tp):
                                return eng.memset(ptb[tp][:, 0:384], 0.0)
                            P.do("gpsimd", f,
                                 waits=[("tD",
                                         t_after_pv.get(pctr - 4, 0))],
                                 inc=[("gD", 1)], tag=f"pad{g}_{hp}")
                            g_pad[(g, hp)] = P.at("gD")
                        w = [("tD", t_st),
                             ("tD", t_after_pv.get(pctr - 4, 0))]

                        def f(eng, tp=tp, bank=bank, c0=c0):
                            return eng.activation(ptb[tp][:, c0:512],
                                                  bank[:, c0:512], AF.Exp)
                        P.do("scalar", f, waits=w, inc=[("sD", 1)],
                             tag=f"exp{g}_{qsup}_{hp}_{kc}")
                        ring_free[ri] = ("sD", P.at("sD"))
                        s_exp = P.at("sD")
                        wpv = [("sD", s_exp)]
                        if j >= 0:
                            def f(eng, tp=tp, c0=c0):
                                return eng.tensor_tensor(
                                    ptb[tp][:, c0:c0 + 128],
                                    ptb[tp][:, c0:c0 + 128],
                                    maskt01, ALU.mult)
                            P.do("gpsimd", f, waits=[("sD", s_exp)],
                                 inc=[("gD", 1)],
                                 tag=f"dmask{g}_{qsup}_{hp}_{kc}")
                            wpv.append(("gD", P.at("gD")))
                        if pad:
                            wpv.append(("gD", g_pad[(g, hp)]))
                        if ei == 0:
                            wpv += yb_free[hp % 2]
                            wpv += [("vD", v_proj_done),
                                    ("sD", s_proj_done)]
                        pv_c0 = 0 if pad else c0

                        def fpv(eng, hp=hp, tp=tp, kc=kc, pv_c0=pv_c0,
                                ei=ei, ydst=ydst, nkc=nkc):
                            return eng.matmul(
                                ydst[0:65, pv_c0:512],
                                v_g4[:, kc, hp, 0:65],
                                ptb[tp][:, pv_c0:512],
                                start=(ei == 0), stop=(ei == nkc - 1))
                        if si < len(sq):
                            hp2, qtl2, ks2 = sq[si]
                            si += 1
                            emit_schunk(qsup, hp2, qtl2, ks2, mb)
                        while len(pend_pv) >= 2:
                            fq, wq_, pc_, kq, hq = pend_pv.pop(0)
                            P.do("tensor", fq, waits=wq_, inc=[("tD", 1)],
                                 tag=f"PV{g}_{qsup}_{hq}_{kq}")
                            t_after_pv[pc_] = P.at("tD")
                            pvdone[hq] = P.at("tD")
                        pend_pv.append((fpv, wpv, pctr, kc, hp))
                        pctr += 1
                        if ei == 1 and pend_zx is not None:
                            pend_zx()
                            pend_zx = None
                        if pend_segs and ei >= 3 and ei % 2 == 1:
                            pend_segs.pop(0)()
                    if hp == HG - 1:
                        for fq, wq_, pc_, kq, hq in pend_pv:
                            P.do("tensor", fq, waits=wq_, inc=[("tD", 1)],
                                 tag=f"PV{g}_{qsup}_{hq}_{kq}")
                            t_after_pv[pc_] = P.at("tD")
                            pvdone[hq] = P.at("tD")
                        pend_pv = []

                    def mk_zx(hp=hp, ydst=ydst, qsup=qsup):
                        def zx():
                            t_pv = pvdone[hp]
                            po = 64 * (hp % 2)

                            def f(eng, hp=hp, ydst=ydst, po=po):
                                return eng.copy(
                                    yscrT[po:po + 64, (hp // 2) * 512:
                                          (hp // 2) * 512 + 512],
                                    ydst[0:64, 0:512])
                            P.do("scalar", f,
                                 waits=[("tD", t_pv),
                                        ("vD", fs["v_yn_hp"][hp])],
                                 inc=[("sD", 1)], tag=f"yev{g}_{qsup}_{hp}")
                            yev_sems[hp] = P.at("sD")

                            def f(eng, hp=hp, ydst=ydst):
                                return eng.copy(
                                    zhi4[32 * hp:32 * hp + 1, :],
                                    ydst[64:65, 0:512])
                            P.do("scalar", f,
                                 waits=[("tD", t_pv), ("tD", fs["t_zt"]),
                                        ("gD", g_zinit)],
                                 inc=[("sD", 1)], tag=f"zhi{g}_{qsup}_{hp}")
                            s_zhi = P.at("sD")

                            def f(eng, hp=hp, ydst=ydst):
                                return eng.tensor_tensor(
                                    zlo4[32 * hp:32 * hp + 1, :],
                                    ydst[64:65, 0:512],
                                    zhi4[32 * hp:32 * hp + 1, :],
                                    ALU.subtract)
                            P.do("vector", f, waits=[("sD", s_zhi)],
                                 inc=[("vD", 1)], tag=f"zlo{g}_{qsup}_{hp}")
                            zlo_sems[hp] = P.at("vD")
                            yb_free[hp % 2] = [("sD", yev_sems[hp]),
                                               ("vD", zlo_sems[hp])]
                        return zx
                    pend_zx = mk_zx()

                if pend_zx is not None:
                    pend_zx()
                    pend_zx = None

                def mk_tail(qsup=qsup, mb=mb, zlo_sems=list(zlo_sems),
                            yev_sems=list(yev_sems), g=g):
                    def tail_a():
                        for qtl in range(4):
                            w = []
                            if qtl == 0:
                                w = [("vD", zlo_sems[3]),
                                     ("vD", fs["v_yn"]),
                                     ("qS", q_esel),
                                     ("sD", s_proj_done)]

                            def f(eng, qtl=qtl):
                                eng.matmul(pb[4][:, 4 * qtl:4 * qtl + 4],
                                           zhi4[0:97,
                                                qtl * 128:(qtl + 1) * 128],
                                           e97sb, start=True, stop=False)
                                return eng.matmul(
                                    pb[4][:, 4 * qtl:4 * qtl + 4],
                                    zlo4[0:97, qtl * 128:(qtl + 1) * 128],
                                    e97sb, start=False, stop=True)
                            P.do("tensor", f, waits=w, inc=[("tD", 1)],
                                 tag=f"zt{g}_{qsup}_{qtl}")
                        fs["t_zt"] = P.at("tD")
                        for qtl in range(4):
                            nc_ = qsup + (1 if qtl == 0 else 2)

                            def f(eng, qtl=qtl, nc_=nc_):
                                return eng.tensor_reduce(
                                    m4a[:, 4 * qtl:4 * qtl + 4].rearrange(
                                        "p (h o) -> p h o", o=1),
                                    mpart[:, mb + qtl * 32:
                                          mb + qtl * 32 + 32].rearrange(
                                        "p (h k) -> p h k",
                                        h=4)[:, :, 0:nc_],
                                    axis=AX.X, op=ALU.max)
                            P.do("vector", f, inc=[("vD", 1)],
                                 tag=f"m4{g}_{qsup}_{qtl}")
                        for qtl in range(4):
                            def f(eng, qtl=qtl):
                                return eng.tensor_tensor(
                                    dn1a[:, 4 * qtl:4 * qtl + 4],
                                    m4a[:, 4 * qtl:4 * qtl + 4],
                                    cbc[:, 4 * g:4 * (g + 1)], ALU.add)
                            P.do("vector", f, inc=[("vD", 1)],
                                 tag=f"dn1{g}_{qsup}_{qtl}")
                        v_dn1 = P.at("vD")
                        P.do("scalar",
                             lambda eng: eng.activation(ema, dn1a, AF.Exp),
                             waits=[("vD", v_dn1)], inc=[("sD", 1)],
                             tag=f"em{g}_{qsup}")
                        fs["s_em"] = P.at("sD")

                    def tail_b():
                        for qtl in range(4):
                            w = [("sD", fs["s_em"])]
                            if qtl == 0:
                                w.append(("tD", fs["t_zt"]))

                            def f(eng, qtl=qtl):
                                return eng.tensor_tensor(
                                    dn2a[:, 4 * qtl:4 * qtl + 4],
                                    ema[:, 4 * qtl:4 * qtl + 4],
                                    pb[4][:, 4 * qtl:4 * qtl + 4], ALU.add)
                            P.do("vector", f, waits=w, inc=[("vD", 1)],
                                 tag=f"dn2{g}_{qsup}_{qtl}")
                        for qtl in range(4):
                            def f(eng, qtl=qtl):
                                return eng.reciprocal(
                                    rd4a[:, 4 * qtl:4 * qtl + 4],
                                    dn2a[:, 4 * qtl:4 * qtl + 4])
                            P.do("vector", f, inc=[("vD", 1)],
                                 tag=f"rd{g}_{qsup}_{qtl}")
                        P.do("vector",
                             lambda eng: eng.nop(cycle_cnt=200),
                             tag=f"rdnop{g}_{qsup}")
                        for qtl in range(4):
                            def f(eng, qtl=qtl):
                                return eng.tensor_copy(
                                    rdhl[:, 8 * qtl:8 * qtl + 4],
                                    rd4a[:, 4 * qtl:4 * qtl + 4])
                            P.do("vector", f, inc=[("vD", 1)],
                                 tag=f"rdhi{g}_{qsup}_{qtl}")
                        for qtl in range(4):
                            def f(eng, qtl=qtl):
                                return eng.tensor_tensor(
                                    rdhl[:, 8 * qtl + 4:8 * qtl + 8],
                                    rd4a[:, 4 * qtl:4 * qtl + 4],
                                    rdhl[:, 8 * qtl:8 * qtl + 4],
                                    ALU.subtract)
                            P.do("vector", f, inc=[("vD", 1)],
                                 tag=f"rdlo{g}_{qsup}_{qtl}")
                        v_rdhl = P.at("vD")
                        for qtl in range(4):
                            w = [("vD", v_rdhl)] if qtl == 0 else []

                            def f(eng, qtl=qtl):
                                return eng.matmul(
                                    pb[5][0:8, qtl * 128:(qtl + 1) * 128],
                                    rdhl[:, 8 * qtl:8 * qtl + 8], ident,
                                    start=True, stop=True)
                            P.do("tensor", f, waits=w, inc=[("tD", 1)],
                                 tag=f"rdT{g}_{qsup}_{qtl}")
                        fs["t_rdT"] = P.at("tD")

                    def tail_c():
                        P.do("vector",
                             lambda eng: eng.tensor_copy(rdT8s,
                                                         pb[5][0:8, :]),
                             waits=[("tD", fs["t_rdT"])], inc=[("vD", 1)],
                             tag=f"rdTc{g}_{qsup}")
                        v_rdT8s = P.at("vD")
                        t_rdbc = {}
                        for hp in range(4):
                            bank = 4 if hp < 2 else 5
                            poff = (hp % 2) * 64
                            w = [("vD", v_rdT8s)] if hp == 0 else []

                            def f(eng, hp=hp, bank=bank, poff=poff):
                                return eng.matmul(
                                    pb[bank][poff:poff + 64, :],
                                    e8sb[0:8, hp * 64:(hp + 1) * 64],
                                    rdT8s, start=True, stop=True)
                            P.do("tensor", f, waits=w, inc=[("tD", 1)],
                                 tag=f"rdbc{g}_{qsup}_{hp}")
                            t_rdbc[hp] = P.at("tD")
                        for hp in range(4):
                            h = g * HG + hp
                            fc = h // 2
                            po = 64 * (hp % 2)
                            bank = 4 if hp < 2 else 5

                            def f(eng, hp=hp, fc=fc, po=po, bank=bank,
                                  qsup=qsup):
                                return eng.tensor_tensor(
                                    y_sb[po:po + 64,
                                         fc * T + qsup * 512:
                                         fc * T + (qsup + 1) * 512],
                                    yscrT[po:po + 64, (hp // 2) * 512:
                                          (hp // 2) * 512 + 512],
                                    pb[bank][po:po + 64, 0:512], ALU.mult)
                            P.do("vector", f,
                                 waits=[("tD",
                                         t_rdbc[1 if hp < 2 else 3]),
                                        ("sD", yev_sems[hp])],
                                 inc=[("vD", 1)], tag=f"yn{g}_{qsup}_{hp}")
                            fs["v_yn_hp"][hp] = P.at("vD")
                        fs["v_yn"] = P.at("vD")
                    return [tail_a, tail_b, tail_c]
                pend_segs = mk_tail()
            while pend_segs:
                pend_segs.pop(0)()
            if debug and g == 0:
                P.do("sync", fdma(dbg["qh0"], qh[0][0:96, :]),
                     waits=[("sD", P.at("sD")), ("vD", P.at("vD"))],
                     inc=[("qS", 16)], tag="dbgqh0")
                P.do("sync", fdma(dbg["kh0"], kh[0][0:96, :]),
                     waits=[("sD", P.at("sD")), ("vD", P.at("vD"))],
                     inc=[("qS", 16)], tag="dbgkh0")
                P.do("sync", fdma(dbg["vg"], v_g),
                     waits=[("vD", P.at("vD"))],
                     inc=[("qS", 16)], tag="dbgvg")
            t_flash_done = P.at("tD")
            s_norm_done = P.at("sD")
            v_flash_done = fs["v_yn"]

        if debug:
            P.do("sync", fdma(dbg["y"], y_sb),
                 waits=[("sD", P.at("sD"))], inc=[("qS", 16)], tag="dbgy")
            P.do("sync", fdma(dbg["lat"], latent),
                 waits=[("sD", P.at("sD"))], inc=[("qS", 16)], tag="dbglat")
            P.do("sync", fdma(dbg["kro"], krope[0:96, :]),
                 waits=[("vD", P.at("vD"))], inc=[("qS", 16)], tag="dbgkro")
        P.barrier()
        A.freeit("xnb0", "xnb1", "wq_g", "wqr_g", "wku_g", "wvu_g",
                 "ctab", "stab2", "latent", "krope", "ro1", "ro2", "ro3",
                 "v_g", "ptb0", "ptb1", "ptb2", "ptb3", "mpart",
                 "yscrT", "zhi4", "zlo4",
                 "m4a", "dn1a", "ema", "dn2a", "rd4a", "rdhl", "rdT8s",
                 "e97", "e8",
                 *[f"qh{j}" for j in range(HG)],
                 *[f"kh{j}" for j in range(HG)])
        st["w1t2"] = [A.alloc(f"w1u{h}", 4 * HF * 2, BF16, [128, 4 * HF])
                      for h in range(2)]
        for h in range(2):
            P.do("sync",
                 fdma(st["w1t2"][h], w1bfd[:, h * 4 * HF:(h + 1) * 4 * HF]),
                 waits=[st["w1cache_q"]], inc=[("qS", 16)],
                 tag=f"w1prefetch{h}")
        st["w1t2_ready"] = ("qS", P.at("qS"))
        if sub < 4.5:
            A.freeit("y", "mask0", "maskt01", "cbc")
            P.barrier()
            return
        wo_sb = A.alloc("wo_sb", CE * E * 2, BF16, [128, CE * E])
        P.do("gpsimd", fdma(wo_sb.rearrange("p (c n) -> p c n", c=CE),
                            wod.rearrange("(c p) n -> p c n", p=128)),
             inc=[("qG", 16)], tag="ld_wo")
        q_wo = P.at("qG")

        for tt in range(NTT):
            for kc in range(CE):
                w = [("qG", q_wo)]
                if kc == 0:
                    w.append(("vD", st["prev_resid"]))

                def f(eng, kc=kc, tt=tt):
                    r = None
                    for m in range(8):
                        r = eng.matmul(
                            pb[m],
                            wo_sb[:, kc * E + m * 128:kc * E + (m + 1) * 128],
                            y_sb[:, kc * T + tt * TTW:kc * T + (tt + 1) * TTW],
                            start=(kc == 0), stop=(kc == CE - 1))
                    return r
                P.do("tensor", f, waits=w, inc=[("tD", 1)],
                     tag=f"wo{tt}_{kc}")
            t_wo = P.at("tD")
            for m in range(8):
                dst = stream[:, m * T + tt * TTW:m * T + (tt + 1) * TTW]

                def f(eng, m=m, dst=dst):
                    return eng.tensor_tensor(dst, dst, pb[m], ALU.add)
                P.do("vector", f, waits=[("tD", t_wo)], inc=[("vD", 1)],
                     tag=f"wores{tt}_{m}")
            st["prev_resid"] = P.at("vD")

        A.freeit("y", "wo_sb", "mask0", "maskt01", "cbc")
        P.barrier()

    emit_ffn(0)
    if stage >= 2:
        emit_attn(sub=sub)
    if stage >= 3:
        P.barrier()
        emit_ffn(1)

    def f(eng):
        return eng.dma_start(out=outT.rearrange("(c p) n -> p c n", p=128),
                             in_=stream.rearrange("p (c n) -> p c n", c=CE))
    P.do("sync", f, waits=[("vD", st["prev_resid"])], inc=[("qS", 16)],
         tag="st_out")
    P.do("sync", None, waits=[("qS", P.at("qS"))], tag="out_wait")

    P.check_deadlock()

    semctx = {n: nc.semaphore(n) for n in SEMS}
    semh = {n: c.__enter__() for n, c in semctx.items()}
    with nc.Block() as block:
        def runner(engname):
            def run(eng):
                seen = {}
                for waits, inc, fn, tag in P.ops[engname]:
                    for s, v in waits:
                        if seen.get(s, -1) >= v:
                            continue
                        seen[s] = v
                        eng.wait_ge(semh[s], v)
                    instr = fn(eng) if fn is not None else None
                    if inc:
                        if instr is None:
                            for s, a in inc:
                                eng.sem_inc(semh[s], a)
                        else:
                            for s, a in inc:
                                instr = instr.then_inc(semh[s], a)
            return run

        block.tensor(runner("tensor"))
        block.vector(runner("vector"))
        block.scalar(runner("scalar"))
        block.gpsimd(runner("gpsimd"))
        block.sync(runner("sync"))
    return nc


def _host_consts():
    ident = np.eye(128, dtype=ml_dtypes.bfloat16)
    i_ = np.arange(128)
    mask0 = np.where(i_[None, :] <= i_[:, None], 0.0, NEG).astype(np.float32)
    maskt01 = (i_[:, None] <= i_[None, :]).astype(ml_dtypes.bfloat16)
    d = np.arange(16, dtype=np.float64)
    inv = 1.0 / (10000.0 ** (d / 16.0))
    t = np.arange(T, dtype=np.float64)
    ang = t[None, :] * inv[:, None]
    c16, s16 = np.cos(ang), np.sin(ang)
    c32 = np.concatenate([c16, c16], axis=0)
    s2 = np.concatenate([s16, -s16], axis=0)
    ctab = np.tile(c32, (4, 1)).astype(ml_dtypes.bfloat16)
    stab2 = np.tile(s2, (4, 1)).astype(ml_dtypes.bfloat16)
    e97 = np.zeros((97, 4), dtype=ml_dtypes.bfloat16)
    for hp in range(4):
        e97[32 * hp, hp] = 1
    e8 = np.zeros((8, 256), dtype=ml_dtypes.bfloat16)
    for hp in range(4):
        e8[hp, hp * 64:(hp + 1) * 64] = 1
        e8[4 + hp, hp * 64:(hp + 1) * 64] = 1
    return ident, mask0, maskt01, ctab, stab2, e97, e8


def kernel(L0_attn_x, L0_attn_Wq, L0_attn_Wqr, L0_attn_Wkr, L0_attn_Wkvd,
           L0_attn_Wku, L0_attn_Wvu, L0_attn_Wo, L0_attn_lobo_log,
           L0_ffn_W1, L0_ffn_b1, L0_ffn_W2, L0_ffn_b2, L0_ffn_scale):
    x = np.asarray(L0_attn_x)
    B = x.shape[0]
    ident, mask0, maskt01, ctab, stab2, e97, e8 = _host_consts()
    shared = {
        "wq": np.ascontiguousarray(L0_attn_Wq, dtype=np.float32),
        "wqr": np.ascontiguousarray(L0_attn_Wqr, dtype=np.float32),
        "wkr": np.ascontiguousarray(L0_attn_Wkr, dtype=np.float32),
        "wkvd": np.ascontiguousarray(L0_attn_Wkvd, dtype=np.float32),
        "wku": np.ascontiguousarray(L0_attn_Wku, dtype=np.float32),
        "wvu": np.ascontiguousarray(L0_attn_Wvu, dtype=np.float32),
        "wo": np.ascontiguousarray(L0_attn_Wo, dtype=np.float32),
        "w1t": np.ascontiguousarray(np.asarray(L0_ffn_W1).T, dtype=np.float32),
        "w2t": np.ascontiguousarray(np.asarray(L0_ffn_W2).T, dtype=np.float32),
        "b1l": np.ascontiguousarray(
            np.asarray(L0_ffn_b1, dtype=np.float32).reshape(HC, 128).T),
        "b2row": np.asarray(L0_ffn_b2, dtype=np.float32).reshape(1, E).copy(),
        "scalebc": np.broadcast_to(
            np.asarray(L0_ffn_scale, dtype=np.float32).reshape(1, 1),
            (128, 1)).copy(),
        "lobobc": np.broadcast_to(
            np.asarray(L0_attn_lobo_log, dtype=np.float32).reshape(1, NH),
            (128, NH)).copy(),
        "ident": ident, "mask0": mask0, "maskt01": maskt01,
        "ctab": ctab, "stab2": stab2, "e97": e97, "e8": e8,
    }
    nc = build_program()
    in_maps = []
    for b in range(B):
        m = dict(shared)
        m["xT"] = np.ascontiguousarray(x[b].T, dtype=np.float32)
        in_maps.append(m)
    res = run_bass_kernel_spmd(nc, in_maps, core_ids=list(range(B)))
    out = np.stack([np.asarray(r["outT"]).T for r in res.results], axis=0)
    return np.ascontiguousarray(out, dtype=np.float32)
